# version 06 (final)
# baseline (speedup 1.0000x reference)
import numpy as np
import ml_dtypes
from contextlib import ExitStack

import concourse.bass as bass
import concourse.mybir as mybir
from concourse.bass_utils import run_bass_kernel_spmd

F32 = mybir.dt.float32
BF16 = mybir.dt.bfloat16
I32 = mybir.dt.int32
AF = mybir.ActivationFunctionType
ALU = mybir.AluOpType
AX = mybir.AxisListType
BF16NP = ml_dtypes.bfloat16

D = 1024
NCORES = 8
ATTN_PIPE = 3
DEFER_EPI = True
XH = 641
QH = 640


ENGINES = ("sync", "vector", "scalar", "tensor", "gpsimd")


class Marker:
    __slots__ = ("key", "count")

    def __init__(self, key, count):
        self.key = key
        self.count = count


class DmaSem:
    __slots__ = ("name", "count", "handle")

    def __init__(self, name):
        self.name = name
        self.count = 0
        self.handle = None


class Op:
    __slots__ = ("eng", "fn", "waits", "inc_amt", "marker", "dma_sem")

    def __init__(self, eng, fn, waits, inc_amt, dma_sem=None):
        self.eng = eng
        self.fn = fn
        self.waits = waits
        self.inc_amt = inc_amt
        self.marker = None
        self.dma_sem = dma_sem


class Prog:
    def __init__(self, nc):
        self.nc = nc
        self.ops = {e: [] for e in ENGINES}
        self.counts = {e: 0 for e in ENGINES}
        self.dma_sems = []

    def new_sem(self, name):
        s = DmaSem(name)
        self.dma_sems.append(s)
        return s

    def op(self, eng, fn, waits=()):
        waits = [w for w in waits if w is not None]
        o = Op(eng, fn, waits, 1)
        self.ops[eng].append(o)
        self.counts[eng] += 1
        o.marker = Marker(eng, self.counts[eng])
        return o

    def dma(self, out, in_, sem, eng="sync", waits=(), **kw):
        waits = [w for w in waits if w is not None]
        o = Op(eng, lambda g: g.dma_start(out=out, in_=in_, **kw), waits, 16, dma_sem=sem)
        self.ops[eng].append(o)
        sem.count += 16
        o.marker = Marker(sem, sem.count)
        return o

    def cc(self, kind, alu, groups, in_ap, out_ap, sem, waits=()):
        waits = [w for w in waits if w is not None]
        o = Op("gpsimd", lambda g: g.collective_compute(
            kind, alu, replica_groups=groups, ins=[in_ap], outs=[out_ap]),
            waits, 1, dma_sem=sem)
        self.ops["gpsimd"].append(o)
        sem.count += 1
        o.marker = Marker(sem, sem.count)
        return o

    def wait(self, eng, waits):
        waits = [w for w in waits if w is not None]
        o = Op(eng, None, waits, 0)
        self.ops[eng].append(o)
        return o

    def barrier(self):
        marks = [Marker(e, self.counts[e]) for e in ENGINES if self.counts[e] > 0]
        marks += [Marker(s, s.count) for s in self.dma_sems if s.count > 0]
        for e in ENGINES:
            self.wait(e, marks)

    def emit(self, stack):
        nc = self.nc
        eng_sems = {e: stack.enter_context(nc.semaphore(f"ms_{e}")) for e in ENGINES}
        for s in self.dma_sems:
            s.handle = stack.enter_context(nc.semaphore(s.name))

        def sem_of(key):
            return eng_sems[key] if isinstance(key, str) else key.handle

        max_waited = {}
        for e in ENGINES:
            for o in self.ops[e]:
                for w in o.waits:
                    k = id(w.key) if not isinstance(w.key, str) else w.key
                    if w.count > max_waited.get(k, 0):
                        max_waited[k] = w.count
        blk = stack.enter_context(nc.Block())
        for e in ENGINES:
            ops = self.ops[e]
            if not ops:
                continue

            def body(eng_handle, ops=ops, e=e):
                cum = 0
                last_waited = {}
                for o in ops:
                    byk = {}
                    for w in o.waits:
                        k = id(w.key) if not isinstance(w.key, str) else w.key
                        if w.count > byk.get(k, (0, None))[0]:
                            byk[k] = (w.count, w.key)
                    for k, (cnt, key) in byk.items():
                        if cnt > last_waited.get(k, 0):
                            eng_handle.wait_ge(sem_of(key), cnt)
                            last_waited[k] = cnt
                    if o.fn is None:
                        continue
                    ins = o.fn(eng_handle)
                    if o.dma_sem is not None:
                        ins.then_inc(o.dma_sem.handle, o.inc_amt)
                    else:
                        cum += 1
                        if cum <= max_waited.get(e, 0):
                            assert ins is not None, f"op on {e} returned no instruction"
                            ins.then_inc(eng_sems[e], 1)

            getattr(blk, e)(body)
        return eng_sems


class PsumPool:

    def __init__(self, nc, stack, n=6, name="ps"):
        self.banks = [stack.enter_context(nc.psum_tensor(f"{name}{i}", [128, 512], F32)) for i in range(n)]
        self.writer = [None] * n
        self.readers = [[] for _ in range(n)]
        self.rr = 0
        self.n = n

    def acquire(self):
        i = self.rr
        self.rr = (self.rr + 1) % self.n
        waits = list(self.readers[i])
        if self.writer[i] is not None:
            waits.append(self.writer[i])
        self.readers[i] = []
        self.writer[i] = None
        return i, waits

    def set_writer(self, i, marker):
        self.writer[i] = marker

    def add_reader(self, i, marker):
        self.readers[i].append(marker)


def consts(nc):
    if not hasattr(nc, "_mk_consts"):
        nc._mk_consts = {}
    return nc._mk_consts


def const_sbuf(nc, p, stack, lds, name, arr):
    c = consts(nc)
    if name in c:
        return c[name]
    dt = {np.dtype(np.float32): F32, np.dtype(BF16NP): BF16, np.dtype(np.int32): I32}[arr.dtype]
    dram = nc.inline_tensor(np.ascontiguousarray(arr), name + "_c").ap()
    sb = stack.enter_context(nc.sbuf_tensor(name + "_sb", list(arr.shape), dt))
    p.dma(sb[:], dram, lds)
    c[name] = sb
    return sb


def sel_matrix(G, P=128):
    n = P // G
    m = np.zeros((P, n), dtype=np.float32)
    for g in range(n):
        m[g * G:(g + 1) * G, g] = 1.0
    return m


def diag_masks_np(qbs=512):
    m = np.zeros((4, 128, qbs), dtype=BF16NP)
    for j in range(4):
        for kk in range(128):
            m[j, kk, 128 * j + kk:] = 1.0
    return m



def emit_group_norm_apply(p, nc, ps_pool, src_f32, dst_bf, G, gain_col, scratch, mode,
                          extra_waits=(), eps_col=None, clip=None, tile_w=512):
    W = src_f32.shape[-1]
    P = src_f32.shape[0]
    assert W <= tile_w
    ngr = P // G
    sq = scratch["sq"]
    st1 = scratch["st1"]
    sel = scratch["sel"]
    selT = scratch["selT"]
    m_sq = p.op("scalar", lambda s: s.square(sq[0:P, 0:W], src_f32), waits=list(extra_waits))
    bi, bw = ps_pool.acquire()
    ps = ps_pool.banks[bi]
    m_st = p.op("tensor", lambda t: t.matmul(ps[0:ngr, 0:W], sel[:], sq[0:P, 0:W], start=True, stop=True),
                waits=[m_sq.marker] + bw)
    ps_pool.set_writer(bi, m_st.marker)
    if mode == "rms":
        m_sc = p.op("scalar", lambda s: s.activation(out=st1[0:ngr, 0:W], in_=ps[0:ngr, 0:W], func=AF.Sqrt,
                                                     bias=eps_col[0:ngr, :], scale=1.0 / G), waits=[m_st.marker])
    else:
        m_sc = p.op("scalar", lambda s: s.activation(out=st1[0:ngr, 0:W], in_=ps[0:ngr, 0:W], func=AF.Sqrt,
                                                     bias=0.0, scale=1.0), waits=[m_st.marker])
    ps_pool.add_reader(bi, m_sc.marker)
    if mode == "l2":
        m_cl = p.op("vector", lambda v: v.tensor_scalar_max(st1[0:ngr, 0:W], st1[0:ngr, 0:W], float(clip)),
                    waits=[m_sc.marker])
    else:
        m_cl = m_sc
    m_rc = p.op("vector", lambda v: v.reciprocal(st1[0:ngr, 0:W], st1[0:ngr, 0:W]), waits=[m_cl.marker])
    bi2, bw2 = ps_pool.acquire()
    ps2 = ps_pool.banks[bi2]
    m_bc = p.op("tensor", lambda t: t.matmul(ps2[0:P, 0:W], selT[:], st1[0:ngr, 0:W], start=True, stop=True),
                waits=[m_rc.marker] + bw2)
    ps_pool.set_writer(bi2, m_bc.marker)
    if gain_col is not None:
        m_ap = p.op("vector", lambda v: v.scalar_tensor_tensor(dst_bf, src_f32, gain_col, ps2[0:P, 0:W],
                                                               ALU.mult, ALU.mult), waits=[m_bc.marker])
    else:
        m_ap = p.op("vector", lambda v: v.tensor_mul(dst_bf, src_f32, ps2[0:P, 0:W]), waits=[m_bc.marker])
    ps_pool.add_reader(bi2, m_ap.marker)
    return m_ap


def emit_attention(p, nc, ps_pool, cfg):
    T = cfg["T"]
    QBS = 512
    NQB = T // QBS
    scale = cfg["scale"]
    mask_sb = cfg["mask_sb"]
    pt = cfg["pt"]
    NPT = len(pt)
    PIPE = min(ATTN_PIPE, NPT - 1)
    recl = cfg["recl"]
    ones1 = cfg["ones1x128"]
    lbc = cfg["lbc"]
    hd = cfg["hd"]

    ot_banks = cfg["ot_banks"]
    ot_state = cfg.setdefault("ot_state", [{"w": None, "r": []}, {"w": None, "r": []}])
    qb_seq = 0

    def flush_epi():
        epi = cfg.pop("_pend_epi", None)
        if epi is not None:
            epi()

    for head in cfg["heads"]:
        k_lhsT = head["k_lhsT"]
        q_rhs = head["q_rhs"]
        v_lhsT = head["v_lhsT"]
        out_write = head["out_write"]
        bias_col = head["bias_col"]
        bias_dep = head.get("bias_dep")
        qkv_dep = head.get("qkv_dep")
        prev_exp = [None] * NPT
        for qb in range(NQB):
            nkc = (qb + 1) * QBS // 128
            osl = qb_seq % 2
            qb_seq += 1
            ps_ot = ot_banks[osl]
            ow = list(ot_state[osl]["r"])
            if ot_state[osl]["w"] is not None:
                ow.append(ot_state[osl]["w"])
            ot_state[osl] = {"w": None, "r": []}
            pend = []
            pv_last = None

            def emit_pv(c, dep, buf, nkc=nkc, ps_ot=ps_ot, v_lhsT=v_lhsT):
                return p.op("tensor", lambda t, c=c, buf=buf, nkc=nkc, ps_ot=ps_ot, v_lhsT=v_lhsT: t.matmul(
                    ps_ot[0:hd + 1, 0:QBS], v_lhsT(c), buf[:], start=(c == 0), stop=(c == nkc - 1)),
                    waits=[dep])

            for c in range(nkc):
                ai, aw = ps_pool.acquire()
                ps_a = ps_pool.banks[ai]
                mm = p.op("tensor", lambda t, c=c, qb=qb, ps_a=ps_a, k_lhsT=k_lhsT, q_rhs=q_rhs: t.matmul(
                    ps_a[:, :], k_lhsT(c), q_rhs(qb), start=True, stop=True),
                    waits=aw + ([qkv_dep] if qkv_dep else []) + ([bias_dep] if bias_dep else []) + ow)
                ow = []
                ps_pool.set_writer(ai, mm.marker)
                buf = pt[c % NPT]
                ex = p.op("scalar", lambda s, ps_a=ps_a, buf=buf, bias_col=bias_col: s.activation(
                    out=buf[:], in_=ps_a[:, :], func=AF.Exp, bias=bias_col[:], scale=scale),
                    waits=[mm.marker] + ([prev_exp[c % NPT].marker] if prev_exp[c % NPT] else []))
                ps_pool.add_reader(ai, ex.marker)
                dep = ex
                if c >= 4 * qb:
                    j = c - 4 * qb
                    dep = p.op("vector", lambda v, j=j, buf=buf: v.tensor_mul(
                        buf[:], buf[:], mask_sb[:, j * QBS:(j + 1) * QBS]), waits=[ex.marker])
                if c == 0:
                    flush_epi()
                pend.append((dep.marker, buf, c))
                if len(pend) > PIPE:
                    d0, b0, c0 = pend.pop(0)
                    pv = emit_pv(c0, d0, b0)
                    prev_exp[c0 % NPT] = pv
                    pv_last = pv
            while pend:
                d0, b0, c0 = pend.pop(0)
                pv = emit_pv(c0, d0, b0)
                prev_exp[c0 % NPT] = pv
                pv_last = pv
            ot_state[osl]["w"] = pv_last.marker
            m_rc = p.op("vector", lambda v, ps_ot=ps_ot: v.reciprocal(recl[:], ps_ot[hd:hd + 1, 0:QBS]),
                        waits=[pv_last.marker] + ([cfg["_recl_user"]] if cfg.get("_recl_user") else []))
            ot_state[osl]["r"].append(m_rc.marker)

            def mk_epi(qb=qb, osl=osl, ps_ot=ps_ot, m_rc=m_rc, out_write=out_write):
                def epi():
                    bi2, bw2 = ps_pool.acquire()
                    ps_b = ps_pool.banks[bi2]
                    lbc_row = cfg.get("lbc_row", ones1)
                    m_bc = p.op("tensor", lambda t, ps_b=ps_b, lbc_row=lbc_row: t.matmul(
                        ps_b[0:hd, :], lbc_row[0:1, 0:hd], recl[:], start=True, stop=True),
                        waits=[m_rc.marker] + bw2)
                    cfg["_recl_user"] = m_bc.marker
                    ps_pool.set_writer(bi2, m_bc.marker)
                    m_cp = p.op("scalar", lambda s, ps_b=ps_b: s.copy(lbc[0:hd, :], ps_b[0:hd, :]),
                                waits=[m_bc.marker] + ([cfg["_lbc_user"]] if cfg.get("_lbc_user") else []))
                    ps_pool.add_reader(bi2, m_cp.marker)
                    m_out = out_write(qb, ps_ot, [m_cp.marker])
                    cfg["_lbc_user"] = m_out.marker
                    ot_state[osl]["r"].append(m_out.marker)
                return epi

            if DEFER_EPI:
                cfg["_pend_epi"] = mk_epi()
            else:
                mk_epi()()
    flush_epi()



def emit_A(p, nc, gctx, sm, io, T):
    NT512 = T // 512
    x_t = io["a_x_t"]
    wq = io["a_wq"]
    wkv = io["a_wkv"]
    wk = io["a_wk"]
    wv = io["a_wv"]
    qg = io["a_qg"]
    kg = io["a_kg"]
    qg32 = io["a_qg32"]
    kgr = io["a_kgr"]
    a_out = io["cc_a_in"]
    lds = sm["lds"]
    sts = sm["sts"]
    xs = [sm["xs0"], sm["xs1"]]

    with ExitStack() as ctx:
        e = ctx.enter_context
        x_stg = [e(nc.sbuf_tensor("x_stg0", [128, 8 * 512], F32))] * 2
        x_bf = [e(nc.sbuf_tensor(f"x_bf{i}", [128, 8 * 512], BF16)) for i in range(2)]
        wq_bf = e(nc.sbuf_tensor("wq_bf", [128, 16 * 128], BF16))
        wkv_bf = e(nc.sbuf_tensor("wkv_bf", [128, 8 * 128], BF16))
        wk_bf = e(nc.sbuf_tensor("wk_bf", [128, 128], BF16))
        wv_bf = e(nc.sbuf_tensor("wv_bf", [128, 128], BF16))
        wstg = e(nc.sbuf_tensor("wstg", [128, 16 * 128], F32))
        qr = [e(nc.sbuf_tensor(f"qr{i}", [64, T], BF16)) for i in range(2)]
        kt = [e(nc.sbuf_tensor(f"kt{i}", [64, T], BF16)) for i in range(2)]
        kv_bf = e(nc.sbuf_tensor("kv_bf", [128, T], BF16))
        v_bf = e(nc.sbuf_tensor("v_bf", [128, (T // 128) * 132], BF16))
        o_bf = [e(nc.sbuf_tensor(f"o_bf{i}", [64, T], BF16)) for i in range(2)]
        qtmp = e(nc.sbuf_tensor("qtmp", [128, 512], F32))
        gcol = e(nc.sbuf_tensor("gcol", [128, 2], F32))
        kcol = e(nc.sbuf_tensor("kcol", [128, 1], F32))
        grow = e(nc.sbuf_tensor("grow", [1, 64], F32))
        red = e(nc.sbuf_tensor("red", [1, 8], F32))
        bias_col = e(nc.sbuf_tensor("biascol", [128, 1], F32))
        pt = [e(nc.sbuf_tensor(f"pt{i}", [128, 512], BF16)) for i in range(4)]
        recl = e(nc.sbuf_tensor("recl", [1, 512], F32))
        lbc = e(nc.sbuf_tensor("lbc", [64, 512], F32))
        sq = e(nc.sbuf_tensor("sq", [128, 512], F32))
        st1 = e(nc.sbuf_tensor("st1", [4, 512], F32))
        ps_pool = PsumPool(nc, ctx, name="psA")
        ot_banks = [e(nc.psum_tensor(f"ot{i}", [128, 512], F32)) for i in range(2)]

        sel64 = const_sbuf(nc, p, gctx, lds, "sel64", sel_matrix(64))
        selT64 = const_sbuf(nc, p, gctx, lds, "selT64", sel_matrix(64).T.copy())
        sel32h = const_sbuf(nc, p, gctx, lds, "sel32h", sel_matrix(32, 64))
        selT32h = const_sbuf(nc, p, gctx, lds, "selT32h", sel_matrix(32, 64).T.copy())
        ones1x128 = const_sbuf(nc, p, gctx, lds, "ones1x128", np.ones((1, 128), np.float32))
        eps6 = const_sbuf(nc, p, gctx, lds, "eps6", np.full((128, 1), 1e-6, np.float32))
        dm = diag_masks_np()
        mask_sb = e(nc.sbuf_tensor("mask_sb", [128, 4 * 512], BF16))
        dm_dram = nc.inline_tensor(dm, "dmasks_c").ap()
        for j in range(4):
            p.dma(mask_sb[:, j * 512:(j + 1) * 512], dm_dram[j], lds)

        p.dma(wstg[:, 0:16 * 128].rearrange("p (kc m) -> p kc m", kc=8), wq.rearrange("(kc p) m -> p kc m", p=128), lds)
        p.dma(gcol[:], qg, lds)
        p.dma(kcol[:], kg, lds)
        p.dma(grow[0:1, 0:32], qg32, lds)
        p.dma(grow[0:1, 32:64], kgr, lds)
        all_lds = Marker(lds, lds.count)
        m_wq = p.op("vector", lambda v: v.tensor_copy(wq_bf[:], wstg[:, 0:16 * 128]), waits=[all_lds])
        ld2 = p.dma(wstg[:, 0:8 * 128].rearrange("p (kc m) -> p kc m", kc=8), wkv.rearrange("(kc p) m -> p kc m", p=128), lds, waits=[m_wq.marker])
        m_wkv = p.op("vector", lambda v: v.tensor_copy(wkv_bf[:], wstg[:, 0:8 * 128]), waits=[ld2.marker])
        ld3 = p.dma(wstg[:, 0:128], wk, lds, waits=[m_wkv.marker])
        ld4 = p.dma(wstg[:, 128:256], wv, lds, waits=[m_wkv.marker])
        m_wk = p.op("vector", lambda v: v.tensor_copy(wk_bf[:], wstg[:, 0:128]), waits=[ld3.marker, ld4.marker])
        m_wv = p.op("vector", lambda v: v.tensor_copy(wv_bf[:], wstg[:, 128:256]), waits=[m_wk.marker])
        w_ready = m_wv.marker

        r1 = p.op("vector", lambda v: v.reduce_max(red[0:1, 0:1], grow[0:1, 0:32], axis=AX.X,
                                                   apply_absolute_value=True), waits=[all_lds])
        r2 = p.op("vector", lambda v: v.reduce_max(red[0:1, 1:2], grow[0:1, 32:64], axis=AX.X,
                                                   apply_absolute_value=True), waits=[r1.marker])
        r3 = p.op("vector", lambda v: v.tensor_mul(red[0:1, 2:3], red[0:1, 0:1], red[0:1, 1:2]), waits=[r2.marker])
        r4 = p.op("vector", lambda v: v.tensor_scalar_mul(red[0:1, 3:4], red[0:1, 2:3], -float(np.sqrt(32.0))),
                  waits=[r3.marker])
        bi, bw = ps_pool.acquire()
        psb = ps_pool.banks[bi]
        r5 = p.op("tensor", lambda t: t.matmul(psb[0:128, 0:1], ones1x128[:], red[0:1, 3:4], start=True, stop=True),
                  waits=[r4.marker] + bw)
        ps_pool.set_writer(bi, r5.marker)
        r6 = p.op("scalar", lambda s: s.copy(bias_col[:], psb[0:128, 0:1]), waits=[r5.marker])
        ps_pool.add_reader(bi, r6.marker)
        bias_ready = r6.marker

        scr32 = {"sq": sq, "st1": st1, "sel": sel32h, "selT": selT32h}
        slot_free = [None, None]
        prev_cast = None
        kv_done = []
        qtmp_users = []
        for tt in range(NT512):
            sl = tt % 2
            ld = p.dma(x_stg[sl][:, :].rearrange("p (kc c) -> p kc c", kc=8),
                       x_t[:, tt * 512:(tt + 1) * 512].rearrange("(kc p) c -> p kc c", p=128),
                       xs[sl], waits=[slot_free[sl], prev_cast])
            m_cast = p.op("vector", lambda v, sl=sl: v.tensor_copy(x_bf[sl][:], x_stg[sl][:]), waits=[ld.marker])
            prev_cast = m_cast.marker
            for m in range(2):
                ai, aw = ps_pool.acquire()
                ps = ps_pool.banks[ai]
                mm = None
                for kc in range(8):
                    mm = p.op("tensor", lambda t, kc=kc, m=m, sl=sl, ps=ps: t.matmul(
                        ps[:, :], wq_bf[:, (kc * 2 + m) * 128:(kc * 2 + m + 1) * 128],
                        x_bf[sl][:, kc * 512:(kc + 1) * 512],
                        start=(kc == 0), stop=(kc == 7)),
                        waits=([m_cast.marker, w_ready] if kc == 0 else []) + (aw if kc == 0 else []))
                ps_pool.set_writer(ai, mm.marker)
                cp = p.op("scalar", lambda s, ps=ps: s.copy(qtmp[:], ps[:, :]),
                          waits=[mm.marker] + qtmp_users)
                qtmp_users = []
                ps_pool.add_reader(ai, cp.marker)
                m_sq = p.op("scalar", lambda s: s.square(sq[:, 0:512], qtmp[:, 0:512]), waits=[cp.marker])
                bi, bw = ps_pool.acquire()
                psS = ps_pool.banks[bi]
                m_st = p.op("tensor", lambda t, psS=psS: t.matmul(psS[0:2, 0:512], sel64[:], sq[:, 0:512],
                                                                 start=True, stop=True), waits=[m_sq.marker] + bw)
                ps_pool.set_writer(bi, m_st.marker)
                m_sc = p.op("scalar", lambda s, psS=psS: s.activation(out=st1[0:2, 0:512], in_=psS[0:2, 0:512],
                                                                      func=AF.Sqrt, bias=eps6[0:2, :], scale=1.0 / 64),
                            waits=[m_st.marker])
                ps_pool.add_reader(bi, m_sc.marker)
                m_rc = p.op("vector", lambda v: v.reciprocal(st1[0:2, 0:512], st1[0:2, 0:512]), waits=[m_sc.marker])
                bi2, bw2 = ps_pool.acquire()
                psB = ps_pool.banks[bi2]
                m_bc = p.op("tensor", lambda t, psB=psB: t.matmul(psB[0:128, 0:512], selT64[:], st1[0:2, 0:512],
                                                                 start=True, stop=True), waits=[m_rc.marker] + bw2)
                ps_pool.set_writer(bi2, m_bc.marker)
                a1 = p.op("vector", lambda v, m=m, tt=tt, psB=psB: v.scalar_tensor_tensor(
                    qr[m][0:32, tt * 512:(tt + 1) * 512], qtmp[0:32, 0:512], gcol[0:32, m:m + 1],
                    psB[0:32, 0:512], ALU.mult, ALU.mult), waits=[m_bc.marker])
                a2 = p.op("vector", lambda v, m=m, tt=tt, psB=psB: v.scalar_tensor_tensor(
                    qr[m][32:64, tt * 512:(tt + 1) * 512], qtmp[64:96, 0:512], gcol[64:96, m:m + 1],
                    psB[64:96, 0:512], ALU.mult, ALU.mult), waits=[m_bc.marker])
                ps_pool.add_reader(bi2, a1.marker)
                ps_pool.add_reader(bi2, a2.marker)
                qtmp_users = [a1.marker, a2.marker]
            ai, aw = ps_pool.acquire()
            ps = ps_pool.banks[ai]
            mm = None
            for kc in range(8):
                mm = p.op("tensor", lambda t, kc=kc, sl=sl, ps=ps: t.matmul(
                    ps[:, :], wkv_bf[:, kc * 128:(kc + 1) * 128], x_bf[sl][:, kc * 512:(kc + 1) * 512],
                    start=(kc == 0), stop=(kc == 7)), waits=(aw if kc == 0 else []))
            ps_pool.set_writer(ai, mm.marker)
            cpkv = p.op("scalar", lambda s, tt=tt, ps=ps: s.copy(kv_bf[:, tt * 512:(tt + 1) * 512], ps[:, :]),
                        waits=[mm.marker])
            ps_pool.add_reader(ai, cpkv.marker)
            kv_done.append(cpkv)
            slot_free[sl] = mm.marker

        k_done = []
        v_done = []
        for tt in range(NT512):
            for half in range(2):
                ai, aw = ps_pool.acquire()
                ps = ps_pool.banks[ai]
                mm = p.op("tensor", lambda t, tt=tt, half=half, ps=ps: t.matmul(
                    ps[0:64, :], wk_bf[:, half * 64:(half + 1) * 64], kv_bf[:, tt * 512:(tt + 1) * 512],
                    start=True, stop=True), waits=[kv_done[tt].marker] + aw)
                ps_pool.set_writer(ai, mm.marker)
                cp = p.op("scalar", lambda s, ps=ps: s.copy(qtmp[0:64, 0:512], ps[0:64, :]),
                          waits=[mm.marker] + qtmp_users)
                qtmp_users = []
                ps_pool.add_reader(ai, cp.marker)
                m_ap = emit_group_norm_apply(p, nc, ps_pool, qtmp[0:64, 0:512],
                                             kt[half][:, tt * 512:(tt + 1) * 512],
                                             32, kcol[0:64, :], scr32, "rms", extra_waits=[cp.marker],
                                             eps_col=eps6)
                qtmp_users = [m_ap.marker]
                k_done.append(m_ap)
            c2 = k_done[-1]
            for st4 in range(4):
                tok = tt * 4 + st4
                vi, vw = ps_pool.acquire()
                psv = ps_pool.banks[vi]
                mv = p.op("tensor", lambda t, tok=tok, psv=psv: t.matmul(
                    psv[:, 0:128], kv_bf[:, tok * 128:(tok + 1) * 128], wv_bf[:],
                    start=True, stop=True), waits=vw + ([c2.marker] if st4 == 0 else []))
                ps_pool.set_writer(vi, mv.marker)
                dst = v_bf[:, tok * 132:(tok + 1) * 132].rearrange("p (h w) -> p h w", h=4)[:, :, 0:32]
                src = psv[:, 0:128].rearrange("p (h w) -> p h w", h=4)
                cpv = p.op("scalar", lambda s, dst=dst, src=src: s.copy(dst, src), waits=[mv.marker])
                ps_pool.add_reader(vi, cpv.marker)
                v_done.append(cpv)
        m_ones = p.op("vector", lambda v: v.memset(
            v_bf[:, :].rearrange("p (t h w) -> p t h w", h=4, w=33)[:, :, :, 32:33], 1.0),
            waits=[v_done[-1].marker])
        last_v = Marker("vector", p.counts["vector"])

        def mk_head(h):
            kth = kt[h // 2]
            koff = 32 * (h % 2)
            qth = qr[h // 2]
            qoff = 32 * (h % 2)

            def out_write(qb, ps_ot, deps, h=h):
                ob = o_bf[h // 2]
                ooff = 32 * (h % 2)
                return p.op("vector", lambda v, qb=qb, ps_ot=ps_ot, ob=ob, ooff=ooff: v.tensor_mul(
                    ob[ooff:ooff + 32, qb * 512:(qb + 1) * 512], ps_ot[0:32, 0:512], lbc[0:32, :]), waits=deps)

            return {
                "k_lhsT": lambda c, kth=kth, koff=koff: kth[koff:koff + 32, c * 128:(c + 1) * 128],
                "q_rhs": lambda qb, qth=qth, qoff=qoff: qth[qoff:qoff + 32, qb * 512:(qb + 1) * 512],
                "v_lhsT": lambda c, h=h: v_bf[:, c * 132 + 33 * h:c * 132 + 33 * (h + 1)],
                "out_write": out_write,
                "bias_col": bias_col,
                "bias_dep": bias_ready,
                "qkv_dep": Marker("vector", last_v.count),
            }

        cfg = {
            "T": T, "hd": 32, "scale": 0.125, "mask_sb": mask_sb,
            "pt": pt, "recl": recl, "lbc": lbc, "ones1x128": ones1x128,
            "ot_banks": ot_banks,
            "heads": [mk_head(h) for h in range(4)],
        }
        cfg["heads"][0]["qkv_dep"] = Marker("vector", last_v.count)
        emit_attention(p, nc, ps_pool, cfg)

        m_all_v = Marker("vector", p.counts["vector"])
        p.dma(a_out[0:64, :], o_bf[0][:], sts, waits=[m_all_v])
        p.dma(a_out[64:128, :], o_bf[1][:], sts, waits=[m_all_v])



def emit_select(p, nc, ctx, sm, dst_ap_fn, cand_ap_fns, oh, row_chunks, col_tiles, dtype, name):
    e = ctx.enter_context
    ncand = len(cand_ap_fns)
    dt = dtype
    cands = [e(nc.sbuf_tensor(f"{name}_c{j}", [128, 512], dt)) for j in range(ncand)]
    accs = [e(nc.sbuf_tensor(f"{name}_a{j}", [128, 512], dt)) for j in range(2)]
    xls = sm["xls"]
    sts = sm["sts"]
    prev_st = None
    prev_cmb = None
    for rc in row_chunks:
        for (t0, w) in col_tiles:
            lds_m = []
            for j in range(ncand):
                lm = p.dma(cands[j][:, 0:w], cand_ap_fns[j](rc, t0, w), xls,
                           waits=[prev_cmb])
                lds_m.append(lm.marker)
            m = p.op("vector", lambda v, w=w: v.tensor_scalar(
                accs[0][:, 0:w], cands[0][:, 0:w], oh[:, 0:1], None, ALU.mult),
                waits=lds_m + [prev_st])
            cur = 0
            for j in range(1, ncand):
                nxt = 1 - cur
                m = p.op("vector", lambda v, j=j, cur=cur, nxt=nxt, w=w: v.scalar_tensor_tensor(
                    accs[nxt][:, 0:w], cands[j][:, 0:w], oh[:, j:j + 1], accs[cur][:, 0:w],
                    ALU.mult, ALU.add), waits=[m.marker])
                cur = nxt
            prev_cmb = m.marker
            st = p.dma(dst_ap_fn(rc, t0, w), accs[cur][:, 0:w], sts, waits=[m.marker])
            prev_st = st.marker



def emit_B(p, nc, gctx, sm, io, T):
    S = T // 4
    SP = min(S, 1024)
    NTS = SP // 512
    x_t = io["b_x_t"]
    attn = io["b_attn_c"]
    wo = io["b_wo"]
    wg = io["b_wg"]
    wu = io["b_wu"]
    wd = io["b_wd"]
    wq1 = io["b_wq1"]
    wk1 = io["b_wk1"]
    wv1 = io["b_wv1"]
    bq1 = io["b_bq1"]
    bk1 = io["b_bk1"]
    bv1 = io["b_bv1"]
    o_q = io["cc_q_in"]
    o_k = io["cc_k_in"]
    o_v = io["cc_v_in"]
    o_x2 = io["xe_d_own"]
    lds = sm["lds"]
    wls = [sm["wl1"], sm["wl2"]]
    wlsd = sm["wlsd"]
    wlsq = sm["wlsq"]
    wlso = sm["wlso"]
    xls = sm["xls"]
    sts = sm["sts"]

    with ExitStack() as ctx:
        e = ctx.enter_context
        X = [e(nc.sbuf_tensor(f"X{i}", [128, SP], F32)) for i in range(8)]
        Y = [e(nc.sbuf_tensor(f"Y{i}", [128, SP], F32)) for i in range(8)]
        xbf = [e(nc.sbuf_tensor(f"xbf{i}", [128, SP], BF16)) for i in range(8)]
        h_bf = e(nc.sbuf_tensor("h_bf", [128, 24 * SP], BF16))
        attn_sb = e(nc.sbuf_tensor("attn_sb", [128, 4 * SP], BF16))
        wo_bf = e(nc.sbuf_tensor("wo_bf", [128, 4 * 1024], BF16))
        wst_i = [e(nc.sbuf_tensor(f"wsti{i}", [128, 1024], I32)) for i in range(2)]
        wbf = [e(nc.sbuf_tensor(f"wbf{i}", [128, 1024], BF16)) for i in range(4)]
        wdst_i = e(nc.sbuf_tensor("wdsti", [128, 1536], I32))
        wd_bf = e(nc.sbuf_tensor("wd_bf", [128, 3072], BF16))
        wq_st = e(nc.sbuf_tensor("wq_st", [128, 1024], F32))
        wq_bf2 = e(nc.sbuf_tensor("wq_bf2", [128, 1024], BF16))
        bcol = e(nc.sbuf_tensor("bcol", [128, 8 * 3], F32))
        bvrow = e(nc.sbuf_tensor("bvrow", [1, D], F32))
        gtmp = e(nc.sbuf_tensor("gtmp", [128, 512], BF16))
        utmp = e(nc.sbuf_tensor("utmp", [128, 512], BF16))
        sqb = [e(nc.sbuf_tensor(f"sqb{i}", [128, 512], F32)) for i in range(2)]
        stm = e(nc.sbuf_tensor("stm", [1, 512], F32))
        stv = e(nc.sbuf_tensor("stv", [1, 512], F32))
        strd = e(nc.sbuf_tensor("strd", [1, 512], F32))
        stmrs = e(nc.sbuf_tensor("stmrs", [1, 512], F32))
        stv2 = e(nc.sbuf_tensor("stv2", [1, 512], F32))
        vstage = e(nc.sbuf_tensor("vstage", [128, 1040], BF16))
        ps_pool = PsumPool(nc, ctx, name="psB")

        ones_col = const_sbuf(nc, p, gctx, lds, "ones_col", np.ones((128, 1), np.float32))
        ones1x128 = const_sbuf(nc, p, gctx, lds, "ones1x128", np.ones((1, 128), np.float32))
        eps5 = const_sbuf(nc, p, gctx, lds, "eps5", np.full((128, 1), 1e-5, np.float32))
        scr = {"sqb": sqb, "stm": stm, "stv": stv, "strd": strd, "stmrs": stmrs, "stv2": stv2}

        p.dma(bcol[:, 0:8], bq1, lds)
        p.dma(bcol[:, 8:16], bk1, lds)
        p.dma(bvrow[:], bv1, lds)
        wo_prev = None
        for kc in range(4):
            lw = p.dma(wq_st[:, :], wo[kc * 128:(kc + 1) * 128, :], wlso, waits=[wo_prev])
            cw = p.op("vector", lambda v, kc=kc: v.tensor_copy(wo_bf[:, kc * 1024:(kc + 1) * 1024], wq_st[:]),
                      waits=[lw.marker])
            wo_prev = cw.marker
        all_lds = Marker(lds, lds.count)
        wo_ready = wo_prev

        prev_sp_users = []
        for sp0 in range(0, S, SP):
            ssl = slice(sp0, sp0 + SP)
            xl = []
            for kc in range(8):
                xl.append(p.dma(X[kc][:, :], x_t[kc * 128:(kc + 1) * 128, ssl], xls, waits=prev_sp_users))
            la = p.dma(attn_sb[:, :].rearrange("p (kc m) -> p kc m", kc=4),
                       attn[:, ssl].rearrange("(kc p) m -> p kc m", p=128), xls, waits=prev_sp_users)
            x_ld = Marker(xls, xls.count)
            prev_sp_users = []

            v1_done = []
            for m in range(8):
                for tt in range(NTS):
                    ai, aw = ps_pool.acquire()
                    ps = ps_pool.banks[ai]
                    mm = None
                    for kc in range(4):
                        mm = p.op("tensor", lambda t, kc=kc, m=m, tt=tt, ps=ps: t.matmul(
                            ps[:, :], wo_bf[:, kc * 1024 + m * 128:kc * 1024 + (m + 1) * 128],
                            attn_sb[:, kc * SP + tt * 512:kc * SP + (tt + 1) * 512],
                            start=(kc == 0), stop=(kc == 3)),
                            waits=([wo_ready, x_ld] if kc == 0 else []) + (aw if kc == 0 else []))
                    ps_pool.set_writer(ai, mm.marker)
                    mo = p.op("vector", lambda v, m=m, tt=tt, ps=ps: v.tensor_add(
                        Y[m][:, tt * 512:(tt + 1) * 512], X[m][:, tt * 512:(tt + 1) * 512], ps[:, :]),
                        waits=[mm.marker])
                    ps_pool.add_reader(ai, mo.marker)
                    v1_done.append(mo.marker)

            def dst_x(kc, tw, ps_b1, ps_b2, deps):
                t0, w = tw
                m1 = p.op("vector", lambda v: v.tensor_mul(X[kc][:, t0:t0 + w], Y[kc][:, t0:t0 + w],
                                                           ps_b1[0:128, 0:w]), waits=deps)
                m2 = p.op("vector", lambda v: v.tensor_add(X[kc][:, t0:t0 + w], X[kc][:, t0:t0 + w],
                                                           ps_b2[0:128, 0:w]), waits=[m1.marker])
                m3 = p.op("vector", lambda v: v.tensor_copy(xbf[kc][:, t0:t0 + w], X[kc][:, t0:t0 + w]),
                          waits=[m2.marker])
                return m3

            ln1 = emit_ln(p, nc, ps_pool, scr, lambda kc: Y[kc][:, :], dst_x, 8, 512, SP, eps5,
                          ones_col, ones1x128, extra_waits=[Marker("vector", p.counts["vector"])],
                          const_dep=all_lds)
            x1_ready = ln1[-1].marker

            wload_prev = [None, None]
            wbf_user = [None] * 4
            h_done = []
            for oc in range(24):
                sl = oc % 2
                lg = p.dma(wst_i[sl][:, :].rearrange("p (kc m) -> p kc m", kc=8),
                           wg[:, oc * 128:(oc + 1) * 128].rearrange("(kc p) m -> p kc m", p=128), wls[sl],
                           waits=[wload_prev[sl]])
                cg = p.op("scalar", lambda s, sl=sl: s.mul(wbf[sl][:], wst_i[sl][:], 1.0 / 127.0),
                          waits=[lg.marker, wbf_user[sl]])
                lu = p.dma(wst_i[sl][:, :].rearrange("p (kc m) -> p kc m", kc=8),
                           wu[:, oc * 128:(oc + 1) * 128].rearrange("(kc p) m -> p kc m", p=128), wls[sl],
                           waits=[cg.marker])
                cu = p.op("scalar", lambda s, sl=sl: s.mul(wbf[2 + sl][:], wst_i[sl][:], 1.0 / 127.0),
                          waits=[lu.marker, wbf_user[2 + sl]])
                wload_prev[sl] = cu.marker
                for tt in range(NTS):
                    gi, gw = ps_pool.acquire()
                    ps_g = ps_pool.banks[gi]
                    mmg = None
                    for kc in range(8):
                        mmg = p.op("tensor", lambda t, kc=kc, sl=sl, tt=tt, ps_g=ps_g: t.matmul(
                            ps_g[:, :], wbf[sl][:, kc * 128:(kc + 1) * 128], xbf[kc][:, tt * 512:(tt + 1) * 512],
                            start=(kc == 0), stop=(kc == 7)),
                            waits=([cg.marker, x1_ready] if kc == 0 else []) + (gw if kc == 0 else []))
                    ps_pool.set_writer(gi, mmg.marker)
                    ui, uw = ps_pool.acquire()
                    ps_u = ps_pool.banks[ui]
                    mmu = None
                    for kc in range(8):
                        mmu = p.op("tensor", lambda t, kc=kc, sl=sl, tt=tt, ps_u=ps_u: t.matmul(
                            ps_u[:, :], wbf[2 + sl][:, kc * 128:(kc + 1) * 128], xbf[kc][:, tt * 512:(tt + 1) * 512],
                            start=(kc == 0), stop=(kc == 7)),
                            waits=([cu.marker] if kc == 0 else []) + (uw if kc == 0 else []))
                    ps_pool.set_writer(ui, mmu.marker)
                    ms = p.op("scalar", lambda s, ps_g=ps_g: s.activation(out=gtmp[:], in_=ps_g[:, :], func=AF.Silu),
                              waits=[mmg.marker] + h_done[-1:])
                    ps_pool.add_reader(gi, ms.marker)
                    mu2 = p.op("scalar", lambda s, ps_u=ps_u: s.copy(utmp[:], ps_u[:, :]),
                               waits=[mmu.marker] + h_done[-1:])
                    ps_pool.add_reader(ui, mu2.marker)
                    mh = p.op("vector", lambda v, oc=oc, tt=tt: v.tensor_mul(
                        h_bf[:, oc * SP + tt * 512:oc * SP + (tt + 1) * 512], gtmp[:], utmp[:]),
                        waits=[ms.marker, mu2.marker])
                    h_done.append(mh.marker)

            wd_prev = None
            wd_read_user = None
            for m in range(8):
                cd = None
                for hh in range(2):
                    ldd = p.dma(wdst_i[:, :].rearrange("p (kc m) -> p kc m", kc=12),
                                wd[hh * 1536:(hh + 1) * 1536, m * 128:(m + 1) * 128]
                                .rearrange("(kc p) m -> p kc m", p=128), wlsd,
                                waits=[wd_prev])
                    cd = p.op("scalar", lambda s, hh=hh: s.mul(wd_bf[:, hh * 1536:(hh + 1) * 1536],
                                                               wdst_i[:], 1.0 / 127.0),
                              waits=[ldd.marker, wd_read_user])
                    wd_prev = cd.marker
                for tt in range(NTS):
                    di, dw = ps_pool.acquire()
                    ps_d = ps_pool.banks[di]
                    mmd = None
                    for kc in range(24):
                        mmd = p.op("tensor", lambda t, kc=kc, tt=tt, ps_d=ps_d: t.matmul(
                            ps_d[:, :], wd_bf[:, kc * 128:(kc + 1) * 128],
                            h_bf[:, kc * SP + tt * 512:kc * SP + (tt + 1) * 512],
                            start=(kc == 0), stop=(kc == 23)),
                            waits=([cd.marker, h_done[-1]] if kc == 0 else []) + (dw if kc == 0 else []))
                    ps_pool.set_writer(di, mmd.marker)
                    mo = p.op("vector", lambda v, m=m, tt=tt, ps_d=ps_d: v.tensor_add(
                        Y[m][:, tt * 512:(tt + 1) * 512], X[m][:, tt * 512:(tt + 1) * 512], ps_d[:, :]),
                        waits=[mmd.marker])
                    ps_pool.add_reader(di, mo.marker)
                wd_read_user = mmd.marker

            ln2 = emit_ln(p, nc, ps_pool, scr, lambda kc: Y[kc][:, :], dst_x, 8, 512, SP, eps5,
                          ones_col, ones1x128, extra_waits=[Marker("vector", p.counts["vector"])],
                          const_dep=all_lds)
            x2_ready = ln2[-1].marker
            st_x2 = [p.dma(o_x2[kc * 128:(kc + 1) * 128, ssl], X[kc][:, :], sts, waits=[x2_ready])
                     for kc in range(8)]
            prev_sp_users.extend(s.marker for s in st_x2)

            wq_prev = None
            wqst_user = None
            st_prev = None
            for wi, (wap, bofs, oap) in enumerate(((wq1, 0, o_q), (wk1, 8, o_k))):
                for m in range(8):
                    ldw = p.dma(wq_st[:, :].rearrange("p (kc m) -> p kc m", kc=8),
                                wap[:, m * 128:(m + 1) * 128].rearrange("(kc p) m -> p kc m", p=128), wlsq,
                                waits=[wqst_user])
                    cw = p.op("vector", lambda v: v.tensor_copy(wq_bf2[:], wq_st[:]),
                              waits=[ldw.marker, wq_prev])
                    wqst_user = cw.marker
                    mmq = None
                    for tt in range(NTS):
                        qi2, qw2 = ps_pool.acquire()
                        ps_q2 = ps_pool.banks[qi2]
                        for kc in range(8):
                            mmq = p.op("tensor", lambda t, kc=kc, tt=tt, ps_q2=ps_q2: t.matmul(
                                ps_q2[:, :], wq_bf2[:, kc * 128:(kc + 1) * 128], xbf[kc][:, tt * 512:(tt + 1) * 512],
                                start=(kc == 0), stop=(kc == 7)),
                                waits=([cw.marker, x2_ready] if kc == 0 else []) + (qw2 if kc == 0 else []))
                        ps_pool.set_writer(qi2, mmq.marker)
                        mq = p.op("scalar", lambda s, m=m, tt=tt, ps_q2=ps_q2, bofs=bofs: s.activation(
                            out=gtmp[:], in_=ps_q2[:, :], func=AF.Identity, bias=bcol[:, bofs + m:bofs + m + 1],
                            scale=1.0), waits=[mmq.marker] + ([st_prev] if st_prev else []))
                        ps_pool.add_reader(qi2, mq.marker)
                        st = p.dma(oap[m * 128:(m + 1) * 128, sp0 + tt * 512:sp0 + (tt + 1) * 512], gtmp[:], sts,
                                   waits=[mq.marker])
                        st_prev = st.marker
                    wq_prev = mmq.marker
                    prev_sp_users.append(st_prev)

            vst_prev = None
            wv_prev = None
            tensor_snap = Marker("tensor", p.counts["tensor"])
            for m in range(8):
                ldm = p.dma(wq_st[:, :].rearrange("p (kc m) -> p kc m", kc=8),
                            wv1[:, m * 128:(m + 1) * 128].rearrange("(kc p) m -> p kc m", p=128), wlsq,
                            waits=[wqst_user])
                cm = p.op("vector", lambda v, m=m: v.tensor_copy(
                    h_bf[:, 0:8192].rearrange("p (kc u) -> p kc u", kc=8)[:, :, m * 128:(m + 1) * 128],
                    wq_st[:, :].rearrange("p (kc u) -> p kc u", kc=8)),
                    waits=[ldm.marker, tensor_snap, wv_prev])
                wqst_user = cm.marker
                wv_prev = cm.marker
            for tk in range(SP // 128):
                pv1, w1_ = ps_pool.acquire()
                ps_v1 = ps_pool.banks[pv1]
                pv2, w2_ = ps_pool.acquire()
                ps_v2 = ps_pool.banks[pv2]
                mmv = {}
                for half, psv, hw, pvi in ((0, ps_v1, w1_, pv1), (1, ps_v2, w2_, pv2)):
                    mm = None
                    for kc in range(8):
                        mm = p.op("tensor", lambda t, kc=kc, tk=tk, half=half, psv=psv: t.matmul(
                            psv[:, :], xbf[kc][:, tk * 128:(tk + 1) * 128],
                            h_bf[:, kc * 1024 + half * 512:kc * 1024 + (half + 1) * 512],
                            start=(kc == 0), stop=False),
                            waits=([wv_prev] if kc == 0 else []) + (hw if kc == 0 else []))
                    mm = p.op("tensor", lambda t, half=half, psv=psv: t.matmul(
                        psv[:, :], ones1x128[:], bvrow[0:1, half * 512:(half + 1) * 512],
                        start=False, stop=True), waits=[all_lds])
                    mmv[half] = mm
                    ps_pool.set_writer(pvi, mm.marker)
                ev = []
                for half, psv, pvi in ((0, ps_v1, pv1), (1, ps_v2, pv2)):
                    dst = vstage[:, :].rearrange("p (h w) -> p h w", h=16)[:, half * 8:(half + 1) * 8, 0:64]
                    srcv = psv[:, :].rearrange("p (h w) -> p h w", h=8)
                    mvx = p.op("scalar", lambda s, dst=dst, srcv=srcv: s.copy(dst, srcv),
                               waits=[mmv[half].marker] + ([vst_prev] if vst_prev else []))
                    ev.append(mvx)
                    ps_pool.add_reader(pvi, mvx.marker)
                mones = p.op("vector", lambda v: v.memset(
                    vstage[:, :].rearrange("p (h w) -> p h w", h=16)[:, :, 64:65], 1.0),
                    waits=[m.marker for m in ev])
                stv3 = p.dma(o_v[sp0 + tk * 128:sp0 + (tk + 1) * 128, :], vstage[:], sts, waits=[mones.marker])
                vst_prev = stv3.marker
            prev_sp_users.append(vst_prev)
            prev_sp_users.append(Marker("tensor", p.counts["tensor"]))



def emit_ln(p, nc, ps_pool, scr, src_of, dst_write, n_chunks, W0, Tloc, eps_col,
            ones_col, ones1x128, ndim=1024, extra_waits=(), tile_list=None, const_dep=None):
    outm = []
    tiles = tile_list or [(i * W0, min(W0, Tloc - i * W0)) for i in range((Tloc + W0 - 1) // W0)]
    sqb = scr["sqb"]
    stm, stv, strd, stmrs = scr["stm"], scr["stv"], scr["strd"], scr["stmrs"]
    prev_stat_users = scr.setdefault("_prev_stat_users", [])
    assert "stv2" in scr
    for (t0, w) in tiles:
        si, swt = ps_pool.acquire()
        ps_s = ps_pool.banks[si]
        qi, qwt = ps_pool.acquire()
        ps_q = ps_pool.banks[qi]
        mm_s = mm_q = None
        sq_users = [None, None]
        for kc in range(n_chunks):
            srcap = src_of(kc)[:, t0:t0 + w]
            mm_s = p.op("tensor", lambda t, srcap=srcap, kc=kc, ps_s=ps_s, w=w: t.matmul(
                ps_s[0:1, 0:w], ones_col[:], srcap, start=(kc == 0), stop=(kc == n_chunks - 1)),
                waits=(swt if kc == 0 else []) + list(extra_waits) + [const_dep])
            sb = sqb[kc % 2]
            m_sq = p.op("scalar", lambda s, srcap=srcap, sb=sb, w=w: s.square(sb[:, 0:w], srcap),
                        waits=list(extra_waits) + ([sq_users[kc % 2]] if sq_users[kc % 2] else []))
            mm_q = p.op("tensor", lambda t, sb=sb, kc=kc, ps_q=ps_q, w=w: t.matmul(
                ps_q[0:1, 0:w], ones_col[:], sb[:, 0:w], start=(kc == 0), stop=(kc == n_chunks - 1)),
                waits=[m_sq.marker] + (qwt if kc == 0 else []))
            sq_users[kc % 2] = mm_q.marker
        ps_pool.set_writer(si, mm_s.marker)
        ps_pool.set_writer(qi, mm_q.marker)
        m1 = p.op("vector", lambda v, ps_s=ps_s, w=w: v.tensor_scalar_mul(stm[0:1, 0:w], ps_s[0:1, 0:w], 1.0 / ndim),
                  waits=[mm_s.marker, mm_q.marker] + prev_stat_users)
        del prev_stat_users[:]
        ps_pool.add_reader(si, m1.marker)
        m2 = p.op("vector", lambda v, ps_q=ps_q, w=w: v.tensor_scalar_mul(stv[0:1, 0:w], ps_q[0:1, 0:w], 1.0 / ndim),
                  waits=[m1.marker])
        ps_pool.add_reader(qi, m2.marker)
        stv2 = scr["stv2"]
        m3 = p.op("vector", lambda v, w=w: v.tensor_mul(stv2[0:1, 0:w], stm[0:1, 0:w], stm[0:1, 0:w]),
                  waits=[m2.marker])
        m4 = p.op("vector", lambda v, w=w: v.tensor_sub(stv2[0:1, 0:w], stv[0:1, 0:w], stv2[0:1, 0:w]),
                  waits=[m3.marker])
        m5 = p.op("scalar", lambda s, w=w: s.activation(out=stv[0:1, 0:w], in_=stv2[0:1, 0:w], func=AF.Sqrt,
                                                        bias=eps_col[0:1, :], scale=1.0), waits=[m4.marker])
        m6 = p.op("vector", lambda v, w=w: v.reciprocal(strd[0:1, 0:w], stv[0:1, 0:w]), waits=[m5.marker])
        m7 = p.op("vector", lambda v, w=w: v.scalar_tensor_tensor(stmrs[0:1, 0:w], stm[0:1, 0:w], -1.0,
                                                                  strd[0:1, 0:w], ALU.mult, ALU.mult), waits=[m6.marker])
        bi1, bw1 = ps_pool.acquire()
        ps_b1 = ps_pool.banks[bi1]
        mb1 = p.op("tensor", lambda t, ps_b1=ps_b1, w=w: t.matmul(ps_b1[0:128, 0:w], ones1x128[:], strd[0:1, 0:w],
                                                                  start=True, stop=True),
                   waits=[m6.marker, const_dep] + bw1)
        ps_pool.set_writer(bi1, mb1.marker)
        bi2, bw2 = ps_pool.acquire()
        ps_b2 = ps_pool.banks[bi2]
        mb2 = p.op("tensor", lambda t, ps_b2=ps_b2, w=w: t.matmul(ps_b2[0:128, 0:w], ones1x128[:], stmrs[0:1, 0:w],
                                                                  start=True, stop=True), waits=[m7.marker] + bw2)
        ps_pool.set_writer(bi2, mb2.marker)
        prev_stat_users.extend([mb1.marker, mb2.marker])
        prev_dst = scr.get("_prev_dst")
        for kc in range(n_chunks):
            mo = dst_write(kc, (t0, w), ps_b1, ps_b2,
                           [mb1.marker, mb2.marker] + ([prev_dst] if prev_dst else []))
            prev_dst = mo.marker
            ps_pool.add_reader(bi1, mo.marker)
            ps_pool.add_reader(bi2, mo.marker)
            outm.append(mo)
        scr["_prev_dst"] = prev_dst
    return outm


def emit_precast(p, nc, ctx, sm, pairs):
    e = ctx.enter_context
    CW = 1408
    fstg = [e(nc.sbuf_tensor(f"pcf{i}", [128, CW], F32)) for i in range(2)]
    bstg = [e(nc.sbuf_tensor(f"pcb{i}", [128, CW], BF16)) for i in range(2)]
    xls = sm["xls"]
    sts = sm["sts"]
    slot_cv = [None, None]
    slot_st = [None, None]
    si = 0
    for (src, dst, R, C) in pairs:
        for r in range(0, R, 128):
            for c0 in range(0, C, CW):
                w = min(CW, C - c0)
                s = si % 2
                si += 1
                ld = p.dma(fstg[s][:, 0:w], src[r:r + 128, c0:c0 + w], xls,
                           waits=[slot_cv[s]])
                cv = p.op("vector", lambda v, s=s, w=w: v.tensor_copy(bstg[s][:, 0:w], fstg[s][:, 0:w]),
                          waits=[ld.marker, slot_st[s]])
                slot_cv[s] = cv.marker
                st = p.dma(dst[r:r + 128, c0:c0 + w], bstg[s][:, 0:w], sts, waits=[cv.marker])
                slot_st[s] = st.marker



def emit_C(p, nc, gctx, sm, io, T):
    q_in = io["c_q_c"]
    k_in = io["c_k_c"]
    v_in = io["c_v_c"]
    c_out = io["cc_c_in"]
    lds = sm["lds"]
    sts = sm["sts"]

    with ExitStack() as ctx:
        e = ctx.enter_context
        q_sb = [e(nc.sbuf_tensor(f"q_sbC{i}", [128, T], BF16)) for i in range(2)]
        k_sb = [e(nc.sbuf_tensor(f"k_sbC{i}", [128, T], BF16)) for i in range(2)]
        v_bf = e(nc.sbuf_tensor("v_bfC", [128, (T // 128) * 260], BF16))
        o_bf = [e(nc.sbuf_tensor(f"o_bfC{i}", [128, T], BF16)) for i in range(2)]
        sqb = e(nc.sbuf_tensor("sqbC", [128, 512], F32))
        accq = e(nc.sbuf_tensor("accq", [2, 4], F32))
        accr = e(nc.sbuf_tensor("accr", [2, 4], F32))
        nrow = e(nc.sbuf_tensor("nrow", [1, 4], F32))
        bias_cols = e(nc.sbuf_tensor("bias_cols", [128, 4], F32))
        pt = [e(nc.sbuf_tensor(f"ptC{i}", [128, 512], BF16)) for i in range(4)]
        recl = e(nc.sbuf_tensor("reclC", [1, 512], F32))
        lbc = e(nc.sbuf_tensor("lbcC", [64, 512], F32))
        mask_sb = e(nc.sbuf_tensor("mask_sbC", [128, 4 * 512], BF16))
        ps_pool = PsumPool(nc, ctx, name="psC")
        ot_banks = [e(nc.psum_tensor(f"otC{i}", [128, 512], F32)) for i in range(2)]

        ones1x128 = const_sbuf(nc, p, gctx, lds, "ones1x128", np.ones((1, 128), np.float32))
        sel64 = const_sbuf(nc, p, gctx, lds, "sel64", sel_matrix(64))
        ident2 = const_sbuf(nc, p, gctx, lds, "ident2", np.eye(2, dtype=np.float32))
        row8 = const_sbuf(nc, p, gctx, lds, "row8", np.full((1, 128), 0.125, np.float32))
        dm = diag_masks_np()
        dm_dram = nc.inline_tensor(dm, "dmasksC_c").ap()
        for j in range(4):
            p.dma(mask_sb[:, j * 512:(j + 1) * 512], dm_dram[j], lds)

        for ch in range(2):
            p.dma(q_sb[ch][:, :], q_in[ch * 128:(ch + 1) * 128, :], lds)
            p.dma(k_sb[ch][:, :], k_in[ch * 128:(ch + 1) * 128, :], lds)
        for tk in range(T // 128):
            p.dma(v_bf[:, tk * 260:(tk + 1) * 260], v_in[tk * 128:(tk + 1) * 128, :], lds)
        all_lds = Marker(lds, lds.count)
        if "precast_pairs" in io:
            emit_precast(p, nc, ctx, sm, io["precast_pairs"])

        minit_q = p.op("vector", lambda v: v.memset(accq[:, :], 0.0), waits=[])
        minit_r = p.op("vector", lambda v: v.memset(accr[:, :], 0.0), waits=[])
        bias_ready = []
        sq_user = None
        for src_sb, acc, mi in ((q_sb, accq, minit_q), (k_sb, accr, minit_r)):
            for ch in range(2):
                for tt in range(T // 512):
                    m_sq = p.op("scalar", lambda s, src_sb=src_sb, ch=ch, tt=tt: s.square(
                        sqb[:, 0:512], src_sb[ch][:, tt * 512:(tt + 1) * 512]),
                        waits=[all_lds, sq_user, mi.marker])
                    si, sw = ps_pool.acquire()
                    psS = ps_pool.banks[si]
                    m_st = p.op("tensor", lambda t, psS=psS: t.matmul(psS[0:2, 0:512], sel64[:], sqb[:, 0:512],
                                                                     start=True, stop=True),
                                waits=[m_sq.marker] + sw)
                    ps_pool.set_writer(si, m_st.marker)
                    sq_user = m_st.marker
                    m_rd = p.op("vector", lambda v, psS=psS, acc=acc, ch=ch: v.reduce_max(
                        (accq if acc is accq else accr)[0:2, 2:3], psS[0:2, 0:512], axis=AX.X),
                        waits=[m_st.marker])
                    ps_pool.add_reader(si, m_rd.marker)
                    m_mx = p.op("vector", lambda v, acc=acc, ch=ch: v.tensor_max(
                        acc[0:2, ch:ch + 1], acc[0:2, ch:ch + 1], (accq if acc is accq else accr)[0:2, 2:3]),
                        waits=[m_rd.marker])
                    sq_user = m_mx.marker
        for ch in range(2):
            m1 = p.op("vector", lambda v, ch=ch: v.tensor_mul(accq[0:2, 3:4], accq[0:2, ch:ch + 1],
                                                              accr[0:2, ch:ch + 1]), waits=[sq_user])
            m2 = p.op("scalar", lambda s: s.sqrt(accq[0:2, 3:4], accq[0:2, 3:4]), waits=[m1.marker])
            m3 = p.op("vector", lambda v: v.tensor_scalar_mul(accq[0:2, 3:4], accq[0:2, 3:4], -0.125),
                      waits=[m2.marker])
            ti, tw = ps_pool.acquire()
            psT = ps_pool.banks[ti]
            m4 = p.op("tensor", lambda t, psT=psT: t.matmul(psT[0:1, 0:2], accq[0:2, 3:4], ident2[:],
                                                            start=True, stop=True), waits=[m3.marker] + tw)
            ps_pool.set_writer(ti, m4.marker)
            m5 = p.op("scalar", lambda s, ch=ch, psT=psT: s.copy(nrow[0:1, ch * 2:(ch + 1) * 2], psT[0:1, 0:2]),
                      waits=[m4.marker])
            ps_pool.add_reader(ti, m5.marker)
            sq_user = m5.marker
        for h in range(4):
            ch, hw = h // 2, h % 2
            bi, bw = ps_pool.acquire()
            psb = ps_pool.banks[bi]
            mb = p.op("tensor", lambda t, h=h, ch=ch, hw=hw, psb=psb: t.matmul(
                psb[0:128, 0:1], ones1x128[:], nrow[0:1, ch * 2 + hw:ch * 2 + hw + 1],
                start=True, stop=True), waits=[sq_user] + bw)
            ps_pool.set_writer(bi, mb.marker)
            mc = p.op("scalar", lambda s, h=h, psb=psb: s.copy(bias_cols[:, h:h + 1], psb[0:128, 0:1]),
                      waits=[mb.marker])
            ps_pool.add_reader(bi, mc.marker)
            bias_ready.append(mc.marker)

        def mk_head(h):
            def out_write(qb, ps_ot, deps, h=h):
                ob = o_bf[h // 2]
                ooff = 64 * (h % 2)
                return p.op("vector", lambda v, qb=qb, ps_ot=ps_ot, ob=ob, ooff=ooff: v.tensor_mul(
                    ob[ooff:ooff + 64, qb * 512:(qb + 1) * 512], ps_ot[0:64, 0:512], lbc[0:64, :]), waits=deps)

            return {
                "k_lhsT": lambda c, h=h: k_sb[h // 2][64 * (h % 2):64 * (h % 2) + 64, c * 128:(c + 1) * 128],
                "q_rhs": lambda qb, h=h: q_sb[h // 2][64 * (h % 2):64 * (h % 2) + 64, qb * 512:(qb + 1) * 512],
                "v_lhsT": lambda c, h=h: v_bf[:, c * 260 + 65 * h:c * 260 + 65 * (h + 1)],
                "out_write": out_write,
                "bias_col": bias_cols[:, h:h + 1],
                "bias_dep": bias_ready[h],
                "qkv_dep": all_lds,
            }

        cfg = {
            "T": T, "hd": 64, "scale": 0.125, "mask_sb": mask_sb,
            "pt": pt, "recl": recl, "lbc": lbc, "ones1x128": ones1x128,
            "lbc_row": row8,
            "ot_banks": ot_banks,
            "heads": [mk_head(h) for h in range(4)],
        }
        emit_attention(p, nc, ps_pool, cfg)
        m_all_v = Marker("vector", p.counts["vector"])
        p.dma(c_out[0:128, :], o_bf[0][:, :], sts, waits=[m_all_v])
        p.dma(c_out[128:256, :], o_bf[1][:, :], sts, waits=[m_all_v])



def emit_DE(p, nc, gctx, sm, io, T, is_D):
    S = T // 4
    TE = S + XH
    SS = min(S, 1024)
    pf = "d_" if is_D else "e_"
    tg = "D" if is_D else "E"

    xe = io[pf + "xe"]
    if is_D:
        attne = io["d_attne_c"]
        proj = io["d_projb"]
    wqkv = io[pf + "wqkvb"]
    qs_c = io[pf + "qs"]
    ks_c = io[pf + "ks"]
    qsrow = io[pf + "qsrow"]
    ksrow = io[pf + "ksrow"]
    wo_i = io[pf + "wob"]
    lng = io[pf + "lng"]
    lnb = io[pf + "lnb"]
    w1_i = io[pf + "w1b"]
    b1a_i = io[pf + "b1a"]
    b1g_i = io[pf + "b1g"]
    w2_i = io[pf + "w2b"]
    b2_i = io[pf + "b2"]
    masks_i = io["m_masks"]
    ind_i = io["m_ind"]
    tabs_i = io["m_tabs"]
    if not is_D:
        x3own = io["e_x3"]
    if is_D:
        o_xn = io["xa_d_own"]
        x3d = io["x3e_d"]
    else:
        o_out = io["e_out"]
        x3d = xe
    xmd = io["xmd_" + tg]

    ext_tiles = [(t0, min(512, TE - t0)) for t0 in range(0, TE, 512)]

    lds = sm["lds"]
    xls = sm["xls"]
    wl1 = sm["wl1"]
    wl2 = sm["wl2"]
    wl3 = sm["wl3"]
    tbl = sm["tbl"]
    mds = sm["mds"]
    sts = sm["sts"]

    with ExitStack() as ctx:
        e = ctx.enter_context
        sqb = [e(nc.sbuf_tensor(f"sqb{tg}{i}", [128, 512], F32)) for i in range(2)]
        stm = e(nc.sbuf_tensor(f"stm{tg}", [1, 512], F32))
        stv = e(nc.sbuf_tensor(f"stv{tg}", [1, 512], F32))
        strd = e(nc.sbuf_tensor(f"strd{tg}", [1, 512], F32))
        stmrs = e(nc.sbuf_tensor(f"stmrs{tg}", [1, 512], F32))
        stv2 = e(nc.sbuf_tensor(f"stv2{tg}", [1, 512], F32))
        st1 = e(nc.sbuf_tensor(f"st1{tg}", [2, 512], F32))
        scr = {"sqb": sqb, "stm": stm, "stv": stv, "strd": strd, "stmrs": stmrs, "stv2": stv2}
        ln_tmp = e(nc.sbuf_tensor(f"ln_tmp{tg}", [128, 512], F32))
        qtmp = e(nc.sbuf_tensor(f"qtmp{tg}", [128, 512], F32))
        qn_f = e(nc.sbuf_tensor(f"qn_f{tg}", [128, 512], F32))
        rt1 = e(nc.sbuf_tensor(f"rt1{tg}", [128, 512], F32))
        rt2 = e(nc.sbuf_tensor(f"rt2{tg}", [128, 512], F32))
        wbf2 = [e(nc.sbuf_tensor(f"wbf{tg}{i}", [128, 8 * 128], BF16)) for i in range(2)]
        wo_bf = e(nc.sbuf_tensor(f"wo_bf{tg}", [128, 4 * 1024], BF16))
        qscol = e(nc.sbuf_tensor(f"qscol{tg}", [128, 1], F32))
        kscol = e(nc.sbuf_tensor(f"kscol{tg}", [128, 1], F32))
        lngc = e(nc.sbuf_tensor(f"lngc{tg}", [128, 8], F32))
        lnbc = e(nc.sbuf_tensor(f"lnbc{tg}", [128, 8], F32))
        b1ac = e(nc.sbuf_tensor(f"b1ac{tg}", [128, 22], F32))
        b1gc = e(nc.sbuf_tensor(f"b1gc{tg}", [128, 22], F32))
        b2c = e(nc.sbuf_tensor(f"b2c{tg}", [128, 8], F32))
        indc = e(nc.sbuf_tensor(f"indc{tg}", [128, 1], F32))
        gkrow = e(nc.sbuf_tensor(f"gkrow{tg}", [1, 128], F32))
        red = e(nc.sbuf_tensor(f"red{tg}", [1, 8], F32))
        bias_col = e(nc.sbuf_tensor(f"biascol{tg}", [128, 1], F32))
        mask_sb = e(nc.sbuf_tensor(f"mask_sb{tg}", [128, 16 * 512], BF16))
        pt = [e(nc.sbuf_tensor(f"pt{tg}{i}", [128, 512], BF16)) for i in range(4)]
        recl = e(nc.sbuf_tensor(f"recl{tg}", [1, 512], F32))
        lbc = e(nc.sbuf_tensor(f"lbc{tg}", [64, 512], F32))
        o1_sb = e(nc.sbuf_tensor(f"o1_sb{tg}", [128, 4], BF16))
        xm1_f = e(nc.sbuf_tensor(f"xm1_f{tg}", [128, 8], F32))
        o_bf = [e(nc.sbuf_tensor(f"o{tg}{i}", [128, SS], BF16)) for i in range(4)]
        xbfstg = [e(nc.sbuf_tensor(f"xbfstg{tg}{i}", [128, 512], BF16)) for i in range(8)]
        xstg = [e(nc.sbuf_tensor(f"xstg{tg}{i}", [128, 513], F32)) for i in range(8)]
        ps_pool = PsumPool(nc, ctx, name=f"ps{tg}")
        ot_banks = [e(nc.psum_tensor(f"ot{tg}{i}", [128, 512], F32)) for i in range(2)]

        ones_col = const_sbuf(nc, p, gctx, lds, "ones_col", np.ones((128, 1), np.float32))
        ones1x128 = const_sbuf(nc, p, gctx, lds, "ones1x128", np.ones((1, 128), np.float32))
        eps5 = const_sbuf(nc, p, gctx, lds, "eps5", np.full((128, 1), 1e-5, np.float32))
        sel64 = const_sbuf(nc, p, gctx, lds, "sel64", sel_matrix(64))
        selT64 = const_sbuf(nc, p, gctx, lds, "selT64", sel_matrix(64).T.copy())
        perm = np.zeros((128, 128), np.float32)
        for pp in range(128):
            perm[(pp // 64) * 64 + (pp + 32) % 64, pp] = 1.0
        permM = const_sbuf(nc, p, gctx, lds, "permM", perm)
        m1np = np.zeros((128, 5), np.float32)
        m1np[127, 0] = 1.0
        m1np[:, 1:] = 1.0
        mask1 = const_sbuf(nc, p, gctx, lds, "mask1", m1np.astype(BF16NP))
        identf = const_sbuf(nc, p, gctx, lds, "identf", np.eye(128, dtype=np.float32))

        p.dma(qscol[:], qs_c, lds)
        p.dma(kscol[:], ks_c, lds)
        p.dma(lngc[:], lng, lds)
        p.dma(lnbc[:], lnb, lds)
        p.dma(b1ac[:], b1a_i, lds)
        p.dma(b1gc[:], b1g_i, lds)
        p.dma(b2c[:], b2_i, lds)
        p.dma(indc[:], ind_i, lds)
        p.dma(gkrow[0:1, 0:64], qsrow, lds)
        p.dma(gkrow[0:1, 64:128], ksrow, lds)
        for j in range(16):
            p.dma(mask_sb[:, j * 512:(j + 1) * 512], masks_i[j], lds)
        all_lds = Marker(lds, lds.count)

        r1 = p.op("vector", lambda v: v.reduce_max(red[0:1, 0:1], gkrow[0:1, 0:64], axis=AX.X,
                                                   apply_absolute_value=True), waits=[all_lds])
        r2 = p.op("vector", lambda v: v.reduce_max(red[0:1, 1:2], gkrow[0:1, 64:128], axis=AX.X,
                                                   apply_absolute_value=True), waits=[r1.marker])
        r3 = p.op("vector", lambda v: v.tensor_mul(red[0:1, 2:3], red[0:1, 0:1], red[0:1, 1:2]), waits=[r2.marker])
        r4 = p.op("vector", lambda v: v.tensor_scalar_mul(red[0:1, 3:4], red[0:1, 2:3], -8.0), waits=[r3.marker])
        bi, bw = ps_pool.acquire()
        psb = ps_pool.banks[bi]
        r5 = p.op("tensor", lambda t: t.matmul(psb[0:128, 0:1], ones1x128[:], red[0:1, 3:4], start=True, stop=True),
                  waits=[r4.marker] + bw)
        ps_pool.set_writer(bi, r5.marker)
        r6 = p.op("scalar", lambda s: s.copy(bias_col[:], psb[0:128, 0:1]), waits=[r5.marker])
        ps_pool.add_reader(bi, r6.marker)
        bias_ready = r6.marker

        for kc in range(4):
            lw = p.dma(wo_bf[:, kc * 1024:(kc + 1) * 1024], wo_i[kc * 128:(kc + 1) * 128, :], wl1)
        wo_ready = Marker(wl1, wl1.count)

        if is_D:
            with ExitStack() as s1:
                e1 = s1.enter_context
                projbf = e1(nc.sbuf_tensor("projbf", [128, 8 * 1024], BF16))
                x3t = [e1(nc.sbuf_tensor(f"x3t{i}", [128, 512], F32)) for i in range(8)]
                for kc in range(8):
                    lw = p.dma(projbf[:, kc * 1024:(kc + 1) * 1024], proj[kc * 128:(kc + 1) * 128, :], wl1)
                proj_ready = Marker(wl1, wl1.count)
                xe_prev = None
                st_prev = []
                for (t0, w) in ext_tiles:
                    lxs = [p.dma(xstg[kc][:, 0:w], xe[kc * 128:(kc + 1) * 128, t0:t0 + w], xls,
                                 waits=[xe_prev]) for kc in range(8)]
                    las = [p.dma(xbfstg[kc][:, 0:w], attne[kc * 128:(kc + 1) * 128, t0:t0 + w], xls,
                                 waits=[xe_prev]) for kc in range(8)]
                    x_ld = Marker(xls, xls.count)
                    vdone = []
                    mm = None
                    for m in range(8):
                        ai, aw = ps_pool.acquire()
                        ps = ps_pool.banks[ai]
                        for kc in range(8):
                            mm = p.op("tensor", lambda t, kc=kc, m=m, w=w, ps=ps: t.matmul(
                                ps[:, 0:w], projbf[:, kc * 1024 + m * 128:kc * 1024 + (m + 1) * 128],
                                xbfstg[kc][:, 0:w], start=(kc == 0), stop=(kc == 7)),
                                waits=([proj_ready, x_ld] if kc == 0 else []) + (aw if kc == 0 else []))
                        ps_pool.set_writer(ai, mm.marker)
                        mo = p.op("vector", lambda v, m=m, w=w, ps=ps: v.tensor_add(
                            x3t[m][:, 0:w], xstg[m][:, 0:w], ps[:, 0:w]),
                            waits=[mm.marker] + st_prev)
                        ps_pool.add_reader(ai, mo.marker)
                        vdone.append(mo.marker)
                    xe_prev = vdone[-1]

                    def dstw(kc, tw, ps_b1, ps_b2, deps):
                        tt0, ww = tw
                        a1 = p.op("vector", lambda v, kc=kc, ww=ww: v.tensor_mul(
                            ln_tmp[:, 0:ww], x3t[kc][:, 0:ww], ps_b1[0:128, 0:ww]), waits=deps)
                        a2 = p.op("vector", lambda v, kc=kc, ww=ww: v.tensor_add(
                            x3t[kc][:, 0:ww], ln_tmp[:, 0:ww], ps_b2[0:128, 0:ww]), waits=[a1.marker])
                        return a2

                    lnm = emit_ln(p, nc, ps_pool, scr, lambda kc: x3t[kc][:, :], dstw, 8, 512, w, eps5,
                                  ones_col, ones1x128, extra_waits=[Marker("vector", p.counts["vector"])],
                                  tile_list=[(0, w)], const_dep=all_lds)
                    ln_last = lnm[-1].marker
                    st_prev = []
                    for kc in range(8):
                        stx = p.dma(x3d[kc * 128:(kc + 1) * 128, t0:t0 + w], x3t[kc][:, 0:w], mds,
                                    waits=[ln_last])
                        st_prev.append(stx.marker)
                x3_wr = Marker(mds, mds.count)
        else:
            x3_wr = None

        NSP = S // SS
        QTs = SS + QH
        NVs = QTs // 128
        NWs = SS // 512
        qkv_tiles = [(t0, min(512, QTs - t0)) for t0 in range(0, QTs, 512)]
        lbc_user = None
        xm_wr = []
        for sp in range(NSP):
            sp0 = sp * SS
            with ExitStack() as s2:
                e2 = s2.enter_context
                q_rot = [e2(nc.sbuf_tensor(f"q_rot{tg}{sp}_{i}", [128, QTs], BF16)) for i in range(4)]
                k_rot = [e2(nc.sbuf_tensor(f"k_rot{tg}{sp}_{i}", [128, QTs], BF16)) for i in range(4)]
                v_bf = e2(nc.sbuf_tensor(f"v_bf{tg}{sp}", [128, NVs * 520], BF16))
                wvres = e2(nc.sbuf_tensor(f"wvres{tg}{sp}", [128, 4096], BF16))
                tabstg = e2(nc.sbuf_tensor(f"tabstg{tg}{sp}", [128, 4 * 512], F32))

                wq_prev = None
                wslot_user = [None, None]
                tab_prev = None
                qk_done = []
                v_done = []
                xbf_user = None
                wv_ready = None
                mmv = None
                for (t0, w) in qkv_tiles:
                    base = sp0 + t0
                    lx3 = [p.dma(xstg[kc][:, 0:w + 1], x3d[kc * 128:(kc + 1) * 128, base:base + w + 1], xls,
                                 waits=[xbf_user, x3_wr]) for kc in range(8)]
                    x_ld = Marker(xls, xls.count)
                    casts = []
                    for kc in range(8):
                        off = 1 if kc < 4 else 0
                        cm = p.op("vector", lambda v, kc=kc, w=w, off=off: v.tensor_copy(
                            xbfstg[kc][:, 0:w], xstg[kc][:, off:off + w]), waits=[x_ld])
                        casts.append(cm.marker)
                    cast_done = casts[-1]
                    xbf_user = cast_done
                    lt = []
                    for ti in range(4):
                        lt.append(p.dma(tabstg[:, ti * 512:ti * 512 + w], tabs_i[ti][:, sp0 + t0:sp0 + t0 + w],
                                        tbl, waits=[tab_prev]))
                    tab_ld = Marker(tbl, tbl.count)
                    for oc in range(8):
                        is_q = oc < 4
                        wsl = oc % 2
                        wb = wbf2[wsl]
                        lw = p.dma(wb[:, :].rearrange("p (kc m) -> p kc m", kc=8),
                                   wqkv[:, oc * 128:(oc + 1) * 128].rearrange("(kc p) m -> p kc m", p=128), wl1,
                                   waits=[wslot_user[wsl]])
                        ai, aw = ps_pool.acquire()
                        ps = ps_pool.banks[ai]
                        mm = None
                        for kc in range(8):
                            mm = p.op("tensor", lambda t, kc=kc, w=w, ps=ps, wb=wb: t.matmul(
                                ps[:, 0:w], wb[:, kc * 128:(kc + 1) * 128], xbfstg[kc][:, 0:w],
                                start=(kc == 0), stop=(kc == 7)),
                                waits=([lw.marker, cast_done] if kc == 0 else []) + (aw if kc == 0 else []))
                        wq_prev = mm.marker
                        wslot_user[wsl] = mm.marker
                        ps_pool.set_writer(ai, mm.marker)
                        cp = p.op("scalar", lambda s, w=w, ps=ps: s.copy(qtmp[:, 0:w], ps[:, 0:w]),
                                  waits=[mm.marker] + qk_done[-1:])
                        ps_pool.add_reader(ai, cp.marker)
                        gcol = qscol if is_q else kscol
                        scrD = {"sq": sqb[0], "st1": st1, "sel": sel64, "selT": selT64}
                        m_ap = emit_group_norm_apply(p, nc, ps_pool, qtmp[:, 0:w], qn_f[:, 0:w], 64, gcol[:, :],
                                                     scrD, "l2", extra_waits=[cp.marker], clip=1e-12)
                        si2, sw2 = ps_pool.acquire()
                        psw = ps_pool.banks[si2]
                        msw = p.op("tensor", lambda t, w=w, psw=psw: t.matmul(psw[:, 0:w], permM[:], qn_f[:, 0:w],
                                                                             start=True, stop=True),
                                   waits=[m_ap.marker] + sw2)
                        ps_pool.set_writer(si2, msw.marker)
                        ta = 0 if is_q else 2 * 512
                        tb = 512 if is_q else 3 * 512
                        mr1 = p.op("vector", lambda v, w=w, ta=ta: v.tensor_mul(rt1[:, 0:w], qn_f[:, 0:w],
                                                                               tabstg[:, ta:ta + w]),
                                   waits=[m_ap.marker, tab_ld])
                        mr2 = p.op("vector", lambda v, w=w, tb=tb, psw=psw: v.tensor_mul(
                            rt2[:, 0:w], psw[:, 0:w], tabstg[:, tb:tb + w]),
                            waits=[msw.marker, tab_ld, mr1.marker])
                        ps_pool.add_reader(si2, mr2.marker)
                        dstt = (q_rot if is_q else k_rot)[oc % 4]
                        mr3 = p.op("vector", lambda v, w=w, t0=t0, dstt=dstt: v.tensor_add(
                            dstt[:, t0:t0 + w], rt1[:, 0:w], rt2[:, 0:w]), waits=[mr2.marker])
                        qk_done.append(mr3.marker)
                    tab_prev = qk_done[-1]
                    if t0 == 0:
                        for vc in range(4):
                            lwv = p.dma(wvres[:, 0:4096].rearrange("p (kc u) -> p kc u", kc=8)[:, :, vc * 128:(vc + 1) * 128],
                                        wqkv[:, 1024 + vc * 128:1024 + (vc + 1) * 128].rearrange(
                                            "(kc p) m -> p kc m", p=128), wl2)
                        wv_ready = Marker(wl2, wl2.count)
                    for st4 in range(w // 128):
                        vtile = (t0 + st4 * 128) // 128
                        vi, vw2 = ps_pool.acquire()
                        psv = ps_pool.banks[vi]
                        for kc in range(8):
                            mmv = p.op("tensor", lambda t, kc=kc, st4=st4, psv=psv: t.matmul(
                                psv[:, :], xbfstg[kc][:, st4 * 128:(st4 + 1) * 128],
                                wvres[:, kc * 512:(kc + 1) * 512],
                                start=(kc == 0), stop=(kc == 7)),
                                waits=([wv_ready, cast_done] if kc == 0 else []) + (vw2 if kc == 0 else []))
                        ps_pool.set_writer(vi, mmv.marker)
                        dst = v_bf[:, vtile * 520:(vtile + 1) * 520].rearrange("p (h u) -> p h u", h=8)[:, :, 0:64]
                        srcv = psv[:, :].rearrange("p (h u) -> p h u", h=8)
                        cpv = p.op("scalar", lambda s, dst=dst, srcv=srcv: s.copy(dst, srcv),
                                   waits=[mmv.marker] + v_done[-1:])
                        ps_pool.add_reader(vi, cpv.marker)
                        v_done.append(cpv.marker)
                    xbf_user = mmv.marker
                m_ones = p.op("vector", lambda v: v.memset(
                    v_bf[:, :].rearrange("p (t h u) -> p t h u", h=8, u=65)[:, :, :, 64:65], 1.0),
                    waits=v_done[-1:])
                v_ready = m_ones.marker
                qk_ready = Marker("vector", p.counts["vector"])

                ot_state = [{"w": None, "r": []}, {"w": None, "r": []}]
                seq = 0
                prev_exp = {}
                pend_epi = [None]
                recl_user = [None]
                for wdx in range(NWs):
                    gw0 = (sp0 // 512 + wdx == 0)
                    for h in range(8):
                        osl = seq % 2
                        seq += 1
                        ps_ot = ot_banks[osl]
                        ow = list(ot_state[osl]["r"])
                        if ot_state[osl]["w"] is not None:
                            ow.append(ot_state[osl]["w"])
                        ot_state[osl] = {"w": None, "r": []}
                        kth = k_rot[h // 2]
                        koff = 64 * (h % 2)
                        qth = q_rot[h // 2]
                        kbase = 512 * wdx + 128
                        qcol = 512 * wdx + QH
                        pv_last = None
                        pend = []
                        for c in range(8):
                            ai, aw = ps_pool.acquire()
                            ps_a = ps_pool.banks[ai]
                            mm = p.op("tensor", lambda t, c=c, ps_a=ps_a, kth=kth, koff=koff, qth=qth,
                                      kbase=kbase, qcol=qcol: t.matmul(
                                ps_a[:, :], kth[koff:koff + 64, kbase + c * 128:kbase + (c + 1) * 128],
                                qth[koff:koff + 64, qcol:qcol + 512], start=True, stop=True),
                                waits=aw + [qk_ready, bias_ready] + ow)
                            ow = []
                            ps_pool.set_writer(ai, mm.marker)
                            buf = pt[c % 4]
                            ex = p.op("scalar", lambda s, ps_a=ps_a, buf=buf: s.activation(
                                out=buf[:], in_=ps_a[:, :], func=AF.Exp, bias=bias_col[:], scale=8.0),
                                waits=[mm.marker] + ([prev_exp.get(c % 4)] if prev_exp.get(c % 4) else []))
                            ps_pool.add_reader(ai, ex.marker)
                            mvar = (8 + c) if gw0 else c
                            mk = p.op("vector", lambda v, buf=buf, mvar=mvar: v.tensor_mul(
                                buf[:], buf[:], mask_sb[:, mvar * 512:(mvar + 1) * 512]),
                                waits=[ex.marker, all_lds])
                            if c == 0 and pend_epi[0] is not None:
                                pend_epi[0]()
                                pend_epi[0] = None
                            pend.append((mk.marker, buf, c))
                            if len(pend) > 3:
                                d0, b0, c0 = pend.pop(0)
                                pv_last = p.op("tensor", lambda t, c0=c0, b0=b0, ps_ot=ps_ot, wdx=wdx, h=h: t.matmul(
                                    ps_ot[0:65, :],
                                    v_bf[:, (4 * wdx + 1 + c0) * 520 + 65 * h:(4 * wdx + 1 + c0) * 520 + 65 * (h + 1)],
                                    b0[:], start=(c0 == 0), stop=(c0 == 7)),
                                    waits=[d0, v_ready])
                                prev_exp[c0 % 4] = pv_last.marker
                        while pend:
                            d0, b0, c0 = pend.pop(0)
                            pv_last = p.op("tensor", lambda t, c0=c0, b0=b0, ps_ot=ps_ot, wdx=wdx, h=h: t.matmul(
                                ps_ot[0:65, :],
                                v_bf[:, (4 * wdx + 1 + c0) * 520 + 65 * h:(4 * wdx + 1 + c0) * 520 + 65 * (h + 1)],
                                b0[:], start=(c0 == 0), stop=(c0 == 7)),
                                waits=[d0, v_ready])
                            prev_exp[c0 % 4] = pv_last.marker
                        ot_state[osl]["w"] = pv_last.marker
                        m_rc = p.op("vector", lambda v, ps_ot=ps_ot: v.reciprocal(recl[:], ps_ot[64:65, 0:512]),
                                    waits=[pv_last.marker] + ([recl_user[0]] if recl_user[0] else []))
                        ot_state[osl]["r"] = [m_rc.marker]

                        def mk_epi(osl=osl, ps_ot=ps_ot, m_rc=m_rc, h=h, wdx=wdx):
                            def epi():
                                nonlocal lbc_user
                                bi2, bw2 = ps_pool.acquire()
                                ps_b = ps_pool.banks[bi2]
                                m_bc = p.op("tensor", lambda t, ps_b=ps_b: t.matmul(
                                    ps_b[0:64, :], ones1x128[0:1, 0:64], recl[:], start=True, stop=True),
                                    waits=[m_rc.marker] + bw2)
                                recl_user[0] = m_bc.marker
                                ps_pool.set_writer(bi2, m_bc.marker)
                                m_cp = p.op("scalar", lambda s, ps_b=ps_b: s.copy(lbc[0:64, :], ps_b[0:64, :]),
                                            waits=[m_bc.marker, lbc_user])
                                ps_pool.add_reader(bi2, m_cp.marker)
                                mo = p.op("vector", lambda v, ps_ot=ps_ot, h=h, wdx=wdx: v.tensor_mul(
                                    o_bf[h // 2][64 * (h % 2):64 * (h % 2) + 64, 512 * wdx:512 * (wdx + 1)],
                                    ps_ot[0:64, 0:512], lbc[0:64, :]), waits=[m_cp.marker])
                                lbc_user = mo.marker
                                ot_state[osl]["r"].append(mo.marker)
                            return epi

                        pend_epi[0] = mk_epi()
                if pend_epi[0] is not None:
                    pend_epi[0]()
                    pend_epi[0] = None
                attn_done = Marker("vector", p.counts["vector"])

                if sp == 0:
                    o1done = []
                    for h in range(8):
                        kth = k_rot[h // 2]
                        koff = 64 * (h % 2)
                        qth = q_rot[h // 2]
                        ai, aw = ps_pool.acquire()
                        ps_a = ps_pool.banks[ai]
                        mmz = None
                        for c in range(5):
                            mmz = p.op("tensor", lambda t, c=c, ps_a=ps_a, kth=kth, koff=koff, qth=qth: t.matmul(
                                ps_a[:, c:c + 1], kth[koff:koff + 64, c * 128:(c + 1) * 128],
                                qth[koff:koff + 64, QH - 1:QH], start=True, stop=True),
                                waits=(aw + [qk_ready, bias_ready, attn_done]) if c == 0 else [])
                        ps_pool.set_writer(ai, mmz.marker)
                        ex = p.op("scalar", lambda s, ps_a=ps_a: s.activation(
                            out=pt[0][:, 0:5], in_=ps_a[:, 0:5], func=AF.Exp, bias=bias_col[:], scale=8.0),
                            waits=[mmz.marker] + o1done[-1:])
                        ps_pool.add_reader(ai, ex.marker)
                        mk = p.op("vector", lambda v: v.tensor_mul(pt[0][:, 0:5], pt[0][:, 0:5], mask1[:]),
                                  waits=[ex.marker])
                        oi2, ow2 = ps_pool.acquire()
                        ps_o1 = ps_pool.banks[oi2]
                        pvz = None
                        for c in range(5):
                            pvz = p.op("tensor", lambda t, c=c, ps_o1=ps_o1, h=h: t.matmul(
                                ps_o1[0:65, 0:1], v_bf[:, c * 520 + 65 * h:c * 520 + 65 * (h + 1)],
                                pt[0][:, c:c + 1], start=(c == 0), stop=(c == 4)),
                                waits=([mk.marker, v_ready] + ow2) if c == 0 else [])
                        ps_pool.set_writer(oi2, pvz.marker)
                        rc1 = p.op("vector", lambda v, ps_o1=ps_o1: v.reciprocal(recl[0:1, 0:1], ps_o1[64:65, 0:1]),
                                   waits=[pvz.marker, lbc_user])
                        bi3, bw3 = ps_pool.acquire()
                        ps_b3 = ps_pool.banks[bi3]
                        mb3 = p.op("tensor", lambda t, ps_b3=ps_b3: t.matmul(
                            ps_b3[0:64, 0:1], ones1x128[0:1, 0:64], recl[0:1, 0:1], start=True, stop=True),
                            waits=[rc1.marker] + bw3)
                        ps_pool.set_writer(bi3, mb3.marker)
                        mc3 = p.op("scalar", lambda s, ps_b3=ps_b3: s.copy(lbc[0:64, 0:1], ps_b3[0:64, 0:1]),
                                   waits=[mb3.marker, lbc_user])
                        ps_pool.add_reader(bi3, mc3.marker)
                        mo1 = p.op("vector", lambda v, h=h, ps_o1=ps_o1: v.tensor_mul(
                            o1_sb[64 * (h % 2):64 * (h % 2) + 64, h // 2:h // 2 + 1],
                            ps_o1[0:64, 0:1], lbc[0:64, 0:1]), waits=[mc3.marker])
                        lbc_user = mo1.marker
                        ps_pool.add_reader(oi2, mo1.marker)
                        o1done.append(mo1.marker)
                    vsnap = Marker("vector", p.counts["vector"])
                    lxcs = [p.dma(xm1_f[:, m:m + 1], x3d[m * 128:(m + 1) * 128, XH - 1:XH], xls,
                                  waits=[vsnap, x3_wr], allow_slow_non_contiguous=True) for m in range(8)]
                    lxc_all = Marker(xls, xls.count)
                    xm_done = []
                    for m in range(8):
                        zi, zw = ps_pool.acquire()
                        psz = ps_pool.banks[zi]
                        mmx = None
                        for kc in range(4):
                            mmx = p.op("tensor", lambda t, kc=kc, m=m, psz=psz: t.matmul(
                                psz[:, 0:1], wo_bf[:, kc * 1024 + m * 128:kc * 1024 + (m + 1) * 128],
                                o1_sb[:, kc:kc + 1], start=(kc == 0), stop=(kc == 3)),
                                waits=([o1done[-1], wo_ready] if kc == 0 else []) + (zw if kc == 0 else []))
                        ps_pool.set_writer(zi, mmx.marker)
                        mz1 = p.op("vector", lambda v, m=m, psz=psz: v.tensor_add(
                            ln_tmp[:, 0:1], xm1_f[:, m:m + 1], psz[:, 0:1]),
                            waits=[mmx.marker, lxc_all] + xm_done[-1:])
                        ps_pool.add_reader(zi, mz1.marker)
                        mz2 = p.op("vector", lambda v, m=m: v.tensor_mul(xm1_f[:, m:m + 1], ln_tmp[:, 0:1],
                                                                         indc[:, :]),
                                   waits=[mz1.marker, all_lds])
                        xm_done.append(mz2.marker)
                    for m in range(8):
                        stm1 = p.dma(xmd[m * 128:(m + 1) * 128, 0:1], xm1_f[:, m:m + 1], mds,
                                     waits=[xm_done[-1]], allow_slow_non_contiguous=True)
                    xm_wr.append(stm1.marker)

                x3rd_prev = None
                for (t0, w) in [(i * 512, 512) for i in range(SS // 512)]:
                    gt0 = sp0 + t0
                    lx3 = [p.dma(xstg[kc][:, 0:512], x3d[kc * 128:(kc + 1) * 128, XH + gt0:XH + gt0 + 512],
                                 xls, waits=[x3rd_prev, x3_wr]) for kc in range(8)]
                    x_ld = Marker(xls, xls.count)
                    tile_done = []
                    for m in range(8):
                        ai, aw = ps_pool.acquire()
                        ps = ps_pool.banks[ai]
                        mm = None
                        for kc in range(4):
                            mm = p.op("tensor", lambda t, kc=kc, m=m, t0=t0, ps=ps: t.matmul(
                                ps[:, :], wo_bf[:, kc * 1024 + m * 128:kc * 1024 + (m + 1) * 128],
                                o_bf[kc][:, t0:t0 + 512], start=(kc == 0), stop=(kc == 3)),
                                waits=([attn_done, wo_ready] if kc == 0 else []) + (aw if kc == 0 else []))
                        ps_pool.set_writer(ai, mm.marker)
                        mo = p.op("vector", lambda v, m=m, ps=ps: v.tensor_add(
                            sqb[1][:, 0:512], xstg[m][:, 0:512], ps[:, :]),
                            waits=[mm.marker, x_ld] + tile_done[-1:])
                        ps_pool.add_reader(ai, mo.marker)
                        stx = p.dma(xmd[m * 128:(m + 1) * 128, 1 + gt0:1 + gt0 + 512], sqb[1][:, 0:512], mds,
                                    waits=[mo.marker])
                        tile_done.append(stx.marker)
                    x3rd_prev = tile_done[-1]
                    xm_wr.append(tile_done[-1])
        xm_all = Marker(mds, mds.count)
        with ExitStack() as s3:
            e3 = s3.enter_context
            hbuf = e3(nc.sbuf_tensor(f"hbuf{tg}", [128, 22 * 512], BF16))
            hln_bf = [e3(nc.sbuf_tensor(f"hln{tg}{i}", [128, 512], BF16)) for i in range(8)]
            abf = e3(nc.sbuf_tensor(f"abf{tg}", [128, 512], BF16))
            gbf = e3(nc.sbuf_tensor(f"gbf{tg}", [128, 512], BF16))
            w2bf2 = [e3(nc.sbuf_tensor(f"w2bf{tg}{i}", [128, 22 * 128], BF16)) for i in range(2)]
            xm_stg = [e3(nc.sbuf_tensor(f"xmstg{tg}{i}", [128, 513], F32)) for i in range(8)]
            xnt = [e3(nc.sbuf_tensor(f"xnt{tg}{i}", [128, 512], F32)) for i in range(8)]
            ed_tmp = e3(nc.sbuf_tensor(f"ed_tmp{tg}", [128, 512], F32))
            trp = e3(nc.sbuf_tensor(f"trp{tg}", [128, 512], F32)) if not is_D else None

            wslot_user5 = [None, None]
            w2slot_user = [None, None]
            hb_users = []
            xn_done = []
            fin_st = []
            xm_rd_prev = None
            x3o_prev = None
            for (t0, w) in [(i * 512, 512) for i in range(S // 512)]:
                lxm = [p.dma(xm_stg[kc][:, 0:513], xmd[kc * 128:(kc + 1) * 128, t0:t0 + 513], xls,
                             waits=[xm_rd_prev, xm_all]) for kc in range(8)]
                xm_ld = Marker(xls, xls.count)

                def srcof(kc):
                    off = 1 if kc < 4 else 0
                    return xm_stg[kc][:, off:off + 512]

                def dstw2(kc, tw, ps_b1, ps_b2, deps):
                    tt0, ww = tw
                    a1 = p.op("vector", lambda v, kc=kc, ww=ww: v.tensor_mul(
                        ln_tmp[:, 0:ww], srcof(kc)[:, 0:ww], ps_b1[0:128, 0:ww]), waits=deps)
                    a2 = p.op("vector", lambda v, kc=kc, ww=ww: v.tensor_add(
                        ln_tmp[:, 0:ww], ln_tmp[:, 0:ww], ps_b2[0:128, 0:ww]), waits=[a1.marker])
                    a3 = p.op("vector", lambda v, kc=kc, ww=ww: v.tensor_scalar(
                        ln_tmp[:, 0:ww], ln_tmp[:, 0:ww], lngc[:, kc:kc + 1], None, ALU.mult),
                        waits=[a2.marker, all_lds])
                    a4 = p.op("vector", lambda v, kc=kc, ww=ww: v.tensor_scalar(
                        hln_bf[kc][:, 0:ww], ln_tmp[:, 0:ww], lnbc[:, kc:kc + 1], None, ALU.add),
                        waits=[a3.marker])
                    return a4

                lnm = emit_ln(p, nc, ps_pool, scr, srcof, dstw2, 8, 512, 512, eps5,
                              ones_col, ones1x128, extra_waits=[xm_ld], tile_list=[(0, 512)],
                              const_dep=all_lds)
                hln_ready = lnm[-1].marker
                xm_rd_prev = hln_ready

                for j in range(22):
                    wba = wbf2[0]
                    lw = p.dma(wba[:, :].rearrange("p (kc m) -> p kc m", kc=8),
                               w1_i[:, 256 * j:256 * j + 128].rearrange("(kc p) m -> p kc m", p=128), wl1,
                               waits=[wslot_user5[0]])
                    ai, aw = ps_pool.acquire()
                    ps_aj = ps_pool.banks[ai]
                    mma = None
                    for kc in range(8):
                        mma = p.op("tensor", lambda t, kc=kc, ps_aj=ps_aj, wba=wba: t.matmul(
                            ps_aj[:, :], wba[:, kc * 128:(kc + 1) * 128], hln_bf[kc][:, 0:512],
                            start=(kc == 0), stop=(kc == 7)),
                            waits=([lw.marker, hln_ready] if kc == 0 else []) + (aw if kc == 0 else []))
                    ps_pool.set_writer(ai, mma.marker)
                    wslot_user5[0] = mma.marker
                    ea = p.op("scalar", lambda s, j=j, ps_aj=ps_aj: s.activation(
                        out=abf[:], in_=ps_aj[:, :], func=AF.Identity, bias=b1ac[:, j:j + 1], scale=1.0),
                        waits=[mma.marker] + hb_users[-1:])
                    ps_pool.add_reader(ai, ea.marker)
                    wbg = wbf2[1]
                    lwg = p.dma(wbg[:, :].rearrange("p (kc m) -> p kc m", kc=8),
                                w1_i[:, 256 * j + 128:256 * j + 256].rearrange("(kc p) m -> p kc m", p=128), wl1,
                                waits=[wslot_user5[1]])
                    gi, gw = ps_pool.acquire()
                    ps_gj = ps_pool.banks[gi]
                    mmg = None
                    for kc in range(8):
                        mmg = p.op("tensor", lambda t, kc=kc, ps_gj=ps_gj, wbg=wbg: t.matmul(
                            ps_gj[:, :], wbg[:, kc * 128:(kc + 1) * 128], hln_bf[kc][:, 0:512],
                            start=(kc == 0), stop=(kc == 7)),
                            waits=([lwg.marker] if kc == 0 else []) + (gw if kc == 0 else []))
                    ps_pool.set_writer(gi, mmg.marker)
                    wslot_user5[1] = mmg.marker
                    eg = p.op("scalar", lambda s, j=j, ps_gj=ps_gj: s.activation(
                        out=gbf[:], in_=ps_gj[:, :], func=AF.Gelu, bias=b1gc[:, j:j + 1], scale=1.0),
                        waits=[mmg.marker] + hb_users[-1:])
                    ps_pool.add_reader(gi, eg.marker)
                    mh = p.op("vector", lambda v, j=j: v.tensor_mul(hbuf[:, j * 512:(j + 1) * 512], abf[:], gbf[:]),
                              waits=[ea.marker, eg.marker])
                    hb_users.append(mh.marker)
                h_ready = hb_users[-1]

                if not is_D:
                    lx3o = [p.dma(xstg[kc][:, 0:512], x3own[kc * 128:(kc + 1) * 128, XH + t0:XH + t0 + 512], xls,
                                  waits=[x3o_prev]) for kc in range(8)]
                    x3o_ld = Marker(xls, xls.count)
                for m in range(8):
                    w2b = w2bf2[m % 2]
                    lw2 = p.dma(w2b[:, :].rearrange("p (kc u) -> p kc u", kc=22),
                                w2_i[:, m * 128:(m + 1) * 128].rearrange("(kc p) u -> p kc u", p=128), wl3,
                                waits=[w2slot_user[m % 2]])
                    di, dw = ps_pool.acquire()
                    ps_d = ps_pool.banks[di]
                    mmd = None
                    for kc in range(22):
                        mmd = p.op("tensor", lambda t, kc=kc, ps_d=ps_d, w2b=w2b: t.matmul(
                            ps_d[:, :], w2b[:, kc * 128:(kc + 1) * 128], hbuf[:, kc * 512:(kc + 1) * 512],
                            start=(kc == 0), stop=(kc == 21)),
                            waits=([lw2.marker, h_ready] if kc == 0 else []) + (dw if kc == 0 else []))
                    ps_pool.set_writer(di, mmd.marker)
                    w2slot_user[m % 2] = mmd.marker
                    ed = p.op("scalar", lambda s, m=m, ps_d=ps_d: s.activation(
                        out=ed_tmp[:, 0:512], in_=ps_d[:, :], func=AF.Identity, bias=b2c[:, m:m + 1], scale=1.0),
                        waits=[mmd.marker] + xn_done[-1:])
                    ps_pool.add_reader(di, ed.marker)
                    mo = p.op("vector", lambda v, m=m: v.tensor_add(
                        xnt[m][:, 0:512], xm_stg[m][:, 1:513], ed_tmp[:, 0:512]),
                        waits=[ed.marker])
                    xn_done.append(mo.marker)
                    if is_D:
                        stn = p.dma(o_xn[m * 128:(m + 1) * 128, t0:t0 + 512], xnt[m][:, 0:512], sts,
                                    waits=[mo.marker])
                        fin_st.append(stn)
                        xn_done.append(Marker("vector", p.counts["vector"]))
                    else:
                        mo2 = p.op("vector", lambda v, m=m: v.tensor_add(
                            xnt[m][:, 0:512], xnt[m][:, 0:512], xstg[m][:, 0:512]),
                            waits=[mo.marker, x3o_ld])
                        xn_done.append(mo2.marker)
                if not is_D:
                    x3o_prev = xn_done[-1]

                    def dstw3(kc, tw, ps_b1, ps_b2, deps):
                        tt0, ww = tw
                        a1 = p.op("vector", lambda v, kc=kc, ww=ww: v.tensor_mul(
                            ln_tmp[:, 0:ww], xnt[kc][:, 0:ww], ps_b1[0:128, 0:ww]), waits=deps)
                        a2 = p.op("vector", lambda v, kc=kc, ww=ww: v.tensor_add(
                            xnt[kc][:, 0:ww], ln_tmp[:, 0:ww], ps_b2[0:128, 0:ww]), waits=[a1.marker])
                        return a2

                    lnf = emit_ln(p, nc, ps_pool, scr, lambda kc: xnt[kc][:, :], dstw3, 8, 512, 512, eps5,
                                  ones_col, ones1x128, extra_waits=[xn_done[-1]], tile_list=[(0, 512)],
                                  const_dep=all_lds)
                    lnf_ready = lnf[-1].marker
                    for m in range(8):
                        ti2, tw2 = ps_pool.acquire()
                        ps_t = ps_pool.banks[ti2]
                        trs = []
                        for s4 in range(4):
                            mt = p.op("tensor", lambda t, m=m, s4=s4, ps_t=ps_t: t.transpose(
                                ps_t[:, s4 * 128:(s4 + 1) * 128], xnt[m][:, s4 * 128:(s4 + 1) * 128],
                                identf[:]), waits=(([lnf_ready] + tw2) if s4 == 0 else []))
                            trs.append(mt)
                        ps_pool.set_writer(ti2, trs[-1].marker)
                        ct = p.op("scalar", lambda s, ps_t=ps_t: s.copy(trp[:, 0:512], ps_t[:, :]),
                                  waits=[trs[-1].marker] + ([fin_st[-1].marker] if fin_st else []))
                        ps_pool.add_reader(ti2, ct.marker)
                        stn = p.dma(o_out[t0:t0 + 512, m * 128:(m + 1) * 128].rearrange("(s q) u -> q s u", q=128),
                                    trp[:, 0:512].rearrange("p (s u) -> p s u", s=4), sts, waits=[ct.marker])
                        fin_st.append(stn)
                        xn_done.append(ct.marker)



G4 = [[0, 1, 2, 3], [4, 5, 6, 7]]


def build_fused(T, upto=5, debug=()):
    S = T // 4
    TE = S + XH
    nc = bass.Bass("TRN2", target_bir_lowering=False)

    def inp(name, shape, dt=F32):
        return nc.dram_tensor(name, shape, dt, kind="ExternalInput").ap()

    io = {}
    io["a_x_t"] = inp("a_x_t", [D, T])
    io["a_wq"] = inp("a_wq", [D, 256])
    io["a_wkv"] = inp("a_wkv", [D, 128])
    io["a_wk"] = inp("a_wk", [128, 128])
    io["a_wv"] = inp("a_wv", [128, 128])
    io["a_qg"] = inp("a_qg", [128, 2])
    io["a_kg"] = inp("a_kg", [128, 1])
    io["a_qg32"] = inp("a_qg32", [1, 32])
    io["a_kgr"] = inp("a_kgr", [1, 32])
    io["b_x_t"] = inp("b_x_t", [D, S])
    io["b_wo"] = inp("b_wo", [512, D])
    io["b_wg"] = inp("b_wg", [D, 3072], I32)
    io["b_wu"] = inp("b_wu", [D, 3072], I32)
    io["b_wd"] = inp("b_wd", [3072, D], I32)
    io["b_wq1"] = inp("b_wq1", [D, D])
    io["b_wk1"] = inp("b_wk1", [D, D])
    io["b_wv1"] = inp("b_wv1", [D, D])
    io["b_bq1"] = inp("b_bq1", [128, 8])
    io["b_bk1"] = inp("b_bk1", [128, 8])
    io["b_bv1"] = inp("b_bv1", [1, D])
    io["d_proj"] = inp("d_proj", [D, D])
    for pf in ("d_", "e_"):
        io[pf + "wqkv"] = inp(pf + "wqkv", [D, 1536])
        io[pf + "qs"] = inp(pf + "qs", [128, 1])
        io[pf + "ks"] = inp(pf + "ks", [128, 1])
        io[pf + "qsrow"] = inp(pf + "qsrow", [1, 64])
        io[pf + "ksrow"] = inp(pf + "ksrow", [1, 64])
        io[pf + "wo"] = inp(pf + "wo", [512, D])
        io[pf + "lng"] = inp(pf + "lng", [128, 8])
        io[pf + "lnb"] = inp(pf + "lnb", [128, 8])
        io[pf + "w1"] = inp(pf + "w1", [D, 5632])
        io[pf + "b1a"] = inp(pf + "b1a", [128, 22])
        io[pf + "b1g"] = inp(pf + "b1g", [128, 22])
        io[pf + "w2"] = inp(pf + "w2", [2816, D])
        io[pf + "b2"] = inp(pf + "b2", [128, 8])
    io["m_masks"] = inp("m_masks", [16, 128, 512], BF16)
    io["m_ind"] = inp("m_ind", [128, 1])
    io["m_tabs"] = inp("m_tabs", [4, 128, S + QH])
    io["oh4"] = inp("oh4", [128, 4])
    io["ohp"] = inp("ohp", [128, 3])
    io["e_out"] = nc.dram_tensor("e_out", [S, D], F32, kind="ExternalOutput").ap()

    def internal(name, shape, dt):
        return nc.dram_tensor(name, shape, dt).ap()

    io["cc_a_in"] = internal("cc_a_in", [128, T], BF16)
    io["cc_a_out"] = internal("cc_a_out", [512, T], BF16)
    io["b_attn_c"] = internal("b_attn_c", [512, S], BF16)
    io["cc_q_in"] = internal("cc_q_in", [D, S], BF16)
    io["cc_q_out"] = internal("cc_q_out", [4 * D, S], BF16)
    io["c_q_c"] = internal("c_q_c", [256, T], BF16)
    io["cc_k_in"] = internal("cc_k_in", [D, S], BF16)
    io["cc_k_out"] = internal("cc_k_out", [4 * D, S], BF16)
    io["c_k_c"] = internal("c_k_c", [256, T], BF16)
    io["cc_v_in"] = internal("cc_v_in", [S, 1040], BF16)
    io["ccv_gath"] = internal("ccv_gath", [4 * S, 1040], BF16)
    io["c_v_c"] = internal("c_v_c", [T, 260], BF16)
    io["cc_x2h_in"] = internal("cc_x2h_in", [D, XH], F32)
    io["cc_x2h_out"] = internal("cc_x2h_out", [4 * D, XH], F32)
    io["xe_d"] = internal("xe_d", [D, TE], F32)
    io["cc_c_in"] = internal("cc_c_in", [256, T], BF16)
    io["cc_c_out"] = internal("cc_c_out", [1024, T], BF16)
    io["d_attne_c"] = internal("d_attne_c", [D, TE], BF16)
    io["x3e_d"] = internal("x3e_d", [D, TE], F32)
    io["cc_xAh_in"] = internal("cc_xAh_in", [D, XH], F32)
    io["cc_xAh_out"] = internal("cc_xAh_out", [4 * D, XH], F32)
    io["xa_d"] = internal("xa_d", [D, TE], F32)
    for pf in ("d_", "e_"):
        io[pf + "w1b"] = internal(pf + "w1b", [D, 5632], BF16)
        io[pf + "w2b"] = internal(pf + "w2b", [2816, D], BF16)
        io[pf + "wqkvb"] = internal(pf + "wqkvb", [D, 1536], BF16)
        io[pf + "wob"] = internal(pf + "wob", [512, D], BF16)
    io["d_projb"] = internal("d_projb", [D, D], BF16)
    io["precast_pairs"] = (
        [(io[pf + "w1"], io[pf + "w1b"], D, 5632) for pf in ("d_", "e_")]
        + [(io[pf + "w2"], io[pf + "w2b"], 2816, D) for pf in ("d_", "e_")]
        + [(io[pf + "wqkv"], io[pf + "wqkvb"], D, 1536) for pf in ("d_", "e_")]
        + [(io[pf + "wo"], io[pf + "wob"], 512, D) for pf in ("d_", "e_")]
        + [(io["d_proj"], io["d_projb"], D, D)])
    io["xmd_D"] = internal("xmd_D", [D, S + 1], F32)
    io["xmd_E"] = internal("xmd_E", [D, S + 1], F32)
    io["xe_d_own"] = io["xe_d"][:, XH:XH + S]
    io["xa_d_own"] = io["xa_d"][:, XH:XH + S]
    io["d_xe"] = io["xe_d"]
    io["e_xe"] = io["xa_d"]
    io["e_x3"] = io["x3e_d"]

    dbg_aps = {}
    for nm in debug:
        shp = {"cc_a_in": [128, T], "cc_a_out": [512, T], "b_attn_c": [512, S], "c_q_c": [256, T],
               "c_k_c": [256, T], "c_v_c": [T, 260], "xe_d": [D, TE],
               "cc_c_out": [1024, T], "d_attne_c": [D, TE], "x3e_d": [D, TE],
               "xa_d": [D, TE], "cc_q_out": [4 * D, S]}[nm]
        dtd = F32 if nm in ("xe_d", "x3e_d", "xa_d") else BF16
        dbg_aps[nm] = nc.dram_tensor("dbg_" + nm, shp, dtd, kind="ExternalOutput").ap()

    with ExitStack() as gctx:
        p = Prog(nc)
        sm = {n: p.new_sem(n) for n in
              ("lds", "sts", "xls", "xs0", "xs1", "wl1", "wl2", "wl3",
               "wlsd", "wlsq", "wlso", "tbl", "mds", "ccs")}
        ccs = sm["ccs"]
        sts = sm["sts"]

        oh4 = gctx.enter_context(nc.sbuf_tensor("oh4_sb", [128, 4], F32))
        ohp = gctx.enter_context(nc.sbuf_tensor("ohp_sb", [128, 3], F32))
        p.dma(oh4[:], io["oh4"], sm["lds"])
        p.dma(ohp[:], io["ohp"], sm["lds"])

        lds = sm["lds"]
        const_sbuf(nc, p, gctx, lds, "sel64", sel_matrix(64))
        const_sbuf(nc, p, gctx, lds, "selT64", sel_matrix(64).T.copy())
        const_sbuf(nc, p, gctx, lds, "sel32h", sel_matrix(32, 64))
        const_sbuf(nc, p, gctx, lds, "selT32h", sel_matrix(32, 64).T.copy())
        const_sbuf(nc, p, gctx, lds, "ones1x128", np.ones((1, 128), np.float32))
        const_sbuf(nc, p, gctx, lds, "eps6", np.full((128, 1), 1e-6, np.float32))
        const_sbuf(nc, p, gctx, lds, "ones_col", np.ones((128, 1), np.float32))
        const_sbuf(nc, p, gctx, lds, "eps5", np.full((128, 1), 1e-5, np.float32))
        const_sbuf(nc, p, gctx, lds, "ident2", np.eye(2, dtype=np.float32))
        const_sbuf(nc, p, gctx, lds, "row8", np.full((1, 128), 0.125, np.float32))
        perm = np.zeros((128, 128), np.float32)
        for pp in range(128):
            perm[(pp // 64) * 64 + (pp + 32) % 64, pp] = 1.0
        const_sbuf(nc, p, gctx, lds, "permM", perm)
        m1np = np.zeros((128, 5), np.float32)
        m1np[127, 0] = 1.0
        m1np[:, 1:] = 1.0
        const_sbuf(nc, p, gctx, lds, "mask1", m1np.astype(BF16NP))
        const_sbuf(nc, p, gctx, lds, "identf", np.eye(128, dtype=np.float32))

        emit_A(p, nc, gctx, sm, io, T)
        p.barrier()
        if upto >= 1:
            p.cc("AllGather", ALU.bypass, G4, io["cc_a_in"].opt(), io["cc_a_out"].opt(), ccs)
            p.barrier()
            with ExitStack() as rs:
                emit_select(p, nc, rs, sm,
                            lambda rc, t0, w: io["b_attn_c"][128 * rc:128 * (rc + 1), t0:t0 + w],
                            [(lambda rc, t0, w, j=j: io["cc_a_out"][128 * rc:128 * (rc + 1), S * j + t0:S * j + t0 + w])
                             for j in range(4)],
                            oh4, range(4), [(i * 512, 512) for i in range(S // 512)], BF16, "rsA")
                p.barrier()
        if upto >= 2:
            emit_B(p, nc, gctx, sm, io, T)
            p.barrier()
            p.dma(io["cc_x2h_in"][:, :], io["xe_d"][:, XH + S - XH:XH + S], sm["mds"])
            p.wait("gpsimd", [Marker(sm["mds"], sm["mds"].count)])
            p.cc("AllGather", ALU.bypass, G4, io["cc_q_in"].opt(), io["cc_q_out"].opt(), ccs)
            p.cc("AllGather", ALU.bypass, G4, io["cc_k_in"].opt(), io["cc_k_out"].opt(), ccs)
            p.cc("AllGather", ALU.bypass, G4, io["cc_v_in"].opt(), io["ccv_gath"].opt(), ccs)
            p.cc("AllGather", ALU.bypass, G4, io["cc_x2h_in"].opt(), io["cc_x2h_out"].opt(), ccs)
            p.barrier()
            with ExitStack() as rs:
                for (nm, src, dst) in (("rsQ", io["cc_q_out"], io["c_q_c"]),
                                       ("rsK", io["cc_k_out"], io["c_k_c"])):
                    emit_select(p, nc, rs, sm,
                                lambda rc, t0, w, dst=dst: dst[128 * (rc % 2):128 * (rc % 2 + 1),
                                                               S * (rc // 2) + t0:S * (rc // 2) + t0 + w],
                                [(lambda rc, t0, w, j=j, src=src: src[D * (rc // 2) + 256 * j + 128 * (rc % 2):
                                                                      D * (rc // 2) + 256 * j + 128 * (rc % 2 + 1),
                                                                      t0:t0 + w])
                                 for j in range(4)],
                                oh4, range(8), [(i * 512, 512) for i in range(S // 512)], BF16, nm)
                emit_select(p, nc, rs, sm,
                            lambda rc, t0, w: io["c_v_c"][128 * rc:128 * (rc + 1), t0:t0 + w],
                            [(lambda rc, t0, w, j=j: io["ccv_gath"][S * (rc // 16) + 128 * (rc % 16):
                                                                    S * (rc // 16) + 128 * (rc % 16) + 128,
                                                                    260 * j + t0:260 * j + t0 + w])
                             for j in range(4)],
                            oh4, range(64), [(0, 260)], BF16, "rsV")
                emit_select(p, nc, rs, sm,
                            lambda rc, t0, w: io["xe_d"][128 * rc:128 * (rc + 1), t0:t0 + w],
                            [(lambda rc, t0, w, j=j: io["cc_x2h_out"][D * j + 128 * rc:D * j + 128 * (rc + 1),
                                                                     t0:t0 + w])
                             for j in range(3)],
                            ohp, range(8), [(0, 512), (512, 129)], F32, "rsH2")
                p.barrier()
        if upto >= 3:
            emit_C(p, nc, gctx, sm, io, T)
            p.barrier()
            p.cc("AllGather", ALU.bypass, G4, io["cc_c_in"].opt(), io["cc_c_out"].opt(), ccs)
            p.barrier()
            with ExitStack() as rs:
                emit_select(p, nc, rs, sm,
                            lambda rc, t0, w: io["d_attne_c"][128 * rc:128 * (rc + 1), XH + t0:XH + t0 + w],
                            [(lambda rc, t0, w, j=j: io["cc_c_out"][128 * rc:128 * (rc + 1), S * j + t0:S * j + t0 + w])
                             for j in range(4)],
                            oh4, range(8), [(i * 512, 512) for i in range(S // 512)], BF16, "rsC1")
                emit_select(p, nc, rs, sm,
                            lambda rc, t0, w: io["d_attne_c"][128 * rc:128 * (rc + 1), t0:t0 + w],
                            [(lambda rc, t0, w, j=j: io["cc_c_out"][128 * rc:128 * (rc + 1),
                                                                    S * (j + 1) - XH + t0:S * (j + 1) - XH + t0 + w])
                             for j in range(3)],
                            ohp, range(8), [(0, 512), (512, 129)], BF16, "rsC2")
                p.barrier()
        if upto >= 4:
            emit_DE(p, nc, gctx, sm, io, T, True)
            p.barrier()
            p.dma(io["cc_xAh_in"][:, :], io["xa_d"][:, XH + S - XH:XH + S], sm["mds"])
            p.wait("gpsimd", [Marker(sm["mds"], sm["mds"].count)])
            p.cc("AllGather", ALU.bypass, G4, io["cc_xAh_in"].opt(), io["cc_xAh_out"].opt(), ccs)
            p.barrier()
            with ExitStack() as rs:
                emit_select(p, nc, rs, sm,
                            lambda rc, t0, w: io["xa_d"][128 * rc:128 * (rc + 1), t0:t0 + w],
                            [(lambda rc, t0, w, j=j: io["cc_xAh_out"][D * j + 128 * rc:D * j + 128 * (rc + 1),
                                                                     t0:t0 + w])
                             for j in range(3)],
                            ohp, range(8), [(0, 512), (512, 129)], F32, "rsHA")
                p.barrier()
        if upto >= 5:
            emit_DE(p, nc, gctx, sm, io, T, False)
            p.barrier()
        for nm, ap in dbg_aps.items():
            p.dma(ap, io[nm][:, :] if nm != "c_v_c" else io[nm][:, :], sts)
        p.wait("sync", [Marker(sts, sts.count)])
        p.emit(gctx)
    return nc



def _bf(x):
    return np.ascontiguousarray(x).astype(BF16NP)


def _f32(x):
    return np.ascontiguousarray(x, dtype=np.float32)


def _rope_tables(T, S, hc):
    pos = S * hc + np.arange(-QH, S, dtype=np.float64)
    half = np.arange(0, 64, 2, dtype=np.float64)
    inv_freq = 1.0 / 10000.0 ** (half / 64.0)
    f = pos[None, :] * inv_freq[:, None]
    f = np.concatenate([f, f], axis=0)
    base = (half + 0.4 * 64) / (1.4 * 64)
    power = (pos - T // 2) / 256.0
    scl = base[:, None] ** power[None, :]
    scl = np.concatenate([scl, scl], axis=0)
    cosf, sinf = np.cos(f), np.sin(f)
    sgn = np.ones((64, 1)); sgn[:32] = -1.0
    Aq = cosf * scl
    Bq = sgn * sinf * scl
    Ak = cosf / scl
    Bk = sgn * sinf / scl
    out = [np.tile(x, (2, 1)).astype(np.float32) for x in (Aq, Bq, Ak, Bk)]
    return np.stack(out, axis=0)


def _de_masks(hc):
    m = np.zeros((16, 128, 512), dtype=BF16NP)
    for c in range(8):
        for kk in range(128):
            l = 128 * c + kk
            lo, hi = max(0, l - 512), min(511, l)
            if lo <= hi:
                m[c, kk, lo:hi + 1] = 1.0
            if l >= 512 and lo <= hi:
                m[8 + c, kk, lo:hi + 1] = 1.0
    if hc != 0:
        m[8:] = m[:8]
    return m


def host_inputs(inputs, T):
    S = T // 4
    x = inputs["L0_attn_x"][:, :T]
    wo_eff = np.ascontiguousarray(inputs["L0_attn_Wo"].reshape(16, 64, D)[:, :32].reshape(512, D))
    wq1 = np.ascontiguousarray(np.transpose(inputs["L1_attn_Wq"], (2, 0, 1)).reshape(D, D))
    wk1 = np.ascontiguousarray(np.transpose(inputs["L1_attn_Wk"], (2, 0, 1)).reshape(D, D))
    wv1 = np.ascontiguousarray(np.transpose(inputs["L1_attn_Wv"], (2, 0, 1)).reshape(D, D))

    de_common = {}
    for l, pf in ((0, "d_"), (1, "e_")):
        w1 = inputs["L1_ffn_w1"][l]
        a_pad = np.zeros((D, 2816), np.float32); a_pad[:, :2730] = w1[:, :2730]
        g_pad = np.zeros((D, 2816), np.float32); g_pad[:, :2730] = w1[:, 2730:]
        w1p = np.stack([a_pad.reshape(D, 22, 128), g_pad.reshape(D, 22, 128)], axis=2).reshape(D, 5632)
        b1 = inputs["L1_ffn_b1"][l]
        b1a = np.zeros(2816, np.float32); b1a[:2730] = b1[:2730]
        b1g = np.zeros(2816, np.float32); b1g[:2730] = b1[2730:]
        w2p = np.concatenate([inputs["L1_ffn_w2"][l], np.zeros((86, D), np.float32)], axis=0)
        de_common[pf] = {
            pf + "wqkv": _f32(inputs["L1_ffn_wqkv"][l]),
            pf + "qs": _f32(np.tile(inputs["L1_ffn_q_scale"][l], 2)[:, None]),
            pf + "ks": _f32(np.tile(inputs["L1_ffn_k_scale"][l], 2)[:, None]),
            pf + "qsrow": _f32(inputs["L1_ffn_q_scale"][l][None, :]),
            pf + "ksrow": _f32(inputs["L1_ffn_k_scale"][l][None, :]),
            pf + "wo": _f32(inputs["L1_ffn_wo"][l]),
            pf + "lng": _f32(inputs["L1_ffn_ln_g"][l].reshape(8, 128).T),
            pf + "lnb": _f32(inputs["L1_ffn_ln_b"][l].reshape(8, 128).T),
            pf + "w1": _f32(w1p),
            pf + "b1a": _f32(b1a.reshape(22, 128).T),
            pf + "b1g": _f32(b1g.reshape(22, 128).T),
            pf + "w2": _f32(w2p),
            pf + "b2": _f32(inputs["L1_ffn_b2"][l].reshape(8, 128).T),
        }

    maps = []
    for c in range(NCORES):
        b, hc = c // 4, c % 4
        sl = slice(S * hc, S * (hc + 1))
        oh4 = np.zeros((128, 4), np.float32); oh4[:, hc] = 1.0
        ohp = np.zeros((128, 3), np.float32)
        if hc >= 1:
            ohp[:, hc - 1] = 1.0
        m = {
            "a_x_t": _f32(x[b].T),
            "a_wq": _f32(inputs["L0_attn_Wq"][:, 256 * hc:256 * (hc + 1)]),
            "a_wkv": _f32(inputs["L0_attn_Wkv"]),
            "a_wk": _f32(inputs["L0_attn_Wk"][:, 128 * hc:128 * (hc + 1)]),
            "a_wv": _f32(inputs["L0_attn_Wv"][:, 128 * hc:128 * (hc + 1)]),
            "a_qg": _f32(np.tile(inputs["L0_attn_q_g"], 4).reshape(2, 128).T),
            "a_kg": _f32(np.tile(inputs["L0_attn_k_g"], 4)[:, None]),
            "a_qg32": _f32(inputs["L0_attn_q_g"][:32][None, :]),
            "a_kgr": _f32(inputs["L0_attn_k_g"][None, :]),
            "b_x_t": _f32(x[b].T[:, sl]),
            "b_wo": wo_eff,
            "b_wg": np.ascontiguousarray(inputs["L0_ffn_w_gate"].T),
            "b_wu": np.ascontiguousarray(inputs["L0_ffn_w_up"].T),
            "b_wd": np.ascontiguousarray(inputs["L0_ffn_w_down"].T),
            "b_wq1": wq1, "b_wk1": wk1, "b_wv1": wv1,
            "b_bq1": _f32(inputs["L1_attn_bq"].reshape(8, 128).T),
            "b_bk1": _f32(inputs["L1_attn_bk"].reshape(8, 128).T),
            "b_bv1": _f32(inputs["L1_attn_bv"].reshape(1, D)),
            "d_proj": _f32(inputs["L1_attn_proj"]),
            "m_masks": _de_masks(hc),
            "m_ind": np.full((128, 1), 0.0 if hc == 0 else 1.0, np.float32),
            "m_tabs": _rope_tables(T, S, hc),
            "oh4": oh4,
            "ohp": ohp,
        }
        m.update(de_common["d_"])
        m.update(de_common["e_"])
        maps.append(m)
    return maps


def kernel(**inputs):
    T = inputs["L0_attn_x"].shape[1]
    S = T // 4
    cores = list(range(NCORES))
    nc = build_fused(T)
    res = run_bass_kernel_spmd(nc, host_inputs(inputs, T), core_ids=cores)
    out = np.empty((2, T, D), dtype=np.float32)
    for c in cores:
        b, hc = c // 4, c % 4
        out[b, S * hc:S * (hc + 1)] = np.asarray(res.results[c]["e_out"])
    return out


# version 07
# speedup vs baseline: 1.1236x; 1.1236x over previous
import numpy as np
import ml_dtypes
from contextlib import ExitStack

import concourse.bass as bass
import concourse.mybir as mybir
from concourse.bass_utils import run_bass_kernel_spmd

F32 = mybir.dt.float32
BF16 = mybir.dt.bfloat16
I32 = mybir.dt.int32
AF = mybir.ActivationFunctionType
ALU = mybir.AluOpType
AX = mybir.AxisListType
BF16NP = ml_dtypes.bfloat16

D = 1024
NCORES = 8
ATTN_PIPE = 3
DEFER_EPI = True
XH = 641
QH = 640


ENGINES = ("sync", "vector", "scalar", "tensor", "gpsimd")


class Marker:
    __slots__ = ("key", "count")

    def __init__(self, key, count):
        self.key = key
        self.count = count


class DmaSem:
    __slots__ = ("name", "count", "handle")

    def __init__(self, name):
        self.name = name
        self.count = 0
        self.handle = None


class Op:
    __slots__ = ("eng", "fn", "waits", "inc_amt", "marker", "dma_sem")

    def __init__(self, eng, fn, waits, inc_amt, dma_sem=None):
        self.eng = eng
        self.fn = fn
        self.waits = waits
        self.inc_amt = inc_amt
        self.marker = None
        self.dma_sem = dma_sem


class Prog:
    def __init__(self, nc):
        self.nc = nc
        self.ops = {e: [] for e in ENGINES}
        self.counts = {e: 0 for e in ENGINES}
        self.dma_sems = []

    def new_sem(self, name):
        s = DmaSem(name)
        self.dma_sems.append(s)
        return s

    def op(self, eng, fn, waits=()):
        waits = [w for w in waits if w is not None]
        o = Op(eng, fn, waits, 1)
        self.ops[eng].append(o)
        self.counts[eng] += 1
        o.marker = Marker(eng, self.counts[eng])
        return o

    def dma(self, out, in_, sem, eng="sync", waits=(), **kw):
        waits = [w for w in waits if w is not None]
        o = Op(eng, lambda g: g.dma_start(out=out, in_=in_, **kw), waits, 16, dma_sem=sem)
        self.ops[eng].append(o)
        sem.count += 16
        o.marker = Marker(sem, sem.count)
        return o

    def cc(self, kind, alu, groups, in_ap, out_ap, sem, waits=()):
        waits = [w for w in waits if w is not None]
        o = Op("gpsimd", lambda g: g.collective_compute(
            kind, alu, replica_groups=groups, ins=[in_ap], outs=[out_ap]),
            waits, 1, dma_sem=sem)
        self.ops["gpsimd"].append(o)
        sem.count += 1
        o.marker = Marker(sem, sem.count)
        return o

    def wait(self, eng, waits):
        waits = [w for w in waits if w is not None]
        o = Op(eng, None, waits, 0)
        self.ops[eng].append(o)
        return o

    def barrier(self):
        marks = [Marker(e, self.counts[e]) for e in ENGINES if self.counts[e] > 0]
        marks += [Marker(s, s.count) for s in self.dma_sems if s.count > 0]
        for e in ENGINES:
            self.wait(e, marks)

    def emit(self, stack):
        nc = self.nc
        eng_sems = {e: stack.enter_context(nc.semaphore(f"ms_{e}")) for e in ENGINES}
        for s in self.dma_sems:
            s.handle = stack.enter_context(nc.semaphore(s.name))

        def sem_of(key):
            return eng_sems[key] if isinstance(key, str) else key.handle

        max_waited = {}
        for e in ENGINES:
            for o in self.ops[e]:
                for w in o.waits:
                    k = id(w.key) if not isinstance(w.key, str) else w.key
                    if w.count > max_waited.get(k, 0):
                        max_waited[k] = w.count
        blk = stack.enter_context(nc.Block())
        for e in ENGINES:
            ops = self.ops[e]
            if not ops:
                continue

            def body(eng_handle, ops=ops, e=e):
                cum = 0
                last_waited = {}
                for o in ops:
                    byk = {}
                    for w in o.waits:
                        k = id(w.key) if not isinstance(w.key, str) else w.key
                        if w.count > byk.get(k, (0, None))[0]:
                            byk[k] = (w.count, w.key)
                    for k, (cnt, key) in byk.items():
                        if cnt > last_waited.get(k, 0):
                            eng_handle.wait_ge(sem_of(key), cnt)
                            last_waited[k] = cnt
                    if o.fn is None:
                        continue
                    ins = o.fn(eng_handle)
                    if o.dma_sem is not None:
                        ins.then_inc(o.dma_sem.handle, o.inc_amt)
                    else:
                        cum += 1
                        if cum <= max_waited.get(e, 0):
                            assert ins is not None, f"op on {e} returned no instruction"
                            ins.then_inc(eng_sems[e], 1)

            getattr(blk, e)(body)
        return eng_sems


class PsumPool:

    def __init__(self, nc, stack, n=6, name="ps"):
        self.banks = [stack.enter_context(nc.psum_tensor(f"{name}{i}", [128, 512], F32)) for i in range(n)]
        self.writer = [None] * n
        self.readers = [[] for _ in range(n)]
        self.rr = 0
        self.n = n

    def acquire(self):
        i = self.rr
        self.rr = (self.rr + 1) % self.n
        waits = list(self.readers[i])
        if self.writer[i] is not None:
            waits.append(self.writer[i])
        self.readers[i] = []
        self.writer[i] = None
        return i, waits

    def set_writer(self, i, marker):
        self.writer[i] = marker

    def add_reader(self, i, marker):
        self.readers[i].append(marker)


def consts(nc):
    if not hasattr(nc, "_mk_consts"):
        nc._mk_consts = {}
    return nc._mk_consts


def const_sbuf(nc, p, stack, lds, name, arr):
    c = consts(nc)
    if name in c:
        return c[name]
    dt = {np.dtype(np.float32): F32, np.dtype(BF16NP): BF16, np.dtype(np.int32): I32}[arr.dtype]
    dram = nc.inline_tensor(np.ascontiguousarray(arr), name + "_c").ap()
    sb = stack.enter_context(nc.sbuf_tensor(name + "_sb", list(arr.shape), dt))
    p.dma(sb[:], dram, lds)
    c[name] = sb
    return sb


def sel_matrix(G, P=128):
    n = P // G
    m = np.zeros((P, n), dtype=np.float32)
    for g in range(n):
        m[g * G:(g + 1) * G, g] = 1.0
    return m


def diag_masks_np(qbs=512):
    m = np.zeros((4, 128, qbs), dtype=BF16NP)
    for j in range(4):
        for kk in range(128):
            m[j, kk, 128 * j + kk:] = 1.0
    return m



def emit_group_norm_apply(p, nc, ps_pool, src_f32, dst_bf, G, gain_col, scratch, mode,
                          extra_waits=(), eps_col=None, clip=None, tile_w=512):
    W = src_f32.shape[-1]
    P = src_f32.shape[0]
    assert W <= tile_w
    ngr = P // G
    sq = scratch["sq"]
    st1 = scratch["st1"]
    sel = scratch["sel"]
    selT = scratch["selT"]
    m_sq = p.op("scalar", lambda s: s.square(sq[0:P, 0:W], src_f32), waits=list(extra_waits))
    bi, bw = ps_pool.acquire()
    ps = ps_pool.banks[bi]
    m_st = p.op("tensor", lambda t: t.matmul(ps[0:ngr, 0:W], sel[:], sq[0:P, 0:W], start=True, stop=True),
                waits=[m_sq.marker] + bw)
    ps_pool.set_writer(bi, m_st.marker)
    if mode == "rms":
        m_sc = p.op("scalar", lambda s: s.activation(out=st1[0:ngr, 0:W], in_=ps[0:ngr, 0:W], func=AF.Sqrt,
                                                     bias=eps_col[0:ngr, :], scale=1.0 / G), waits=[m_st.marker])
    else:
        m_sc = p.op("scalar", lambda s: s.activation(out=st1[0:ngr, 0:W], in_=ps[0:ngr, 0:W], func=AF.Sqrt,
                                                     bias=0.0, scale=1.0), waits=[m_st.marker])
    ps_pool.add_reader(bi, m_sc.marker)
    if mode == "l2":
        m_cl = p.op("vector", lambda v: v.tensor_scalar_max(st1[0:ngr, 0:W], st1[0:ngr, 0:W], float(clip)),
                    waits=[m_sc.marker])
    else:
        m_cl = m_sc
    m_rc = p.op("vector", lambda v: v.reciprocal(st1[0:ngr, 0:W], st1[0:ngr, 0:W]), waits=[m_cl.marker])
    bi2, bw2 = ps_pool.acquire()
    ps2 = ps_pool.banks[bi2]
    m_bc = p.op("tensor", lambda t: t.matmul(ps2[0:P, 0:W], selT[:], st1[0:ngr, 0:W], start=True, stop=True),
                waits=[m_rc.marker] + bw2)
    ps_pool.set_writer(bi2, m_bc.marker)
    if gain_col is not None:
        m_ap = p.op("vector", lambda v: v.scalar_tensor_tensor(dst_bf, src_f32, gain_col, ps2[0:P, 0:W],
                                                               ALU.mult, ALU.mult), waits=[m_bc.marker])
    else:
        m_ap = p.op("vector", lambda v: v.tensor_mul(dst_bf, src_f32, ps2[0:P, 0:W]), waits=[m_bc.marker])
    ps_pool.add_reader(bi2, m_ap.marker)
    return m_ap


def emit_attention(p, nc, ps_pool, cfg):
    T = cfg["T"]
    QBS = 512
    NQB = T // QBS
    scale = cfg["scale"]
    mask_sb = cfg["mask_sb"]
    pt = cfg["pt"]
    NPT = len(pt)
    PIPE = min(ATTN_PIPE, NPT - 1)
    recl = cfg["recl"]
    ones1 = cfg["ones1x128"]
    lbc = cfg["lbc"]
    hd = cfg["hd"]

    ot_banks = cfg["ot_banks"]
    ot_state = cfg.setdefault("ot_state", [{"w": None, "r": []}, {"w": None, "r": []}])
    qb_seq = 0

    def flush_epi():
        epi = cfg.pop("_pend_epi", None)
        if epi is not None:
            epi()

    for head in cfg["heads"]:
        k_lhsT = head["k_lhsT"]
        q_rhs = head["q_rhs"]
        v_lhsT = head["v_lhsT"]
        out_write = head["out_write"]
        bias_col = head["bias_col"]
        bias_dep = head.get("bias_dep")
        qkv_dep = head.get("qkv_dep")
        prev_exp = [None] * NPT
        for qb in range(NQB):
            nkc = (qb + 1) * QBS // 128
            osl = qb_seq % 2
            qb_seq += 1
            ps_ot = ot_banks[osl]
            ow = list(ot_state[osl]["r"])
            if ot_state[osl]["w"] is not None:
                ow.append(ot_state[osl]["w"])
            ot_state[osl] = {"w": None, "r": []}
            pend = []
            pv_last = None

            def emit_pv(c, dep, buf, nkc=nkc, ps_ot=ps_ot, v_lhsT=v_lhsT):
                return p.op("tensor", lambda t, c=c, buf=buf, nkc=nkc, ps_ot=ps_ot, v_lhsT=v_lhsT: t.matmul(
                    ps_ot[0:hd + 1, 0:QBS], v_lhsT(c), buf[:], start=(c == 0), stop=(c == nkc - 1)),
                    waits=[dep])

            for c in range(nkc):
                ai, aw = ps_pool.acquire()
                ps_a = ps_pool.banks[ai]
                mm = p.op("tensor", lambda t, c=c, qb=qb, ps_a=ps_a, k_lhsT=k_lhsT, q_rhs=q_rhs: t.matmul(
                    ps_a[:, :], k_lhsT(c), q_rhs(qb), start=True, stop=True),
                    waits=aw + ([qkv_dep] if qkv_dep else []) + ([bias_dep] if bias_dep else []) + ow)
                ow = []
                ps_pool.set_writer(ai, mm.marker)
                buf = pt[c % NPT]
                ex = p.op("scalar", lambda s, ps_a=ps_a, buf=buf, bias_col=bias_col: s.activation(
                    out=buf[:], in_=ps_a[:, :], func=AF.Exp, bias=bias_col[:], scale=scale),
                    waits=[mm.marker] + ([prev_exp[c % NPT].marker] if prev_exp[c % NPT] else []))
                ps_pool.add_reader(ai, ex.marker)
                dep = ex
                if c >= 4 * qb:
                    j = c - 4 * qb
                    dep = p.op("vector", lambda v, j=j, buf=buf: v.tensor_mul(
                        buf[:], buf[:], mask_sb[:, j * QBS:(j + 1) * QBS]), waits=[ex.marker])
                if c == 0:
                    flush_epi()
                pend.append((dep.marker, buf, c))
                if len(pend) > PIPE:
                    d0, b0, c0 = pend.pop(0)
                    pv = emit_pv(c0, d0, b0)
                    prev_exp[c0 % NPT] = pv
                    pv_last = pv
            while pend:
                d0, b0, c0 = pend.pop(0)
                pv = emit_pv(c0, d0, b0)
                prev_exp[c0 % NPT] = pv
                pv_last = pv
            ot_state[osl]["w"] = pv_last.marker
            m_rc = p.op("vector", lambda v, ps_ot=ps_ot: v.reciprocal(recl[:], ps_ot[hd:hd + 1, 0:QBS]),
                        waits=[pv_last.marker] + ([cfg["_recl_user"]] if cfg.get("_recl_user") else []))
            ot_state[osl]["r"].append(m_rc.marker)

            def mk_epi(qb=qb, osl=osl, ps_ot=ps_ot, m_rc=m_rc, out_write=out_write):
                def epi():
                    bi2, bw2 = ps_pool.acquire()
                    ps_b = ps_pool.banks[bi2]
                    lbc_row = cfg.get("lbc_row", ones1)
                    m_bc = p.op("tensor", lambda t, ps_b=ps_b, lbc_row=lbc_row: t.matmul(
                        ps_b[0:hd, :], lbc_row[0:1, 0:hd], recl[:], start=True, stop=True),
                        waits=[m_rc.marker] + bw2)
                    cfg["_recl_user"] = m_bc.marker
                    ps_pool.set_writer(bi2, m_bc.marker)
                    m_cp = p.op("scalar", lambda s, ps_b=ps_b: s.copy(lbc[0:hd, :], ps_b[0:hd, :]),
                                waits=[m_bc.marker] + ([cfg["_lbc_user"]] if cfg.get("_lbc_user") else []))
                    ps_pool.add_reader(bi2, m_cp.marker)
                    m_out = out_write(qb, ps_ot, [m_cp.marker])
                    cfg["_lbc_user"] = m_out.marker
                    ot_state[osl]["r"].append(m_out.marker)
                return epi

            if DEFER_EPI:
                cfg["_pend_epi"] = mk_epi()
            else:
                mk_epi()()
    flush_epi()



def emit_A(p, nc, gctx, sm, io, T):
    NT512 = T // 512
    x_t = io["a_x_t"]
    wq = io["a_wq"]
    wkv = io["a_wkv"]
    wk = io["a_wk"]
    wv = io["a_wv"]
    qg = io["a_qg"]
    kg = io["a_kg"]
    qg32 = io["a_qg32"]
    kgr = io["a_kgr"]
    a_out = io["cc_a_in"]
    lds = sm["lds"]
    sts = sm["sts"]
    xs = [sm["xs0"], sm["xs1"]]

    with ExitStack() as ctx:
        e = ctx.enter_context
        x_stg = [e(nc.sbuf_tensor(f"x_stg{i}", [128, 8 * 512], F32)) for i in range(2)]
        x_bf = [e(nc.sbuf_tensor(f"x_bf{i}", [128, 8 * 512], BF16)) for i in range(2)]
        wq_bf = e(nc.sbuf_tensor("wq_bf", [128, 16 * 128], BF16))
        wkv_bf = e(nc.sbuf_tensor("wkv_bf", [128, 8 * 128], BF16))
        wk_bf = e(nc.sbuf_tensor("wk_bf", [128, 128], BF16))
        wv_bf = e(nc.sbuf_tensor("wv_bf", [128, 128], BF16))
        wstg = e(nc.sbuf_tensor("wstg", [128, 8 * 128], F32))
        qr = [e(nc.sbuf_tensor(f"qr{i}", [64, T], BF16)) for i in range(2)]
        kt = [e(nc.sbuf_tensor(f"kt{i}", [64, T], BF16)) for i in range(2)]
        kv_bf = e(nc.sbuf_tensor("kv_bf", [128, T], BF16))
        v_bf = e(nc.sbuf_tensor("v_bf", [128, (T // 128) * 132], BF16))
        o_bf = [e(nc.sbuf_tensor(f"o_bf{i}", [64, T], BF16)) for i in range(2)]
        qtmp = e(nc.sbuf_tensor("qtmp", [128, 512], F32))
        gcol = e(nc.sbuf_tensor("gcol", [128, 2], F32))
        kcol = e(nc.sbuf_tensor("kcol", [128, 1], F32))
        grow = e(nc.sbuf_tensor("grow", [1, 64], F32))
        red = e(nc.sbuf_tensor("red", [1, 8], F32))
        bias_col = e(nc.sbuf_tensor("biascol", [128, 1], F32))
        pt = [e(nc.sbuf_tensor(f"pt{i}", [128, 512], BF16)) for i in range(3)]
        recl = e(nc.sbuf_tensor("recl", [1, 512], F32))
        lbc = e(nc.sbuf_tensor("lbc", [64, 512], F32))
        sq = e(nc.sbuf_tensor("sq", [128, 512], F32))
        st1 = e(nc.sbuf_tensor("st1", [4, 512], F32))
        ps_pool = PsumPool(nc, ctx, name="psA")
        ot_banks = [e(nc.psum_tensor(f"ot{i}", [128, 512], F32)) for i in range(2)]

        sel64 = const_sbuf(nc, p, gctx, lds, "sel64", sel_matrix(64))
        selT64 = const_sbuf(nc, p, gctx, lds, "selT64", sel_matrix(64).T.copy())
        sel32h = const_sbuf(nc, p, gctx, lds, "sel32h", sel_matrix(32, 64))
        selT32h = const_sbuf(nc, p, gctx, lds, "selT32h", sel_matrix(32, 64).T.copy())
        ones1x128 = const_sbuf(nc, p, gctx, lds, "ones1x128", np.ones((1, 128), np.float32))
        eps6 = const_sbuf(nc, p, gctx, lds, "eps6", np.full((128, 1), 1e-6, np.float32))
        dm = diag_masks_np()
        mask_sb = e(nc.sbuf_tensor("mask_sb", [128, 4 * 512], BF16))
        dm_dram = nc.inline_tensor(dm, "dmasks_c").ap()
        for j in range(4):
            p.dma(mask_sb[:, j * 512:(j + 1) * 512], dm_dram[j], lds)

        p.dma(wstg[:, 0:8 * 128].rearrange("p (kc m) -> p kc m", kc=8),
              wq[:, 0:128].rearrange("(kc p) m -> p kc m", p=128), lds)
        p.dma(gcol[:], qg, lds)
        p.dma(kcol[:], kg, lds)
        p.dma(grow[0:1, 0:32], qg32, lds)
        p.dma(grow[0:1, 32:64], kgr, lds)
        all_lds = Marker(lds, lds.count)
        m_wq0 = p.op("vector", lambda v: v.tensor_copy(wq_bf[:, 0:1024], wstg[:, 0:8 * 128]), waits=[all_lds])
        ld1b = p.dma(wstg[:, 0:8 * 128].rearrange("p (kc m) -> p kc m", kc=8),
                     wq[:, 128:256].rearrange("(kc p) m -> p kc m", p=128), lds, waits=[m_wq0.marker])
        m_wq = p.op("vector", lambda v: v.tensor_copy(wq_bf[:, 1024:2048], wstg[:, 0:8 * 128]),
                    waits=[ld1b.marker])
        ld2 = p.dma(wstg[:, 0:8 * 128].rearrange("p (kc m) -> p kc m", kc=8), wkv.rearrange("(kc p) m -> p kc m", p=128), lds, waits=[m_wq.marker])
        m_wkv = p.op("vector", lambda v: v.tensor_copy(wkv_bf[:], wstg[:, 0:8 * 128]), waits=[ld2.marker])
        ld3 = p.dma(wstg[:, 0:128], wk, lds, waits=[m_wkv.marker])
        ld4 = p.dma(wstg[:, 128:256], wv, lds, waits=[m_wkv.marker])
        m_wk = p.op("vector", lambda v: v.tensor_copy(wk_bf[:], wstg[:, 0:128]), waits=[ld3.marker, ld4.marker])
        m_wv = p.op("vector", lambda v: v.tensor_copy(wv_bf[:], wstg[:, 128:256]), waits=[m_wk.marker])
        w_ready = m_wv.marker

        r1 = p.op("vector", lambda v: v.reduce_max(red[0:1, 0:1], grow[0:1, 0:32], axis=AX.X,
                                                   apply_absolute_value=True), waits=[all_lds])
        r2 = p.op("vector", lambda v: v.reduce_max(red[0:1, 1:2], grow[0:1, 32:64], axis=AX.X,
                                                   apply_absolute_value=True), waits=[r1.marker])
        r3 = p.op("vector", lambda v: v.tensor_mul(red[0:1, 2:3], red[0:1, 0:1], red[0:1, 1:2]), waits=[r2.marker])
        r4 = p.op("vector", lambda v: v.tensor_scalar_mul(red[0:1, 3:4], red[0:1, 2:3], -float(np.sqrt(32.0))),
                  waits=[r3.marker])
        bi, bw = ps_pool.acquire()
        psb = ps_pool.banks[bi]
        r5 = p.op("tensor", lambda t: t.matmul(psb[0:128, 0:1], ones1x128[:], red[0:1, 3:4], start=True, stop=True),
                  waits=[r4.marker] + bw)
        ps_pool.set_writer(bi, r5.marker)
        r6 = p.op("scalar", lambda s: s.copy(bias_col[:], psb[0:128, 0:1]), waits=[r5.marker])
        ps_pool.add_reader(bi, r6.marker)
        bias_ready = r6.marker

        scr32 = {"sq": sq, "st1": st1, "sel": sel32h, "selT": selT32h}
        slot_free = [None, None]
        cast_slot = [None, None]
        kv_done = []
        qtmp_users = []
        for tt in range(NT512):
            sl = tt % 2
            ld = p.dma(x_stg[sl][:, :].rearrange("p (kc c) -> p kc c", kc=8),
                       x_t[:, tt * 512:(tt + 1) * 512].rearrange("(kc p) c -> p kc c", p=128),
                       xs[sl], eng="gpsimd", waits=[slot_free[sl], cast_slot[sl]])
            m_cast = p.op("vector", lambda v, sl=sl: v.tensor_copy(x_bf[sl][:], x_stg[sl][:]), waits=[ld.marker])
            cast_slot[sl] = m_cast.marker
            for m in range(2):
                ai, aw = ps_pool.acquire()
                ps = ps_pool.banks[ai]
                mm = None
                for kc in range(8):
                    mm = p.op("tensor", lambda t, kc=kc, m=m, sl=sl, ps=ps: t.matmul(
                        ps[:, :], wq_bf[:, m * 1024 + kc * 128:m * 1024 + (kc + 1) * 128],
                        x_bf[sl][:, kc * 512:(kc + 1) * 512],
                        start=(kc == 0), stop=(kc == 7)),
                        waits=([m_cast.marker, w_ready] if kc == 0 else []) + (aw if kc == 0 else []))
                ps_pool.set_writer(ai, mm.marker)
                cp = p.op("scalar", lambda s, ps=ps: s.copy(qtmp[:], ps[:, :]),
                          waits=[mm.marker] + qtmp_users)
                qtmp_users = []
                ps_pool.add_reader(ai, cp.marker)
                m_sq = p.op("scalar", lambda s: s.square(sq[:, 0:512], qtmp[:, 0:512]), waits=[cp.marker])
                bi, bw = ps_pool.acquire()
                psS = ps_pool.banks[bi]
                m_st = p.op("tensor", lambda t, psS=psS: t.matmul(psS[0:2, 0:512], sel64[:], sq[:, 0:512],
                                                                 start=True, stop=True), waits=[m_sq.marker] + bw)
                ps_pool.set_writer(bi, m_st.marker)
                m_sc = p.op("scalar", lambda s, psS=psS: s.activation(out=st1[0:2, 0:512], in_=psS[0:2, 0:512],
                                                                      func=AF.Sqrt, bias=eps6[0:2, :], scale=1.0 / 64),
                            waits=[m_st.marker])
                ps_pool.add_reader(bi, m_sc.marker)
                m_rc = p.op("vector", lambda v: v.reciprocal(st1[0:2, 0:512], st1[0:2, 0:512]), waits=[m_sc.marker])
                bi2, bw2 = ps_pool.acquire()
                psB = ps_pool.banks[bi2]
                m_bc = p.op("tensor", lambda t, psB=psB: t.matmul(psB[0:128, 0:512], selT64[:], st1[0:2, 0:512],
                                                                 start=True, stop=True), waits=[m_rc.marker] + bw2)
                ps_pool.set_writer(bi2, m_bc.marker)
                a1 = p.op("vector", lambda v, m=m, tt=tt, psB=psB: v.scalar_tensor_tensor(
                    qr[m][0:32, tt * 512:(tt + 1) * 512], qtmp[0:32, 0:512], gcol[0:32, m:m + 1],
                    psB[0:32, 0:512], ALU.mult, ALU.mult), waits=[m_bc.marker])
                a2 = p.op("vector", lambda v, m=m, tt=tt, psB=psB: v.scalar_tensor_tensor(
                    qr[m][32:64, tt * 512:(tt + 1) * 512], qtmp[64:96, 0:512], gcol[64:96, m:m + 1],
                    psB[64:96, 0:512], ALU.mult, ALU.mult), waits=[m_bc.marker])
                ps_pool.add_reader(bi2, a1.marker)
                ps_pool.add_reader(bi2, a2.marker)
                qtmp_users = [a1.marker, a2.marker]
            ai, aw = ps_pool.acquire()
            ps = ps_pool.banks[ai]
            mm = None
            for kc in range(8):
                mm = p.op("tensor", lambda t, kc=kc, sl=sl, ps=ps: t.matmul(
                    ps[:, :], wkv_bf[:, kc * 128:(kc + 1) * 128], x_bf[sl][:, kc * 512:(kc + 1) * 512],
                    start=(kc == 0), stop=(kc == 7)), waits=(aw if kc == 0 else []))
            ps_pool.set_writer(ai, mm.marker)
            cpkv = p.op("scalar", lambda s, tt=tt, ps=ps: s.copy(kv_bf[:, tt * 512:(tt + 1) * 512], ps[:, :]),
                        waits=[mm.marker])
            ps_pool.add_reader(ai, cpkv.marker)
            kv_done.append(cpkv)
            slot_free[sl] = mm.marker

        k_done = []
        v_done = []
        for tt in range(NT512):
            for half in range(2):
                ai, aw = ps_pool.acquire()
                ps = ps_pool.banks[ai]
                mm = p.op("tensor", lambda t, tt=tt, half=half, ps=ps: t.matmul(
                    ps[0:64, :], wk_bf[:, half * 64:(half + 1) * 64], kv_bf[:, tt * 512:(tt + 1) * 512],
                    start=True, stop=True), waits=[kv_done[tt].marker] + aw)
                ps_pool.set_writer(ai, mm.marker)
                cp = p.op("scalar", lambda s, ps=ps: s.copy(qtmp[0:64, 0:512], ps[0:64, :]),
                          waits=[mm.marker] + qtmp_users)
                qtmp_users = []
                ps_pool.add_reader(ai, cp.marker)
                m_ap = emit_group_norm_apply(p, nc, ps_pool, qtmp[0:64, 0:512],
                                             kt[half][:, tt * 512:(tt + 1) * 512],
                                             32, kcol[0:64, :], scr32, "rms", extra_waits=[cp.marker],
                                             eps_col=eps6)
                qtmp_users = [m_ap.marker]
                k_done.append(m_ap)
            c2 = k_done[-1]
            for st4 in range(4):
                tok = tt * 4 + st4
                vi, vw = ps_pool.acquire()
                psv = ps_pool.banks[vi]
                mv = p.op("tensor", lambda t, tok=tok, psv=psv: t.matmul(
                    psv[:, 0:128], kv_bf[:, tok * 128:(tok + 1) * 128], wv_bf[:],
                    start=True, stop=True), waits=vw + ([c2.marker] if st4 == 0 else []))
                ps_pool.set_writer(vi, mv.marker)
                dst = v_bf[:, tok * 132:(tok + 1) * 132].rearrange("p (h w) -> p h w", h=4)[:, :, 0:32]
                src = psv[:, 0:128].rearrange("p (h w) -> p h w", h=4)
                cpv = p.op("scalar", lambda s, dst=dst, src=src: s.copy(dst, src), waits=[mv.marker])
                ps_pool.add_reader(vi, cpv.marker)
                v_done.append(cpv)
        m_ones = p.op("vector", lambda v: v.memset(
            v_bf[:, :].rearrange("p (t h w) -> p t h w", h=4, w=33)[:, :, :, 32:33], 1.0),
            waits=[v_done[-1].marker])
        last_v = Marker("vector", p.counts["vector"])

        def mk_head(h):
            kth = kt[h // 2]
            koff = 32 * (h % 2)
            qth = qr[h // 2]
            qoff = 32 * (h % 2)

            def out_write(qb, ps_ot, deps, h=h):
                ob = o_bf[h // 2]
                ooff = 32 * (h % 2)
                return p.op("vector", lambda v, qb=qb, ps_ot=ps_ot, ob=ob, ooff=ooff: v.tensor_mul(
                    ob[ooff:ooff + 32, qb * 512:(qb + 1) * 512], ps_ot[0:32, 0:512], lbc[0:32, :]), waits=deps)

            return {
                "k_lhsT": lambda c, kth=kth, koff=koff: kth[koff:koff + 32, c * 128:(c + 1) * 128],
                "q_rhs": lambda qb, qth=qth, qoff=qoff: qth[qoff:qoff + 32, qb * 512:(qb + 1) * 512],
                "v_lhsT": lambda c, h=h: v_bf[:, c * 132 + 33 * h:c * 132 + 33 * (h + 1)],
                "out_write": out_write,
                "bias_col": bias_col,
                "bias_dep": bias_ready,
                "qkv_dep": Marker("vector", last_v.count),
            }

        cfg = {
            "T": T, "hd": 32, "scale": 0.125, "mask_sb": mask_sb,
            "pt": pt, "recl": recl, "lbc": lbc, "ones1x128": ones1x128,
            "ot_banks": ot_banks,
            "heads": [mk_head(h) for h in range(4)],
        }
        cfg["heads"][0]["qkv_dep"] = Marker("vector", last_v.count)
        emit_attention(p, nc, ps_pool, cfg)

        m_all_v = Marker("vector", p.counts["vector"])
        p.dma(a_out[0:64, :], o_bf[0][:], sts, waits=[m_all_v])
        p.dma(a_out[64:128, :], o_bf[1][:], sts, waits=[m_all_v])



def emit_select(p, nc, ctx, sm, dst_ap_fn, cand_ap_fns, oh, row_chunks, col_tiles, dtype, name):
    e = ctx.enter_context
    ncand = len(cand_ap_fns)
    dt = dtype
    cands = [e(nc.sbuf_tensor(f"{name}_c{j}", [128, 512], dt)) for j in range(ncand)]
    accs = [e(nc.sbuf_tensor(f"{name}_a{j}", [128, 512], dt)) for j in range(2)]
    xls = sm["xls"]
    sts = sm["sts"]
    prev_st = None
    prev_cmb = None
    for rc in row_chunks:
        for (t0, w) in col_tiles:
            lds_m = []
            for j in range(ncand):
                lm = p.dma(cands[j][:, 0:w], cand_ap_fns[j](rc, t0, w), xls,
                           waits=[prev_cmb])
                lds_m.append(lm.marker)
            m = p.op("vector", lambda v, w=w: v.tensor_scalar(
                accs[0][:, 0:w], cands[0][:, 0:w], oh[:, 0:1], None, ALU.mult),
                waits=lds_m + [prev_st])
            cur = 0
            for j in range(1, ncand):
                nxt = 1 - cur
                m = p.op("vector", lambda v, j=j, cur=cur, nxt=nxt, w=w: v.scalar_tensor_tensor(
                    accs[nxt][:, 0:w], cands[j][:, 0:w], oh[:, j:j + 1], accs[cur][:, 0:w],
                    ALU.mult, ALU.add), waits=[m.marker])
                cur = nxt
            prev_cmb = m.marker
            st = p.dma(dst_ap_fn(rc, t0, w), accs[cur][:, 0:w], sts, waits=[m.marker])
            prev_st = st.marker



def emit_B(p, nc, gctx, sm, io, T):
    S = T // 4
    SP = min(S, 1024)
    NTS = SP // 512
    x_t = io["b_x_t"]
    attn = io["b_attn_c"]
    wo = io["b_wo"]
    wg = io["b_wg"]
    wu = io["b_wu"]
    wd = io["b_wd"]
    wq1 = io["b_wq1"]
    wk1 = io["b_wk1"]
    wv1 = io["b_wv1"]
    bq1 = io["b_bq1"]
    bk1 = io["b_bk1"]
    bv1 = io["b_bv1"]
    o_q = io["cc_q_in"]
    o_k = io["cc_k_in"]
    o_v = io["cc_v_in"]
    o_x2 = io["xe_d_own"]
    lds = sm["lds"]
    wls = [sm["wl1"], sm["wl2"]]
    wlsd = sm["wlsd"]
    wlsq = sm["wlsq"]
    wlso = sm["wlso"]
    xls = sm["xls"]
    sts = sm["sts"]

    with ExitStack() as ctx:
        e = ctx.enter_context
        X = [e(nc.sbuf_tensor(f"X{i}", [128, SP], F32)) for i in range(8)]
        Y = [e(nc.sbuf_tensor(f"Y{i}", [128, SP], F32)) for i in range(8)]
        xbf = [e(nc.sbuf_tensor(f"xbf{i}", [128, SP], BF16)) for i in range(8)]
        h_bf = e(nc.sbuf_tensor("h_bf", [128, 24 * SP], BF16))
        attn_sb = e(nc.sbuf_tensor("attn_sb", [128, 4 * SP], BF16))
        wo_bf = e(nc.sbuf_tensor("wo_bf", [128, 4 * 1024], BF16))
        wst_i = [e(nc.sbuf_tensor(f"wsti{i}", [128, 1024], I32)) for i in range(2)]
        wbf = [e(nc.sbuf_tensor(f"wbf{i}", [128, 1024], BF16)) for i in range(4)]
        wdst_i = e(nc.sbuf_tensor("wdsti", [128, 1536], I32))
        wd_bf = e(nc.sbuf_tensor("wd_bf", [128, 3072], BF16))
        wq_st = e(nc.sbuf_tensor("wq_st", [128, 1024], F32))
        wq_bf2 = e(nc.sbuf_tensor("wq_bf2", [128, 1024], BF16))
        bcol = e(nc.sbuf_tensor("bcol", [128, 8 * 3], F32))
        bvrow = e(nc.sbuf_tensor("bvrow", [1, D], F32))
        gtmp = e(nc.sbuf_tensor("gtmp", [128, 512], BF16))
        utmp = e(nc.sbuf_tensor("utmp", [128, 512], BF16))
        sqb = [e(nc.sbuf_tensor(f"sqb{i}", [128, 512], F32)) for i in range(2)]
        stm = e(nc.sbuf_tensor("stm", [1, 512], F32))
        stv = e(nc.sbuf_tensor("stv", [1, 512], F32))
        strd = e(nc.sbuf_tensor("strd", [1, 512], F32))
        stmrs = e(nc.sbuf_tensor("stmrs", [1, 512], F32))
        stv2 = e(nc.sbuf_tensor("stv2", [1, 512], F32))
        vstage = e(nc.sbuf_tensor("vstage", [128, 1040], BF16))
        ps_pool = PsumPool(nc, ctx, name="psB")

        ones_col = const_sbuf(nc, p, gctx, lds, "ones_col", np.ones((128, 1), np.float32))
        ones1x128 = const_sbuf(nc, p, gctx, lds, "ones1x128", np.ones((1, 128), np.float32))
        eps5 = const_sbuf(nc, p, gctx, lds, "eps5", np.full((128, 1), 1e-5, np.float32))
        scr = {"sqb": sqb, "stm": stm, "stv": stv, "strd": strd, "stmrs": stmrs, "stv2": stv2}

        p.dma(bcol[:, 0:8], bq1, lds)
        p.dma(bcol[:, 8:16], bk1, lds)
        p.dma(bvrow[:], bv1, lds)
        wo_prev = None
        for kc in range(4):
            lw = p.dma(wq_st[:, :], wo[kc * 128:(kc + 1) * 128, :], wlso, waits=[wo_prev])
            cw = p.op("vector", lambda v, kc=kc: v.tensor_copy(wo_bf[:, kc * 1024:(kc + 1) * 1024], wq_st[:]),
                      waits=[lw.marker])
            wo_prev = cw.marker
        all_lds = Marker(lds, lds.count)
        wo_ready = wo_prev

        prev_sp_users = []
        for sp0 in range(0, S, SP):
            ssl = slice(sp0, sp0 + SP)
            xl = []
            for kc in range(8):
                xl.append(p.dma(X[kc][:, :], x_t[kc * 128:(kc + 1) * 128, ssl], xls, waits=prev_sp_users))
            la = p.dma(attn_sb[:, :].rearrange("p (kc m) -> p kc m", kc=4),
                       attn[:, ssl].rearrange("(kc p) m -> p kc m", p=128), xls, waits=prev_sp_users)
            x_ld = Marker(xls, xls.count)
            prev_sp_users = []

            v1_done = []
            for m in range(8):
                for tt in range(NTS):
                    ai, aw = ps_pool.acquire()
                    ps = ps_pool.banks[ai]
                    mm = None
                    for kc in range(4):
                        mm = p.op("tensor", lambda t, kc=kc, m=m, tt=tt, ps=ps: t.matmul(
                            ps[:, :], wo_bf[:, kc * 1024 + m * 128:kc * 1024 + (m + 1) * 128],
                            attn_sb[:, kc * SP + tt * 512:kc * SP + (tt + 1) * 512],
                            start=(kc == 0), stop=(kc == 3)),
                            waits=([wo_ready, x_ld] if kc == 0 else []) + (aw if kc == 0 else []))
                    ps_pool.set_writer(ai, mm.marker)
                    mo = p.op("vector", lambda v, m=m, tt=tt, ps=ps: v.tensor_add(
                        Y[m][:, tt * 512:(tt + 1) * 512], X[m][:, tt * 512:(tt + 1) * 512], ps[:, :]),
                        waits=[mm.marker])
                    ps_pool.add_reader(ai, mo.marker)
                    v1_done.append(mo.marker)

            def dst_x(kc, tw, ps_b1, ps_b2, deps):
                t0, w = tw
                m1 = p.op("vector", lambda v: v.tensor_mul(X[kc][:, t0:t0 + w], Y[kc][:, t0:t0 + w],
                                                           ps_b1[0:128, 0:w]), waits=deps)
                m2 = p.op("vector", lambda v: v.tensor_add(X[kc][:, t0:t0 + w], X[kc][:, t0:t0 + w],
                                                           ps_b2[0:128, 0:w]), waits=[m1.marker])
                m3 = p.op("vector", lambda v: v.tensor_copy(xbf[kc][:, t0:t0 + w], X[kc][:, t0:t0 + w]),
                          waits=[m2.marker])
                return m3

            ln1 = emit_ln(p, nc, ps_pool, scr, lambda kc: Y[kc][:, :], dst_x, 8, 512, SP, eps5,
                          ones_col, ones1x128, extra_waits=[Marker("vector", p.counts["vector"])],
                          const_dep=all_lds)
            x1_ready = ln1[-1].marker

            wload_prev = [None, None]
            wbf_user = [None] * 4
            h_done = []
            for oc in range(24):
                sl = oc % 2
                lg = p.dma(wst_i[sl][:, :].rearrange("p (kc m) -> p kc m", kc=8),
                           wg[:, oc * 128:(oc + 1) * 128].rearrange("(kc p) m -> p kc m", p=128), wls[sl],
                           eng="gpsimd", waits=[wload_prev[sl]])
                cg = p.op("scalar", lambda s, sl=sl: s.mul(wbf[sl][:], wst_i[sl][:], 1.0 / 127.0),
                          waits=[lg.marker, wbf_user[sl]])
                lu = p.dma(wst_i[sl][:, :].rearrange("p (kc m) -> p kc m", kc=8),
                           wu[:, oc * 128:(oc + 1) * 128].rearrange("(kc p) m -> p kc m", p=128), wls[sl],
                           eng="gpsimd", waits=[cg.marker])
                cu = p.op("scalar", lambda s, sl=sl: s.mul(wbf[2 + sl][:], wst_i[sl][:], 1.0 / 127.0),
                          waits=[lu.marker, wbf_user[2 + sl]])
                wload_prev[sl] = cu.marker
                for tt in range(NTS):
                    gi, gw = ps_pool.acquire()
                    ps_g = ps_pool.banks[gi]
                    mmg = None
                    for kc in range(8):
                        mmg = p.op("tensor", lambda t, kc=kc, sl=sl, tt=tt, ps_g=ps_g: t.matmul(
                            ps_g[:, :], wbf[sl][:, kc * 128:(kc + 1) * 128], xbf[kc][:, tt * 512:(tt + 1) * 512],
                            start=(kc == 0), stop=(kc == 7)),
                            waits=([cg.marker, x1_ready] if kc == 0 else []) + (gw if kc == 0 else []))
                    ps_pool.set_writer(gi, mmg.marker)
                    ui, uw = ps_pool.acquire()
                    ps_u = ps_pool.banks[ui]
                    mmu = None
                    for kc in range(8):
                        mmu = p.op("tensor", lambda t, kc=kc, sl=sl, tt=tt, ps_u=ps_u: t.matmul(
                            ps_u[:, :], wbf[2 + sl][:, kc * 128:(kc + 1) * 128], xbf[kc][:, tt * 512:(tt + 1) * 512],
                            start=(kc == 0), stop=(kc == 7)),
                            waits=([cu.marker] if kc == 0 else []) + (uw if kc == 0 else []))
                    ps_pool.set_writer(ui, mmu.marker)
                    ms = p.op("scalar", lambda s, ps_g=ps_g: s.activation(out=gtmp[:], in_=ps_g[:, :], func=AF.Silu),
                              waits=[mmg.marker] + h_done[-1:])
                    ps_pool.add_reader(gi, ms.marker)
                    mu2 = p.op("scalar", lambda s, ps_u=ps_u: s.copy(utmp[:], ps_u[:, :]),
                               waits=[mmu.marker] + h_done[-1:])
                    ps_pool.add_reader(ui, mu2.marker)
                    mh = p.op("vector", lambda v, oc=oc, tt=tt: v.tensor_mul(
                        h_bf[:, oc * SP + tt * 512:oc * SP + (tt + 1) * 512], gtmp[:], utmp[:]),
                        waits=[ms.marker, mu2.marker])
                    h_done.append(mh.marker)

            wd_prev = None
            wd_read_user = None
            for m in range(8):
                cd = None
                for hh in range(2):
                    ldd = p.dma(wdst_i[:, :].rearrange("p (kc m) -> p kc m", kc=12),
                                wd[hh * 1536:(hh + 1) * 1536, m * 128:(m + 1) * 128]
                                .rearrange("(kc p) m -> p kc m", p=128), wlsd,
                                eng="gpsimd", waits=[wd_prev])
                    cd = p.op("scalar", lambda s, hh=hh: s.mul(wd_bf[:, hh * 1536:(hh + 1) * 1536],
                                                               wdst_i[:], 1.0 / 127.0),
                              waits=[ldd.marker, wd_read_user])
                    wd_prev = cd.marker
                for tt in range(NTS):
                    di, dw = ps_pool.acquire()
                    ps_d = ps_pool.banks[di]
                    mmd = None
                    for kc in range(24):
                        mmd = p.op("tensor", lambda t, kc=kc, tt=tt, ps_d=ps_d: t.matmul(
                            ps_d[:, :], wd_bf[:, kc * 128:(kc + 1) * 128],
                            h_bf[:, kc * SP + tt * 512:kc * SP + (tt + 1) * 512],
                            start=(kc == 0), stop=(kc == 23)),
                            waits=([cd.marker, h_done[-1]] if kc == 0 else []) + (dw if kc == 0 else []))
                    ps_pool.set_writer(di, mmd.marker)
                    mo = p.op("vector", lambda v, m=m, tt=tt, ps_d=ps_d: v.tensor_add(
                        Y[m][:, tt * 512:(tt + 1) * 512], X[m][:, tt * 512:(tt + 1) * 512], ps_d[:, :]),
                        waits=[mmd.marker])
                    ps_pool.add_reader(di, mo.marker)
                wd_read_user = mmd.marker

            ln2 = emit_ln(p, nc, ps_pool, scr, lambda kc: Y[kc][:, :], dst_x, 8, 512, SP, eps5,
                          ones_col, ones1x128, extra_waits=[Marker("vector", p.counts["vector"])],
                          const_dep=all_lds)
            x2_ready = ln2[-1].marker
            st_x2 = [p.dma(o_x2[kc * 128:(kc + 1) * 128, ssl], X[kc][:, :], sts, waits=[x2_ready])
                     for kc in range(8)]
            prev_sp_users.extend(s.marker for s in st_x2)

            wq_prev = None
            wqst_user = None
            st_prev = None
            for wi, (wap, bofs, oap) in enumerate(((wq1, 0, o_q), (wk1, 8, o_k))):
                for m in range(8):
                    ldw = p.dma(wq_st[:, :].rearrange("p (kc m) -> p kc m", kc=8),
                                wap[:, m * 128:(m + 1) * 128].rearrange("(kc p) m -> p kc m", p=128), wlsq,
                                waits=[wqst_user])
                    cw = p.op("vector", lambda v: v.tensor_copy(wq_bf2[:], wq_st[:]),
                              waits=[ldw.marker, wq_prev])
                    wqst_user = cw.marker
                    mmq = None
                    for tt in range(NTS):
                        qi2, qw2 = ps_pool.acquire()
                        ps_q2 = ps_pool.banks[qi2]
                        for kc in range(8):
                            mmq = p.op("tensor", lambda t, kc=kc, tt=tt, ps_q2=ps_q2: t.matmul(
                                ps_q2[:, :], wq_bf2[:, kc * 128:(kc + 1) * 128], xbf[kc][:, tt * 512:(tt + 1) * 512],
                                start=(kc == 0), stop=(kc == 7)),
                                waits=([cw.marker, x2_ready] if kc == 0 else []) + (qw2 if kc == 0 else []))
                        ps_pool.set_writer(qi2, mmq.marker)
                        mq = p.op("scalar", lambda s, m=m, tt=tt, ps_q2=ps_q2, bofs=bofs: s.activation(
                            out=gtmp[:], in_=ps_q2[:, :], func=AF.Identity, bias=bcol[:, bofs + m:bofs + m + 1],
                            scale=1.0), waits=[mmq.marker] + ([st_prev] if st_prev else []))
                        ps_pool.add_reader(qi2, mq.marker)
                        st = p.dma(oap[m * 128:(m + 1) * 128, sp0 + tt * 512:sp0 + (tt + 1) * 512], gtmp[:], sts,
                                   waits=[mq.marker])
                        st_prev = st.marker
                    wq_prev = mmq.marker
                    prev_sp_users.append(st_prev)

            vst_prev = None
            wv_prev = None
            tensor_snap = Marker("tensor", p.counts["tensor"])
            for m in range(8):
                ldm = p.dma(wq_st[:, :].rearrange("p (kc m) -> p kc m", kc=8),
                            wv1[:, m * 128:(m + 1) * 128].rearrange("(kc p) m -> p kc m", p=128), wlsq,
                            waits=[wqst_user])
                cm = p.op("vector", lambda v, m=m: v.tensor_copy(
                    h_bf[:, 0:8192].rearrange("p (kc u) -> p kc u", kc=8)[:, :, m * 128:(m + 1) * 128],
                    wq_st[:, :].rearrange("p (kc u) -> p kc u", kc=8)),
                    waits=[ldm.marker, tensor_snap, wv_prev])
                wqst_user = cm.marker
                wv_prev = cm.marker
            for tk in range(SP // 128):
                pv1, w1_ = ps_pool.acquire()
                ps_v1 = ps_pool.banks[pv1]
                pv2, w2_ = ps_pool.acquire()
                ps_v2 = ps_pool.banks[pv2]
                mmv = {}
                for half, psv, hw, pvi in ((0, ps_v1, w1_, pv1), (1, ps_v2, w2_, pv2)):
                    mm = None
                    for kc in range(8):
                        mm = p.op("tensor", lambda t, kc=kc, tk=tk, half=half, psv=psv: t.matmul(
                            psv[:, :], xbf[kc][:, tk * 128:(tk + 1) * 128],
                            h_bf[:, kc * 1024 + half * 512:kc * 1024 + (half + 1) * 512],
                            start=(kc == 0), stop=False),
                            waits=([wv_prev] if kc == 0 else []) + (hw if kc == 0 else []))
                    mm = p.op("tensor", lambda t, half=half, psv=psv: t.matmul(
                        psv[:, :], ones1x128[:], bvrow[0:1, half * 512:(half + 1) * 512],
                        start=False, stop=True), waits=[all_lds])
                    mmv[half] = mm
                    ps_pool.set_writer(pvi, mm.marker)
                ev = []
                for half, psv, pvi in ((0, ps_v1, pv1), (1, ps_v2, pv2)):
                    dst = vstage[:, :].rearrange("p (h w) -> p h w", h=16)[:, half * 8:(half + 1) * 8, 0:64]
                    srcv = psv[:, :].rearrange("p (h w) -> p h w", h=8)
                    mvx = p.op("scalar", lambda s, dst=dst, srcv=srcv: s.copy(dst, srcv),
                               waits=[mmv[half].marker] + ([vst_prev] if vst_prev else []))
                    ev.append(mvx)
                    ps_pool.add_reader(pvi, mvx.marker)
                mones = p.op("vector", lambda v: v.memset(
                    vstage[:, :].rearrange("p (h w) -> p h w", h=16)[:, :, 64:65], 1.0),
                    waits=[m.marker for m in ev])
                stv3 = p.dma(o_v[sp0 + tk * 128:sp0 + (tk + 1) * 128, :], vstage[:], sts, waits=[mones.marker])
                vst_prev = stv3.marker
            prev_sp_users.append(vst_prev)
            prev_sp_users.append(Marker("tensor", p.counts["tensor"]))



def emit_ln(p, nc, ps_pool, scr, src_of, dst_write, n_chunks, W0, Tloc, eps_col,
            ones_col, ones1x128, ndim=1024, extra_waits=(), tile_list=None, const_dep=None):
    outm = []
    tiles = tile_list or [(i * W0, min(W0, Tloc - i * W0)) for i in range((Tloc + W0 - 1) // W0)]
    sqb = scr["sqb"]
    stm, stv, strd, stmrs = scr["stm"], scr["stv"], scr["strd"], scr["stmrs"]
    prev_stat_users = scr.setdefault("_prev_stat_users", [])
    assert "stv2" in scr
    for (t0, w) in tiles:
        si, swt = ps_pool.acquire()
        ps_s = ps_pool.banks[si]
        qi, qwt = ps_pool.acquire()
        ps_q = ps_pool.banks[qi]
        mm_s = mm_q = None
        sq_users = [None, None]
        for kc in range(n_chunks):
            srcap = src_of(kc)[:, t0:t0 + w]
            mm_s = p.op("tensor", lambda t, srcap=srcap, kc=kc, ps_s=ps_s, w=w: t.matmul(
                ps_s[0:1, 0:w], ones_col[:], srcap, start=(kc == 0), stop=(kc == n_chunks - 1)),
                waits=(swt if kc == 0 else []) + list(extra_waits) + [const_dep])
            sb = sqb[kc % 2]
            m_sq = p.op("scalar", lambda s, srcap=srcap, sb=sb, w=w: s.square(sb[:, 0:w], srcap),
                        waits=list(extra_waits) + ([sq_users[kc % 2]] if sq_users[kc % 2] else []))
            mm_q = p.op("tensor", lambda t, sb=sb, kc=kc, ps_q=ps_q, w=w: t.matmul(
                ps_q[0:1, 0:w], ones_col[:], sb[:, 0:w], start=(kc == 0), stop=(kc == n_chunks - 1)),
                waits=[m_sq.marker] + (qwt if kc == 0 else []))
            sq_users[kc % 2] = mm_q.marker
        ps_pool.set_writer(si, mm_s.marker)
        ps_pool.set_writer(qi, mm_q.marker)
        m1 = p.op("vector", lambda v, ps_s=ps_s, w=w: v.tensor_scalar_mul(stm[0:1, 0:w], ps_s[0:1, 0:w], 1.0 / ndim),
                  waits=[mm_s.marker, mm_q.marker] + prev_stat_users)
        del prev_stat_users[:]
        ps_pool.add_reader(si, m1.marker)
        m2 = p.op("vector", lambda v, ps_q=ps_q, w=w: v.tensor_scalar_mul(stv[0:1, 0:w], ps_q[0:1, 0:w], 1.0 / ndim),
                  waits=[m1.marker])
        ps_pool.add_reader(qi, m2.marker)
        stv2 = scr["stv2"]
        m3 = p.op("vector", lambda v, w=w: v.tensor_mul(stv2[0:1, 0:w], stm[0:1, 0:w], stm[0:1, 0:w]),
                  waits=[m2.marker])
        m4 = p.op("vector", lambda v, w=w: v.tensor_sub(stv2[0:1, 0:w], stv[0:1, 0:w], stv2[0:1, 0:w]),
                  waits=[m3.marker])
        m5 = p.op("scalar", lambda s, w=w: s.activation(out=stv[0:1, 0:w], in_=stv2[0:1, 0:w], func=AF.Sqrt,
                                                        bias=eps_col[0:1, :], scale=1.0), waits=[m4.marker])
        m6 = p.op("vector", lambda v, w=w: v.reciprocal(strd[0:1, 0:w], stv[0:1, 0:w]), waits=[m5.marker])
        m7 = p.op("vector", lambda v, w=w: v.scalar_tensor_tensor(stmrs[0:1, 0:w], stm[0:1, 0:w], -1.0,
                                                                  strd[0:1, 0:w], ALU.mult, ALU.mult), waits=[m6.marker])
        bi1, bw1 = ps_pool.acquire()
        ps_b1 = ps_pool.banks[bi1]
        mb1 = p.op("tensor", lambda t, ps_b1=ps_b1, w=w: t.matmul(ps_b1[0:128, 0:w], ones1x128[:], strd[0:1, 0:w],
                                                                  start=True, stop=True),
                   waits=[m6.marker, const_dep] + bw1)
        ps_pool.set_writer(bi1, mb1.marker)
        bi2, bw2 = ps_pool.acquire()
        ps_b2 = ps_pool.banks[bi2]
        mb2 = p.op("tensor", lambda t, ps_b2=ps_b2, w=w: t.matmul(ps_b2[0:128, 0:w], ones1x128[:], stmrs[0:1, 0:w],
                                                                  start=True, stop=True), waits=[m7.marker] + bw2)
        ps_pool.set_writer(bi2, mb2.marker)
        prev_stat_users.extend([mb1.marker, mb2.marker])
        prev_dst = scr.get("_prev_dst")
        for kc in range(n_chunks):
            mo = dst_write(kc, (t0, w), ps_b1, ps_b2,
                           [mb1.marker, mb2.marker] + ([prev_dst] if prev_dst else []))
            prev_dst = mo.marker
            ps_pool.add_reader(bi1, mo.marker)
            ps_pool.add_reader(bi2, mo.marker)
            outm.append(mo)
        scr["_prev_dst"] = prev_dst
    return outm


def emit_precast(p, nc, ctx, sm, pairs):
    e = ctx.enter_context
    CW = 1408
    NS = 2
    fstg = [e(nc.sbuf_tensor(f"pcf{i}", [128, CW], F32)) for i in range(NS)]
    bstg = [e(nc.sbuf_tensor(f"pcb{i}", [128, CW], BF16)) for i in range(NS)]
    xls = sm["xls"]
    sts = sm["sts"]
    slot_cv = [None] * NS
    slot_st = [None] * NS
    si = 0
    for (src, dst, R, C) in pairs:
        for r in range(0, R, 128):
            for c0 in range(0, C, CW):
                w = min(CW, C - c0)
                s = si % NS
                si += 1
                ld = p.dma(fstg[s][:, 0:w], src[r:r + 128, c0:c0 + w], xls,
                           waits=[slot_cv[s]])
                cv = p.op("vector", lambda v, s=s, w=w: v.tensor_copy(bstg[s][:, 0:w], fstg[s][:, 0:w]),
                          waits=[ld.marker, slot_st[s]])
                slot_cv[s] = cv.marker
                st = p.dma(dst[r:r + 128, c0:c0 + w], bstg[s][:, 0:w], sts, waits=[cv.marker])
                slot_st[s] = st.marker



def emit_C(p, nc, gctx, sm, io, T):
    q_in = io["c_q_c"]
    k_in = io["c_k_c"]
    v_in = io["c_v_c"]
    c_out = io["cc_c_in"]
    lds = sm["lds"]
    sts = sm["sts"]

    with ExitStack() as ctx:
        e = ctx.enter_context
        q_sb = [e(nc.sbuf_tensor(f"q_sbC{i}", [128, T], BF16)) for i in range(2)]
        k_sb = [e(nc.sbuf_tensor(f"k_sbC{i}", [128, T], BF16)) for i in range(2)]
        v_bf = e(nc.sbuf_tensor("v_bfC", [128, (T // 128) * 260], BF16))
        o_bf = [e(nc.sbuf_tensor(f"o_bfC{i}", [128, T], BF16)) for i in range(2)]
        sqb = e(nc.sbuf_tensor("sqbC", [128, 512], F32))
        accq = e(nc.sbuf_tensor("accq", [2, 4], F32))
        accr = e(nc.sbuf_tensor("accr", [2, 4], F32))
        nrow = e(nc.sbuf_tensor("nrow", [1, 4], F32))
        bias_cols = e(nc.sbuf_tensor("bias_cols", [128, 4], F32))
        pt = [e(nc.sbuf_tensor(f"ptC{i}", [128, 512], BF16)) for i in range(4)]
        recl = e(nc.sbuf_tensor("reclC", [1, 512], F32))
        lbc = e(nc.sbuf_tensor("lbcC", [64, 512], F32))
        mask_sb = e(nc.sbuf_tensor("mask_sbC", [128, 4 * 512], BF16))
        ps_pool = PsumPool(nc, ctx, name="psC")
        ot_banks = [e(nc.psum_tensor(f"otC{i}", [128, 512], F32)) for i in range(2)]

        ones1x128 = const_sbuf(nc, p, gctx, lds, "ones1x128", np.ones((1, 128), np.float32))
        sel64 = const_sbuf(nc, p, gctx, lds, "sel64", sel_matrix(64))
        ident2 = const_sbuf(nc, p, gctx, lds, "ident2", np.eye(2, dtype=np.float32))
        row8 = const_sbuf(nc, p, gctx, lds, "row8", np.full((1, 128), 0.125, np.float32))
        dm = diag_masks_np()
        dm_dram = nc.inline_tensor(dm, "dmasksC_c").ap()
        for j in range(4):
            p.dma(mask_sb[:, j * 512:(j + 1) * 512], dm_dram[j], lds)

        for ch in range(2):
            p.dma(q_sb[ch][:, :], q_in[ch * 128:(ch + 1) * 128, :], lds)
            p.dma(k_sb[ch][:, :], k_in[ch * 128:(ch + 1) * 128, :], lds)
        for tk in range(T // 128):
            p.dma(v_bf[:, tk * 260:(tk + 1) * 260], v_in[tk * 128:(tk + 1) * 128, :], lds)
        all_lds = Marker(lds, lds.count)
        if "precast_pairs" in io:
            emit_precast(p, nc, ctx, sm, io["precast_pairs"])

        minit_q = p.op("vector", lambda v: v.memset(accq[:, :], 0.0), waits=[])
        minit_r = p.op("vector", lambda v: v.memset(accr[:, :], 0.0), waits=[])
        bias_ready = []
        sq_user = None
        for src_sb, acc, mi in ((q_sb, accq, minit_q), (k_sb, accr, minit_r)):
            for ch in range(2):
                for tt in range(T // 512):
                    m_sq = p.op("scalar", lambda s, src_sb=src_sb, ch=ch, tt=tt: s.square(
                        sqb[:, 0:512], src_sb[ch][:, tt * 512:(tt + 1) * 512]),
                        waits=[all_lds, sq_user, mi.marker])
                    si, sw = ps_pool.acquire()
                    psS = ps_pool.banks[si]
                    m_st = p.op("tensor", lambda t, psS=psS: t.matmul(psS[0:2, 0:512], sel64[:], sqb[:, 0:512],
                                                                     start=True, stop=True),
                                waits=[m_sq.marker] + sw)
                    ps_pool.set_writer(si, m_st.marker)
                    sq_user = m_st.marker
                    m_rd = p.op("vector", lambda v, psS=psS, acc=acc, ch=ch: v.reduce_max(
                        (accq if acc is accq else accr)[0:2, 2:3], psS[0:2, 0:512], axis=AX.X),
                        waits=[m_st.marker])
                    ps_pool.add_reader(si, m_rd.marker)
                    m_mx = p.op("vector", lambda v, acc=acc, ch=ch: v.tensor_max(
                        acc[0:2, ch:ch + 1], acc[0:2, ch:ch + 1], (accq if acc is accq else accr)[0:2, 2:3]),
                        waits=[m_rd.marker])
                    sq_user = m_mx.marker
        for ch in range(2):
            m1 = p.op("vector", lambda v, ch=ch: v.tensor_mul(accq[0:2, 3:4], accq[0:2, ch:ch + 1],
                                                              accr[0:2, ch:ch + 1]), waits=[sq_user])
            m2 = p.op("scalar", lambda s: s.sqrt(accq[0:2, 3:4], accq[0:2, 3:4]), waits=[m1.marker])
            m3 = p.op("vector", lambda v: v.tensor_scalar_mul(accq[0:2, 3:4], accq[0:2, 3:4], -0.125),
                      waits=[m2.marker])
            ti, tw = ps_pool.acquire()
            psT = ps_pool.banks[ti]
            m4 = p.op("tensor", lambda t, psT=psT: t.matmul(psT[0:1, 0:2], accq[0:2, 3:4], ident2[:],
                                                            start=True, stop=True), waits=[m3.marker] + tw)
            ps_pool.set_writer(ti, m4.marker)
            m5 = p.op("scalar", lambda s, ch=ch, psT=psT: s.copy(nrow[0:1, ch * 2:(ch + 1) * 2], psT[0:1, 0:2]),
                      waits=[m4.marker])
            ps_pool.add_reader(ti, m5.marker)
            sq_user = m5.marker
        for h in range(4):
            ch, hw = h // 2, h % 2
            bi, bw = ps_pool.acquire()
            psb = ps_pool.banks[bi]
            mb = p.op("tensor", lambda t, h=h, ch=ch, hw=hw, psb=psb: t.matmul(
                psb[0:128, 0:1], ones1x128[:], nrow[0:1, ch * 2 + hw:ch * 2 + hw + 1],
                start=True, stop=True), waits=[sq_user] + bw)
            ps_pool.set_writer(bi, mb.marker)
            mc = p.op("scalar", lambda s, h=h, psb=psb: s.copy(bias_cols[:, h:h + 1], psb[0:128, 0:1]),
                      waits=[mb.marker])
            ps_pool.add_reader(bi, mc.marker)
            bias_ready.append(mc.marker)

        def mk_head(h):
            def out_write(qb, ps_ot, deps, h=h):
                ob = o_bf[h // 2]
                ooff = 64 * (h % 2)
                return p.op("vector", lambda v, qb=qb, ps_ot=ps_ot, ob=ob, ooff=ooff: v.tensor_mul(
                    ob[ooff:ooff + 64, qb * 512:(qb + 1) * 512], ps_ot[0:64, 0:512], lbc[0:64, :]), waits=deps)

            return {
                "k_lhsT": lambda c, h=h: k_sb[h // 2][64 * (h % 2):64 * (h % 2) + 64, c * 128:(c + 1) * 128],
                "q_rhs": lambda qb, h=h: q_sb[h // 2][64 * (h % 2):64 * (h % 2) + 64, qb * 512:(qb + 1) * 512],
                "v_lhsT": lambda c, h=h: v_bf[:, c * 260 + 65 * h:c * 260 + 65 * (h + 1)],
                "out_write": out_write,
                "bias_col": bias_cols[:, h:h + 1],
                "bias_dep": bias_ready[h],
                "qkv_dep": all_lds,
            }

        cfg = {
            "T": T, "hd": 64, "scale": 0.125, "mask_sb": mask_sb,
            "pt": pt, "recl": recl, "lbc": lbc, "ones1x128": ones1x128,
            "lbc_row": row8,
            "ot_banks": ot_banks,
            "heads": [mk_head(h) for h in range(4)],
        }
        emit_attention(p, nc, ps_pool, cfg)
        m_all_v = Marker("vector", p.counts["vector"])
        p.dma(c_out[0:128, :], o_bf[0][:, :], sts, waits=[m_all_v])
        p.dma(c_out[128:256, :], o_bf[1][:, :], sts, waits=[m_all_v])



def emit_DE(p, nc, gctx, sm, io, T, is_D):
    S = T // 4
    TE = S + XH
    SS = min(S, 1024)
    pf = "d_" if is_D else "e_"
    tg = "D" if is_D else "E"

    xe = io[pf + "xe"]
    if is_D:
        attne = io["d_attne_c"]
        proj = io["d_projb"]
    wqkv = io[pf + "wqkvb"]
    qs_c = io[pf + "qs"]
    ks_c = io[pf + "ks"]
    qsrow = io[pf + "qsrow"]
    ksrow = io[pf + "ksrow"]
    wo_i = io[pf + "wob"]
    lng = io[pf + "lng"]
    lnb = io[pf + "lnb"]
    w1_i = io[pf + "w1b"]
    b1a_i = io[pf + "b1a"]
    b1g_i = io[pf + "b1g"]
    w2_i = io[pf + "w2b"]
    b2_i = io[pf + "b2"]
    masks_i = io["m_masks"]
    ind_i = io["m_ind"]
    tabs_i = io["m_tabs"]
    if not is_D:
        x3own = io["e_x3"]
    if is_D:
        o_xn = io["xa_d_own"]
        x3d = io["x3e_d"]
    else:
        o_out = io["e_out"]
        x3d = xe
    xmd = io["xmd_" + tg]

    ext_tiles = [(t0, min(512, TE - t0)) for t0 in range(0, TE, 512)]

    lds = sm["lds"]
    xls = sm["xls"]
    wl1 = sm["wl1"]
    wl2 = sm["wl2"]
    wl3 = sm["wl3"]
    tbl = sm["tbl"]
    mds = sm["mds"]
    sts = sm["sts"]

    with ExitStack() as ctx:
        e = ctx.enter_context
        sqb = [e(nc.sbuf_tensor(f"sqb{tg}{i}", [128, 512], F32)) for i in range(2)]
        stm = e(nc.sbuf_tensor(f"stm{tg}", [1, 512], F32))
        stv = e(nc.sbuf_tensor(f"stv{tg}", [1, 512], F32))
        strd = e(nc.sbuf_tensor(f"strd{tg}", [1, 512], F32))
        stmrs = e(nc.sbuf_tensor(f"stmrs{tg}", [1, 512], F32))
        stv2 = e(nc.sbuf_tensor(f"stv2{tg}", [1, 512], F32))
        st1 = e(nc.sbuf_tensor(f"st1{tg}", [2, 512], F32))
        scr = {"sqb": sqb, "stm": stm, "stv": stv, "strd": strd, "stmrs": stmrs, "stv2": stv2}
        ln_tmp = e(nc.sbuf_tensor(f"ln_tmp{tg}", [128, 512], F32))
        qtmp = e(nc.sbuf_tensor(f"qtmp{tg}", [128, 512], F32))
        qn_f = e(nc.sbuf_tensor(f"qn_f{tg}", [128, 512], F32))
        rt1 = e(nc.sbuf_tensor(f"rt1{tg}", [128, 512], F32))
        rt2 = e(nc.sbuf_tensor(f"rt2{tg}", [128, 512], F32))
        wbf2 = [e(nc.sbuf_tensor(f"wbf{tg}{i}", [128, 8 * 128], BF16)) for i in range(2)]
        wo_bf = e(nc.sbuf_tensor(f"wo_bf{tg}", [128, 4 * 1024], BF16))
        qscol = e(nc.sbuf_tensor(f"qscol{tg}", [128, 1], F32))
        kscol = e(nc.sbuf_tensor(f"kscol{tg}", [128, 1], F32))
        lngc = e(nc.sbuf_tensor(f"lngc{tg}", [128, 8], F32))
        lnbc = e(nc.sbuf_tensor(f"lnbc{tg}", [128, 8], F32))
        b1ac = e(nc.sbuf_tensor(f"b1ac{tg}", [128, 22], F32))
        b1gc = e(nc.sbuf_tensor(f"b1gc{tg}", [128, 22], F32))
        b2c = e(nc.sbuf_tensor(f"b2c{tg}", [128, 8], F32))
        indc = e(nc.sbuf_tensor(f"indc{tg}", [128, 1], F32))
        gkrow = e(nc.sbuf_tensor(f"gkrow{tg}", [1, 128], F32))
        red = e(nc.sbuf_tensor(f"red{tg}", [1, 8], F32))
        bias_col = e(nc.sbuf_tensor(f"biascol{tg}", [128, 1], F32))
        mask_sb = e(nc.sbuf_tensor(f"mask_sb{tg}", [128, 16 * 512], BF16))
        pt = [e(nc.sbuf_tensor(f"pt{tg}{i}", [128, 512], BF16)) for i in range(4)]
        recl = e(nc.sbuf_tensor(f"recl{tg}", [1, 512], F32))
        lbc = e(nc.sbuf_tensor(f"lbc{tg}", [64, 512], F32))
        o1_sb = e(nc.sbuf_tensor(f"o1_sb{tg}", [128, 4], BF16))
        xm1_f = e(nc.sbuf_tensor(f"xm1_f{tg}", [128, 8], F32))
        o_bf = [e(nc.sbuf_tensor(f"o{tg}{i}", [128, SS], BF16)) for i in range(4)]
        xbfstg = [e(nc.sbuf_tensor(f"xbfstg{tg}{i}", [128, 512], BF16)) for i in range(8)]
        xstg = [e(nc.sbuf_tensor(f"xstg{tg}{i}", [128, 513], F32)) for i in range(8)]
        ps_pool = PsumPool(nc, ctx, name=f"ps{tg}")
        ot_banks = [e(nc.psum_tensor(f"ot{tg}{i}", [128, 512], F32)) for i in range(2)]

        ones_col = const_sbuf(nc, p, gctx, lds, "ones_col", np.ones((128, 1), np.float32))
        ones1x128 = const_sbuf(nc, p, gctx, lds, "ones1x128", np.ones((1, 128), np.float32))
        eps5 = const_sbuf(nc, p, gctx, lds, "eps5", np.full((128, 1), 1e-5, np.float32))
        sel64 = const_sbuf(nc, p, gctx, lds, "sel64", sel_matrix(64))
        selT64 = const_sbuf(nc, p, gctx, lds, "selT64", sel_matrix(64).T.copy())
        perm = np.zeros((128, 128), np.float32)
        for pp in range(128):
            perm[(pp // 64) * 64 + (pp + 32) % 64, pp] = 1.0
        permM = const_sbuf(nc, p, gctx, lds, "permM", perm)
        m1np = np.zeros((128, 5), np.float32)
        m1np[127, 0] = 1.0
        m1np[:, 1:] = 1.0
        mask1 = const_sbuf(nc, p, gctx, lds, "mask1", m1np.astype(BF16NP))
        identf = const_sbuf(nc, p, gctx, lds, "identf", np.eye(128, dtype=np.float32))

        p.dma(qscol[:], qs_c, lds)
        p.dma(kscol[:], ks_c, lds)
        p.dma(lngc[:], lng, lds)
        p.dma(lnbc[:], lnb, lds)
        p.dma(b1ac[:], b1a_i, lds)
        p.dma(b1gc[:], b1g_i, lds)
        p.dma(b2c[:], b2_i, lds)
        p.dma(indc[:], ind_i, lds)
        p.dma(gkrow[0:1, 0:64], qsrow, lds)
        p.dma(gkrow[0:1, 64:128], ksrow, lds)
        for j in range(16):
            p.dma(mask_sb[:, j * 512:(j + 1) * 512], masks_i[j], lds)
        all_lds = Marker(lds, lds.count)

        r1 = p.op("vector", lambda v: v.reduce_max(red[0:1, 0:1], gkrow[0:1, 0:64], axis=AX.X,
                                                   apply_absolute_value=True), waits=[all_lds])
        r2 = p.op("vector", lambda v: v.reduce_max(red[0:1, 1:2], gkrow[0:1, 64:128], axis=AX.X,
                                                   apply_absolute_value=True), waits=[r1.marker])
        r3 = p.op("vector", lambda v: v.tensor_mul(red[0:1, 2:3], red[0:1, 0:1], red[0:1, 1:2]), waits=[r2.marker])
        r4 = p.op("vector", lambda v: v.tensor_scalar_mul(red[0:1, 3:4], red[0:1, 2:3], -8.0), waits=[r3.marker])
        bi, bw = ps_pool.acquire()
        psb = ps_pool.banks[bi]
        r5 = p.op("tensor", lambda t: t.matmul(psb[0:128, 0:1], ones1x128[:], red[0:1, 3:4], start=True, stop=True),
                  waits=[r4.marker] + bw)
        ps_pool.set_writer(bi, r5.marker)
        r6 = p.op("scalar", lambda s: s.copy(bias_col[:], psb[0:128, 0:1]), waits=[r5.marker])
        ps_pool.add_reader(bi, r6.marker)
        bias_ready = r6.marker

        for kc in range(4):
            lw = p.dma(wo_bf[:, kc * 1024:(kc + 1) * 1024], wo_i[kc * 128:(kc + 1) * 128, :], wl1)
        wo_ready = Marker(wl1, wl1.count)

        if is_D:
            with ExitStack() as s1:
                e1 = s1.enter_context
                projbf = e1(nc.sbuf_tensor("projbf", [128, 8 * 1024], BF16))
                x3t = [e1(nc.sbuf_tensor(f"x3t{i}", [128, 512], F32)) for i in range(8)]
                for kc in range(8):
                    lw = p.dma(projbf[:, kc * 1024:(kc + 1) * 1024], proj[kc * 128:(kc + 1) * 128, :], wl1)
                proj_ready = Marker(wl1, wl1.count)
                xe_prev = None
                st_prev = []
                for (t0, w) in ext_tiles:
                    lxs = [p.dma(xstg[kc][:, 0:w], xe[kc * 128:(kc + 1) * 128, t0:t0 + w], xls,
                                 waits=[xe_prev]) for kc in range(8)]
                    las = [p.dma(xbfstg[kc][:, 0:w], attne[kc * 128:(kc + 1) * 128, t0:t0 + w], xls,
                                 waits=[xe_prev]) for kc in range(8)]
                    x_ld = Marker(xls, xls.count)
                    vdone = []
                    mm = None
                    for m in range(8):
                        ai, aw = ps_pool.acquire()
                        ps = ps_pool.banks[ai]
                        for kc in range(8):
                            mm = p.op("tensor", lambda t, kc=kc, m=m, w=w, ps=ps: t.matmul(
                                ps[:, 0:w], projbf[:, kc * 1024 + m * 128:kc * 1024 + (m + 1) * 128],
                                xbfstg[kc][:, 0:w], start=(kc == 0), stop=(kc == 7)),
                                waits=([proj_ready, x_ld] if kc == 0 else []) + (aw if kc == 0 else []))
                        ps_pool.set_writer(ai, mm.marker)
                        mo = p.op("vector", lambda v, m=m, w=w, ps=ps: v.tensor_add(
                            x3t[m][:, 0:w], xstg[m][:, 0:w], ps[:, 0:w]),
                            waits=[mm.marker] + st_prev)
                        ps_pool.add_reader(ai, mo.marker)
                        vdone.append(mo.marker)
                    xe_prev = vdone[-1]

                    def dstw(kc, tw, ps_b1, ps_b2, deps):
                        tt0, ww = tw
                        a1 = p.op("vector", lambda v, kc=kc, ww=ww: v.tensor_mul(
                            ln_tmp[:, 0:ww], x3t[kc][:, 0:ww], ps_b1[0:128, 0:ww]), waits=deps)
                        a2 = p.op("vector", lambda v, kc=kc, ww=ww: v.tensor_add(
                            x3t[kc][:, 0:ww], ln_tmp[:, 0:ww], ps_b2[0:128, 0:ww]), waits=[a1.marker])
                        return a2

                    lnm = emit_ln(p, nc, ps_pool, scr, lambda kc: x3t[kc][:, :], dstw, 8, 512, w, eps5,
                                  ones_col, ones1x128, extra_waits=[Marker("vector", p.counts["vector"])],
                                  tile_list=[(0, w)], const_dep=all_lds)
                    ln_last = lnm[-1].marker
                    st_prev = []
                    for kc in range(8):
                        stx = p.dma(x3d[kc * 128:(kc + 1) * 128, t0:t0 + w], x3t[kc][:, 0:w], mds,
                                    waits=[ln_last])
                        st_prev.append(stx.marker)
                x3_wr = Marker(mds, mds.count)
        else:
            x3_wr = None

        NSP = S // SS
        QTs = SS + QH
        NVs = QTs // 128
        NWs = SS // 512
        qkv_tiles = [(t0, min(512, QTs - t0)) for t0 in range(0, QTs, 512)]
        lbc_user = None
        xm_wr = []
        for sp in range(NSP):
            sp0 = sp * SS
            with ExitStack() as s2:
                e2 = s2.enter_context
                q_rot = [e2(nc.sbuf_tensor(f"q_rot{tg}{sp}_{i}", [128, QTs], BF16)) for i in range(4)]
                k_rot = [e2(nc.sbuf_tensor(f"k_rot{tg}{sp}_{i}", [128, QTs], BF16)) for i in range(4)]
                v_bf = e2(nc.sbuf_tensor(f"v_bf{tg}{sp}", [128, NVs * 520], BF16))
                wvres = e2(nc.sbuf_tensor(f"wvres{tg}{sp}", [128, 4096], BF16))
                tabstg = e2(nc.sbuf_tensor(f"tabstg{tg}{sp}", [128, 4 * 512], F32))

                sp_tsnap = Marker("tensor", p.counts["tensor"])
                wq_prev = None
                wslot_user = [sp_tsnap, sp_tsnap]
                tab_prev = None
                qk_done = []
                v_done = []
                xbf_user = None
                wv_ready = None
                mmv = None
                for (t0, w) in qkv_tiles:
                    base = sp0 + t0
                    lx3 = [p.dma(xstg[kc][:, 0:w + 1], x3d[kc * 128:(kc + 1) * 128, base:base + w + 1], xls,
                                 waits=[xbf_user, x3_wr]) for kc in range(8)]
                    x_ld = Marker(xls, xls.count)
                    casts = []
                    for kc in range(8):
                        off = 1 if kc < 4 else 0
                        cm = p.op("vector", lambda v, kc=kc, w=w, off=off: v.tensor_copy(
                            xbfstg[kc][:, 0:w], xstg[kc][:, off:off + w]), waits=[x_ld])
                        casts.append(cm.marker)
                    cast_done = casts[-1]
                    xbf_user = cast_done
                    lt = []
                    for ti in range(4):
                        lt.append(p.dma(tabstg[:, ti * 512:ti * 512 + w], tabs_i[ti][:, sp0 + t0:sp0 + t0 + w],
                                        tbl, waits=[tab_prev]))
                    tab_ld = Marker(tbl, tbl.count)
                    for oc in range(8):
                        is_q = oc < 4
                        wsl = oc % 2
                        wb = wbf2[wsl]
                        lw = p.dma(wb[:, :].rearrange("p (kc m) -> p kc m", kc=8),
                                   wqkv[:, oc * 128:(oc + 1) * 128].rearrange("(kc p) m -> p kc m", p=128), wl1,
                                   waits=[wslot_user[wsl]])
                        ai, aw = ps_pool.acquire()
                        ps = ps_pool.banks[ai]
                        mm = None
                        for kc in range(8):
                            mm = p.op("tensor", lambda t, kc=kc, w=w, ps=ps, wb=wb: t.matmul(
                                ps[:, 0:w], wb[:, kc * 128:(kc + 1) * 128], xbfstg[kc][:, 0:w],
                                start=(kc == 0), stop=(kc == 7)),
                                waits=([lw.marker, cast_done] if kc == 0 else []) + (aw if kc == 0 else []))
                        wq_prev = mm.marker
                        wslot_user[wsl] = mm.marker
                        ps_pool.set_writer(ai, mm.marker)
                        cp = p.op("scalar", lambda s, w=w, ps=ps: s.copy(qtmp[:, 0:w], ps[:, 0:w]),
                                  waits=[mm.marker] + qk_done[-1:])
                        ps_pool.add_reader(ai, cp.marker)
                        gcol = qscol if is_q else kscol
                        scrD = {"sq": sqb[0], "st1": st1, "sel": sel64, "selT": selT64}
                        m_ap = emit_group_norm_apply(p, nc, ps_pool, qtmp[:, 0:w], qn_f[:, 0:w], 64, gcol[:, :],
                                                     scrD, "l2", extra_waits=[cp.marker], clip=1e-12)
                        si2, sw2 = ps_pool.acquire()
                        psw = ps_pool.banks[si2]
                        msw = p.op("tensor", lambda t, w=w, psw=psw: t.matmul(psw[:, 0:w], permM[:], qn_f[:, 0:w],
                                                                             start=True, stop=True),
                                   waits=[m_ap.marker] + sw2)
                        ps_pool.set_writer(si2, msw.marker)
                        ta = 0 if is_q else 2 * 512
                        tb = 512 if is_q else 3 * 512
                        mr1 = p.op("vector", lambda v, w=w, ta=ta: v.tensor_mul(rt1[:, 0:w], qn_f[:, 0:w],
                                                                               tabstg[:, ta:ta + w]),
                                   waits=[m_ap.marker, tab_ld])
                        mr2 = p.op("vector", lambda v, w=w, tb=tb, psw=psw: v.tensor_mul(
                            rt2[:, 0:w], psw[:, 0:w], tabstg[:, tb:tb + w]),
                            waits=[msw.marker, tab_ld, mr1.marker])
                        ps_pool.add_reader(si2, mr2.marker)
                        dstt = (q_rot if is_q else k_rot)[oc % 4]
                        mr3 = p.op("vector", lambda v, w=w, t0=t0, dstt=dstt: v.tensor_add(
                            dstt[:, t0:t0 + w], rt1[:, 0:w], rt2[:, 0:w]), waits=[mr2.marker])
                        qk_done.append(mr3.marker)
                    tab_prev = qk_done[-1]
                    if t0 == 0:
                        for vc in range(4):
                            lwv = p.dma(wvres[:, 0:4096].rearrange("p (kc u) -> p kc u", kc=8)[:, :, vc * 128:(vc + 1) * 128],
                                        wqkv[:, 1024 + vc * 128:1024 + (vc + 1) * 128].rearrange(
                                            "(kc p) m -> p kc m", p=128), wl2, waits=[sp_tsnap])
                        wv_ready = Marker(wl2, wl2.count)
                    for st4 in range(w // 128):
                        vtile = (t0 + st4 * 128) // 128
                        vi, vw2 = ps_pool.acquire()
                        psv = ps_pool.banks[vi]
                        for kc in range(8):
                            mmv = p.op("tensor", lambda t, kc=kc, st4=st4, psv=psv: t.matmul(
                                psv[:, :], xbfstg[kc][:, st4 * 128:(st4 + 1) * 128],
                                wvres[:, kc * 512:(kc + 1) * 512],
                                start=(kc == 0), stop=(kc == 7)),
                                waits=([wv_ready, cast_done] if kc == 0 else []) + (vw2 if kc == 0 else []))
                        ps_pool.set_writer(vi, mmv.marker)
                        dst = v_bf[:, vtile * 520:(vtile + 1) * 520].rearrange("p (h u) -> p h u", h=8)[:, :, 0:64]
                        srcv = psv[:, :].rearrange("p (h u) -> p h u", h=8)
                        cpv = p.op("scalar", lambda s, dst=dst, srcv=srcv: s.copy(dst, srcv),
                                   waits=[mmv.marker] + v_done[-1:])
                        ps_pool.add_reader(vi, cpv.marker)
                        v_done.append(cpv.marker)
                    xbf_user = mmv.marker
                m_ones = p.op("vector", lambda v: v.memset(
                    v_bf[:, :].rearrange("p (t h u) -> p t h u", h=8, u=65)[:, :, :, 64:65], 1.0),
                    waits=v_done[-1:])
                v_ready = m_ones.marker
                qk_ready = Marker("vector", p.counts["vector"])

                ot_state = [{"w": None, "r": []}, {"w": None, "r": []}]
                seq = 0
                prev_exp = {}
                pend_epi = [None]
                recl_user = [None]
                for wdx in range(NWs):
                    gw0 = (sp0 // 512 + wdx == 0)
                    for h in range(8):
                        osl = seq % 2
                        seq += 1
                        ps_ot = ot_banks[osl]
                        ow = list(ot_state[osl]["r"])
                        if ot_state[osl]["w"] is not None:
                            ow.append(ot_state[osl]["w"])
                        ot_state[osl] = {"w": None, "r": []}
                        kth = k_rot[h // 2]
                        koff = 64 * (h % 2)
                        qth = q_rot[h // 2]
                        kbase = 512 * wdx + 128
                        qcol = 512 * wdx + QH
                        pv_last = None
                        pend = []
                        for c in range(8):
                            ai, aw = ps_pool.acquire()
                            ps_a = ps_pool.banks[ai]
                            mm = p.op("tensor", lambda t, c=c, ps_a=ps_a, kth=kth, koff=koff, qth=qth,
                                      kbase=kbase, qcol=qcol: t.matmul(
                                ps_a[:, :], kth[koff:koff + 64, kbase + c * 128:kbase + (c + 1) * 128],
                                qth[koff:koff + 64, qcol:qcol + 512], start=True, stop=True),
                                waits=aw + [qk_ready, bias_ready] + ow)
                            ow = []
                            ps_pool.set_writer(ai, mm.marker)
                            buf = pt[c % 4]
                            ex = p.op("scalar", lambda s, ps_a=ps_a, buf=buf: s.activation(
                                out=buf[:], in_=ps_a[:, :], func=AF.Exp, bias=bias_col[:], scale=8.0),
                                waits=[mm.marker] + ([prev_exp.get(c % 4)] if prev_exp.get(c % 4) else []))
                            ps_pool.add_reader(ai, ex.marker)
                            mvar = (8 + c) if gw0 else c
                            mk = p.op("vector", lambda v, buf=buf, mvar=mvar: v.tensor_mul(
                                buf[:], buf[:], mask_sb[:, mvar * 512:(mvar + 1) * 512]),
                                waits=[ex.marker, all_lds])
                            if c == 0 and pend_epi[0] is not None:
                                pend_epi[0]()
                                pend_epi[0] = None
                            pend.append((mk.marker, buf, c))
                            if len(pend) > 3:
                                d0, b0, c0 = pend.pop(0)
                                pv_last = p.op("tensor", lambda t, c0=c0, b0=b0, ps_ot=ps_ot, wdx=wdx, h=h: t.matmul(
                                    ps_ot[0:65, :],
                                    v_bf[:, (4 * wdx + 1 + c0) * 520 + 65 * h:(4 * wdx + 1 + c0) * 520 + 65 * (h + 1)],
                                    b0[:], start=(c0 == 0), stop=(c0 == 7)),
                                    waits=[d0, v_ready])
                                prev_exp[c0 % 4] = pv_last.marker
                        while pend:
                            d0, b0, c0 = pend.pop(0)
                            pv_last = p.op("tensor", lambda t, c0=c0, b0=b0, ps_ot=ps_ot, wdx=wdx, h=h: t.matmul(
                                ps_ot[0:65, :],
                                v_bf[:, (4 * wdx + 1 + c0) * 520 + 65 * h:(4 * wdx + 1 + c0) * 520 + 65 * (h + 1)],
                                b0[:], start=(c0 == 0), stop=(c0 == 7)),
                                waits=[d0, v_ready])
                            prev_exp[c0 % 4] = pv_last.marker
                        ot_state[osl]["w"] = pv_last.marker
                        m_rc = p.op("vector", lambda v, ps_ot=ps_ot: v.reciprocal(recl[:], ps_ot[64:65, 0:512]),
                                    waits=[pv_last.marker] + ([recl_user[0]] if recl_user[0] else []))
                        ot_state[osl]["r"] = [m_rc.marker]

                        def mk_epi(osl=osl, ps_ot=ps_ot, m_rc=m_rc, h=h, wdx=wdx):
                            def epi():
                                nonlocal lbc_user
                                bi2, bw2 = ps_pool.acquire()
                                ps_b = ps_pool.banks[bi2]
                                m_bc = p.op("tensor", lambda t, ps_b=ps_b: t.matmul(
                                    ps_b[0:64, :], ones1x128[0:1, 0:64], recl[:], start=True, stop=True),
                                    waits=[m_rc.marker] + bw2)
                                recl_user[0] = m_bc.marker
                                ps_pool.set_writer(bi2, m_bc.marker)
                                m_cp = p.op("scalar", lambda s, ps_b=ps_b: s.copy(lbc[0:64, :], ps_b[0:64, :]),
                                            waits=[m_bc.marker, lbc_user])
                                ps_pool.add_reader(bi2, m_cp.marker)
                                mo = p.op("vector", lambda v, ps_ot=ps_ot, h=h, wdx=wdx: v.tensor_mul(
                                    o_bf[h // 2][64 * (h % 2):64 * (h % 2) + 64, 512 * wdx:512 * (wdx + 1)],
                                    ps_ot[0:64, 0:512], lbc[0:64, :]), waits=[m_cp.marker])
                                lbc_user = mo.marker
                                ot_state[osl]["r"].append(mo.marker)
                            return epi

                        pend_epi[0] = mk_epi()
                if pend_epi[0] is not None:
                    pend_epi[0]()
                    pend_epi[0] = None
                attn_done = Marker("vector", p.counts["vector"])

                if sp == 0:
                    o1done = []
                    for h in range(8):
                        kth = k_rot[h // 2]
                        koff = 64 * (h % 2)
                        qth = q_rot[h // 2]
                        ai, aw = ps_pool.acquire()
                        ps_a = ps_pool.banks[ai]
                        mmz = None
                        for c in range(5):
                            mmz = p.op("tensor", lambda t, c=c, ps_a=ps_a, kth=kth, koff=koff, qth=qth: t.matmul(
                                ps_a[:, c:c + 1], kth[koff:koff + 64, c * 128:(c + 1) * 128],
                                qth[koff:koff + 64, QH - 1:QH], start=True, stop=True),
                                waits=(aw + [qk_ready, bias_ready, attn_done]) if c == 0 else [])
                        ps_pool.set_writer(ai, mmz.marker)
                        ex = p.op("scalar", lambda s, ps_a=ps_a: s.activation(
                            out=pt[0][:, 0:5], in_=ps_a[:, 0:5], func=AF.Exp, bias=bias_col[:], scale=8.0),
                            waits=[mmz.marker] + o1done[-1:])
                        ps_pool.add_reader(ai, ex.marker)
                        mk = p.op("vector", lambda v: v.tensor_mul(pt[0][:, 0:5], pt[0][:, 0:5], mask1[:]),
                                  waits=[ex.marker])
                        oi2, ow2 = ps_pool.acquire()
                        ps_o1 = ps_pool.banks[oi2]
                        pvz = None
                        for c in range(5):
                            pvz = p.op("tensor", lambda t, c=c, ps_o1=ps_o1, h=h: t.matmul(
                                ps_o1[0:65, 0:1], v_bf[:, c * 520 + 65 * h:c * 520 + 65 * (h + 1)],
                                pt[0][:, c:c + 1], start=(c == 0), stop=(c == 4)),
                                waits=([mk.marker, v_ready] + ow2) if c == 0 else [])
                        ps_pool.set_writer(oi2, pvz.marker)
                        rc1 = p.op("vector", lambda v, ps_o1=ps_o1: v.reciprocal(recl[0:1, 0:1], ps_o1[64:65, 0:1]),
                                   waits=[pvz.marker, lbc_user])
                        bi3, bw3 = ps_pool.acquire()
                        ps_b3 = ps_pool.banks[bi3]
                        mb3 = p.op("tensor", lambda t, ps_b3=ps_b3: t.matmul(
                            ps_b3[0:64, 0:1], ones1x128[0:1, 0:64], recl[0:1, 0:1], start=True, stop=True),
                            waits=[rc1.marker] + bw3)
                        ps_pool.set_writer(bi3, mb3.marker)
                        mc3 = p.op("scalar", lambda s, ps_b3=ps_b3: s.copy(lbc[0:64, 0:1], ps_b3[0:64, 0:1]),
                                   waits=[mb3.marker, lbc_user])
                        ps_pool.add_reader(bi3, mc3.marker)
                        mo1 = p.op("vector", lambda v, h=h, ps_o1=ps_o1: v.tensor_mul(
                            o1_sb[64 * (h % 2):64 * (h % 2) + 64, h // 2:h // 2 + 1],
                            ps_o1[0:64, 0:1], lbc[0:64, 0:1]), waits=[mc3.marker])
                        lbc_user = mo1.marker
                        ps_pool.add_reader(oi2, mo1.marker)
                        o1done.append(mo1.marker)
                    vsnap = Marker("vector", p.counts["vector"])
                    lxcs = [p.dma(xm1_f[:, m:m + 1], x3d[m * 128:(m + 1) * 128, XH - 1:XH], xls,
                                  waits=[vsnap, x3_wr], allow_slow_non_contiguous=True) for m in range(8)]
                    lxc_all = Marker(xls, xls.count)
                    xm_done = []
                    for m in range(8):
                        zi, zw = ps_pool.acquire()
                        psz = ps_pool.banks[zi]
                        mmx = None
                        for kc in range(4):
                            mmx = p.op("tensor", lambda t, kc=kc, m=m, psz=psz: t.matmul(
                                psz[:, 0:1], wo_bf[:, kc * 1024 + m * 128:kc * 1024 + (m + 1) * 128],
                                o1_sb[:, kc:kc + 1], start=(kc == 0), stop=(kc == 3)),
                                waits=([o1done[-1], wo_ready] if kc == 0 else []) + (zw if kc == 0 else []))
                        ps_pool.set_writer(zi, mmx.marker)
                        mz1 = p.op("vector", lambda v, m=m, psz=psz: v.tensor_add(
                            ln_tmp[:, 0:1], xm1_f[:, m:m + 1], psz[:, 0:1]),
                            waits=[mmx.marker, lxc_all] + xm_done[-1:])
                        ps_pool.add_reader(zi, mz1.marker)
                        mz2 = p.op("vector", lambda v, m=m: v.tensor_mul(xm1_f[:, m:m + 1], ln_tmp[:, 0:1],
                                                                         indc[:, :]),
                                   waits=[mz1.marker, all_lds])
                        xm_done.append(mz2.marker)
                    for m in range(8):
                        stm1 = p.dma(xmd[m * 128:(m + 1) * 128, 0:1], xm1_f[:, m:m + 1], mds,
                                     waits=[xm_done[-1]], allow_slow_non_contiguous=True)
                    xm_wr.append(stm1.marker)

                x3rd_prev = None
                for (t0, w) in [(i * 512, 512) for i in range(SS // 512)]:
                    gt0 = sp0 + t0
                    lx3 = [p.dma(xstg[kc][:, 0:512], x3d[kc * 128:(kc + 1) * 128, XH + gt0:XH + gt0 + 512],
                                 xls, waits=[x3rd_prev, x3_wr]) for kc in range(8)]
                    x_ld = Marker(xls, xls.count)
                    tile_done = []
                    for m in range(8):
                        ai, aw = ps_pool.acquire()
                        ps = ps_pool.banks[ai]
                        mm = None
                        for kc in range(4):
                            mm = p.op("tensor", lambda t, kc=kc, m=m, t0=t0, ps=ps: t.matmul(
                                ps[:, :], wo_bf[:, kc * 1024 + m * 128:kc * 1024 + (m + 1) * 128],
                                o_bf[kc][:, t0:t0 + 512], start=(kc == 0), stop=(kc == 3)),
                                waits=([attn_done, wo_ready] if kc == 0 else []) + (aw if kc == 0 else []))
                        ps_pool.set_writer(ai, mm.marker)
                        mo = p.op("vector", lambda v, m=m, ps=ps: v.tensor_add(
                            sqb[1][:, 0:512], xstg[m][:, 0:512], ps[:, :]),
                            waits=[mm.marker, x_ld] + tile_done[-1:])
                        ps_pool.add_reader(ai, mo.marker)
                        stx = p.dma(xmd[m * 128:(m + 1) * 128, 1 + gt0:1 + gt0 + 512], sqb[1][:, 0:512], mds,
                                    waits=[mo.marker])
                        tile_done.append(stx.marker)
                    x3rd_prev = tile_done[-1]
                    xm_wr.append(tile_done[-1])
        xm_all = Marker(mds, mds.count)
        with ExitStack() as s3:
            e3 = s3.enter_context
            hbuf = e3(nc.sbuf_tensor(f"hbuf{tg}", [128, 22 * 512], BF16))
            hln_bf = [e3(nc.sbuf_tensor(f"hln{tg}{i}", [128, 512], BF16)) for i in range(8)]
            abf = e3(nc.sbuf_tensor(f"abf{tg}", [128, 512], BF16))
            gbf = e3(nc.sbuf_tensor(f"gbf{tg}", [128, 512], BF16))
            w2bf2 = [e3(nc.sbuf_tensor(f"w2bf{tg}{i}", [128, 22 * 128], BF16)) for i in range(2)]
            xm_stg = [e3(nc.sbuf_tensor(f"xmstg{tg}{i}", [128, 513], F32)) for i in range(8)]
            xnt = [e3(nc.sbuf_tensor(f"xnt{tg}{i}", [128, 512], F32)) for i in range(8)]
            ed_tmp = e3(nc.sbuf_tensor(f"ed_tmp{tg}", [128, 512], F32))
            trp = e3(nc.sbuf_tensor(f"trp{tg}", [128, 512], F32)) if not is_D else None

            tsnap5 = Marker("tensor", p.counts["tensor"])
            wslot_user5 = [tsnap5, tsnap5]
            w2slot_user = [tsnap5, tsnap5]
            hb_users = []
            xn_done = []
            fin_st = []
            xm_rd_prev = None
            x3o_prev = None
            for (t0, w) in [(i * 512, 512) for i in range(S // 512)]:
                lxm = [p.dma(xm_stg[kc][:, 0:513], xmd[kc * 128:(kc + 1) * 128, t0:t0 + 513], xls,
                             waits=[xm_rd_prev, xm_all]) for kc in range(8)]
                xm_ld = Marker(xls, xls.count)

                def srcof(kc):
                    off = 1 if kc < 4 else 0
                    return xm_stg[kc][:, off:off + 512]

                def dstw2(kc, tw, ps_b1, ps_b2, deps):
                    tt0, ww = tw
                    a1 = p.op("vector", lambda v, kc=kc, ww=ww: v.tensor_mul(
                        ln_tmp[:, 0:ww], srcof(kc)[:, 0:ww], ps_b1[0:128, 0:ww]), waits=deps)
                    a2 = p.op("vector", lambda v, kc=kc, ww=ww: v.tensor_add(
                        ln_tmp[:, 0:ww], ln_tmp[:, 0:ww], ps_b2[0:128, 0:ww]), waits=[a1.marker])
                    a3 = p.op("vector", lambda v, kc=kc, ww=ww: v.tensor_scalar(
                        ln_tmp[:, 0:ww], ln_tmp[:, 0:ww], lngc[:, kc:kc + 1], None, ALU.mult),
                        waits=[a2.marker, all_lds])
                    a4 = p.op("vector", lambda v, kc=kc, ww=ww: v.tensor_scalar(
                        hln_bf[kc][:, 0:ww], ln_tmp[:, 0:ww], lnbc[:, kc:kc + 1], None, ALU.add),
                        waits=[a3.marker])
                    return a4

                lnm = emit_ln(p, nc, ps_pool, scr, srcof, dstw2, 8, 512, 512, eps5,
                              ones_col, ones1x128, extra_waits=[xm_ld], tile_list=[(0, 512)],
                              const_dep=all_lds)
                hln_ready = lnm[-1].marker
                xm_rd_prev = hln_ready

                for j in range(22):
                    wba = wbf2[0]
                    lw = p.dma(wba[:, :].rearrange("p (kc m) -> p kc m", kc=8),
                               w1_i[:, 256 * j:256 * j + 128].rearrange("(kc p) m -> p kc m", p=128), wl1,
                               waits=[wslot_user5[0]])
                    ai, aw = ps_pool.acquire()
                    ps_aj = ps_pool.banks[ai]
                    mma = None
                    for kc in range(8):
                        mma = p.op("tensor", lambda t, kc=kc, ps_aj=ps_aj, wba=wba: t.matmul(
                            ps_aj[:, :], wba[:, kc * 128:(kc + 1) * 128], hln_bf[kc][:, 0:512],
                            start=(kc == 0), stop=(kc == 7)),
                            waits=([lw.marker, hln_ready] if kc == 0 else []) + (aw if kc == 0 else []))
                    ps_pool.set_writer(ai, mma.marker)
                    wslot_user5[0] = mma.marker
                    ea = p.op("scalar", lambda s, j=j, ps_aj=ps_aj: s.activation(
                        out=abf[:], in_=ps_aj[:, :], func=AF.Identity, bias=b1ac[:, j:j + 1], scale=1.0),
                        waits=[mma.marker] + hb_users[-1:])
                    ps_pool.add_reader(ai, ea.marker)
                    wbg = wbf2[1]
                    lwg = p.dma(wbg[:, :].rearrange("p (kc m) -> p kc m", kc=8),
                                w1_i[:, 256 * j + 128:256 * j + 256].rearrange("(kc p) m -> p kc m", p=128), wl1,
                                waits=[wslot_user5[1]])
                    gi, gw = ps_pool.acquire()
                    ps_gj = ps_pool.banks[gi]
                    mmg = None
                    for kc in range(8):
                        mmg = p.op("tensor", lambda t, kc=kc, ps_gj=ps_gj, wbg=wbg: t.matmul(
                            ps_gj[:, :], wbg[:, kc * 128:(kc + 1) * 128], hln_bf[kc][:, 0:512],
                            start=(kc == 0), stop=(kc == 7)),
                            waits=([lwg.marker] if kc == 0 else []) + (gw if kc == 0 else []))
                    ps_pool.set_writer(gi, mmg.marker)
                    wslot_user5[1] = mmg.marker
                    eg = p.op("scalar", lambda s, j=j, ps_gj=ps_gj: s.activation(
                        out=gbf[:], in_=ps_gj[:, :], func=AF.Gelu, bias=b1gc[:, j:j + 1], scale=1.0),
                        waits=[mmg.marker] + hb_users[-1:])
                    ps_pool.add_reader(gi, eg.marker)
                    mh = p.op("vector", lambda v, j=j: v.tensor_mul(hbuf[:, j * 512:(j + 1) * 512], abf[:], gbf[:]),
                              waits=[ea.marker, eg.marker])
                    hb_users.append(mh.marker)
                h_ready = hb_users[-1]

                if not is_D:
                    lx3o = [p.dma(xstg[kc][:, 0:512], x3own[kc * 128:(kc + 1) * 128, XH + t0:XH + t0 + 512], xls,
                                  waits=[x3o_prev]) for kc in range(8)]
                    x3o_ld = Marker(xls, xls.count)
                for m in range(8):
                    w2b = w2bf2[m % 2]
                    lw2 = p.dma(w2b[:, :].rearrange("p (kc u) -> p kc u", kc=22),
                                w2_i[:, m * 128:(m + 1) * 128].rearrange("(kc p) u -> p kc u", p=128), wl3,
                                waits=[w2slot_user[m % 2]])
                    di, dw = ps_pool.acquire()
                    ps_d = ps_pool.banks[di]
                    mmd = None
                    for kc in range(22):
                        mmd = p.op("tensor", lambda t, kc=kc, ps_d=ps_d, w2b=w2b: t.matmul(
                            ps_d[:, :], w2b[:, kc * 128:(kc + 1) * 128], hbuf[:, kc * 512:(kc + 1) * 512],
                            start=(kc == 0), stop=(kc == 21)),
                            waits=([lw2.marker, h_ready] if kc == 0 else []) + (dw if kc == 0 else []))
                    ps_pool.set_writer(di, mmd.marker)
                    w2slot_user[m % 2] = mmd.marker
                    ed = p.op("scalar", lambda s, m=m, ps_d=ps_d: s.activation(
                        out=ed_tmp[:, 0:512], in_=ps_d[:, :], func=AF.Identity, bias=b2c[:, m:m + 1], scale=1.0),
                        waits=[mmd.marker] + xn_done[-1:])
                    ps_pool.add_reader(di, ed.marker)
                    mo = p.op("vector", lambda v, m=m: v.tensor_add(
                        xnt[m][:, 0:512], xm_stg[m][:, 1:513], ed_tmp[:, 0:512]),
                        waits=[ed.marker])
                    xn_done.append(mo.marker)
                    if is_D:
                        stn = p.dma(o_xn[m * 128:(m + 1) * 128, t0:t0 + 512], xnt[m][:, 0:512], sts,
                                    waits=[mo.marker])
                        fin_st.append(stn)
                        xn_done.append(Marker("vector", p.counts["vector"]))
                    else:
                        mo2 = p.op("vector", lambda v, m=m: v.tensor_add(
                            xnt[m][:, 0:512], xnt[m][:, 0:512], xstg[m][:, 0:512]),
                            waits=[mo.marker, x3o_ld])
                        xn_done.append(mo2.marker)
                if not is_D:
                    x3o_prev = xn_done[-1]

                    def dstw3(kc, tw, ps_b1, ps_b2, deps):
                        tt0, ww = tw
                        a1 = p.op("vector", lambda v, kc=kc, ww=ww: v.tensor_mul(
                            ln_tmp[:, 0:ww], xnt[kc][:, 0:ww], ps_b1[0:128, 0:ww]), waits=deps)
                        a2 = p.op("vector", lambda v, kc=kc, ww=ww: v.tensor_add(
                            xnt[kc][:, 0:ww], ln_tmp[:, 0:ww], ps_b2[0:128, 0:ww]), waits=[a1.marker])
                        return a2

                    lnf = emit_ln(p, nc, ps_pool, scr, lambda kc: xnt[kc][:, :], dstw3, 8, 512, 512, eps5,
                                  ones_col, ones1x128, extra_waits=[xn_done[-1]], tile_list=[(0, 512)],
                                  const_dep=all_lds)
                    lnf_ready = lnf[-1].marker
                    for m in range(8):
                        ti2, tw2 = ps_pool.acquire()
                        ps_t = ps_pool.banks[ti2]
                        trs = []
                        for s4 in range(4):
                            mt = p.op("tensor", lambda t, m=m, s4=s4, ps_t=ps_t: t.transpose(
                                ps_t[:, s4 * 128:(s4 + 1) * 128], xnt[m][:, s4 * 128:(s4 + 1) * 128],
                                identf[:]), waits=(([lnf_ready] + tw2) if s4 == 0 else []))
                            trs.append(mt)
                        ps_pool.set_writer(ti2, trs[-1].marker)
                        ct = p.op("scalar", lambda s, ps_t=ps_t: s.copy(trp[:, 0:512], ps_t[:, :]),
                                  waits=[trs[-1].marker] + ([fin_st[-1].marker] if fin_st else []))
                        ps_pool.add_reader(ti2, ct.marker)
                        stn = p.dma(o_out[t0:t0 + 512, m * 128:(m + 1) * 128].rearrange("(s q) u -> q s u", q=128),
                                    trp[:, 0:512].rearrange("p (s u) -> p s u", s=4), sts, waits=[ct.marker])
                        fin_st.append(stn)
                        xn_done.append(ct.marker)



G4 = [[0, 1, 2, 3], [4, 5, 6, 7]]


def build_fused(T, upto=5, debug=()):
    S = T // 4
    TE = S + XH
    nc = bass.Bass("TRN2", target_bir_lowering=False)

    def inp(name, shape, dt=F32):
        return nc.dram_tensor(name, shape, dt, kind="ExternalInput").ap()

    io = {}
    io["a_x_t"] = inp("a_x_t", [D, T])
    io["a_wq"] = inp("a_wq", [D, 256])
    io["a_wkv"] = inp("a_wkv", [D, 128])
    io["a_wk"] = inp("a_wk", [128, 128])
    io["a_wv"] = inp("a_wv", [128, 128])
    io["a_qg"] = inp("a_qg", [128, 2])
    io["a_kg"] = inp("a_kg", [128, 1])
    io["a_qg32"] = inp("a_qg32", [1, 32])
    io["a_kgr"] = inp("a_kgr", [1, 32])
    io["b_x_t"] = inp("b_x_t", [D, S])
    io["b_wo"] = inp("b_wo", [512, D])
    io["b_wg"] = inp("b_wg", [D, 3072], I32)
    io["b_wu"] = inp("b_wu", [D, 3072], I32)
    io["b_wd"] = inp("b_wd", [3072, D], I32)
    io["b_wq1"] = inp("b_wq1", [D, D])
    io["b_wk1"] = inp("b_wk1", [D, D])
    io["b_wv1"] = inp("b_wv1", [D, D])
    io["b_bq1"] = inp("b_bq1", [128, 8])
    io["b_bk1"] = inp("b_bk1", [128, 8])
    io["b_bv1"] = inp("b_bv1", [1, D])
    io["d_proj"] = inp("d_proj", [D, D])
    for pf in ("d_", "e_"):
        io[pf + "wqkv"] = inp(pf + "wqkv", [D, 1536])
        io[pf + "qs"] = inp(pf + "qs", [128, 1])
        io[pf + "ks"] = inp(pf + "ks", [128, 1])
        io[pf + "qsrow"] = inp(pf + "qsrow", [1, 64])
        io[pf + "ksrow"] = inp(pf + "ksrow", [1, 64])
        io[pf + "wo"] = inp(pf + "wo", [512, D])
        io[pf + "lng"] = inp(pf + "lng", [128, 8])
        io[pf + "lnb"] = inp(pf + "lnb", [128, 8])
        io[pf + "w1"] = inp(pf + "w1", [D, 5632])
        io[pf + "b1a"] = inp(pf + "b1a", [128, 22])
        io[pf + "b1g"] = inp(pf + "b1g", [128, 22])
        io[pf + "w2"] = inp(pf + "w2", [2816, D])
        io[pf + "b2"] = inp(pf + "b2", [128, 8])
    io["m_masks"] = inp("m_masks", [16, 128, 512], BF16)
    io["m_ind"] = inp("m_ind", [128, 1])
    io["m_tabs"] = inp("m_tabs", [4, 128, S + QH])
    io["oh4"] = inp("oh4", [128, 4])
    io["ohp"] = inp("ohp", [128, 3])
    io["e_out"] = nc.dram_tensor("e_out", [S, D], F32, kind="ExternalOutput").ap()

    def internal(name, shape, dt):
        return nc.dram_tensor(name, shape, dt).ap()

    io["cc_a_in"] = internal("cc_a_in", [128, T], BF16)
    io["cc_a_out"] = internal("cc_a_out", [512, T], BF16)
    io["b_attn_c"] = internal("b_attn_c", [512, S], BF16)
    io["cc_q_in"] = internal("cc_q_in", [D, S], BF16)
    io["cc_q_out"] = internal("cc_q_out", [4 * D, S], BF16)
    io["c_q_c"] = internal("c_q_c", [256, T], BF16)
    io["cc_k_in"] = internal("cc_k_in", [D, S], BF16)
    io["cc_k_out"] = internal("cc_k_out", [4 * D, S], BF16)
    io["c_k_c"] = internal("c_k_c", [256, T], BF16)
    io["cc_v_in"] = internal("cc_v_in", [S, 1040], BF16)
    io["ccv_gath"] = internal("ccv_gath", [4 * S, 1040], BF16)
    io["c_v_c"] = internal("c_v_c", [T, 260], BF16)
    io["cc_x2h_in"] = internal("cc_x2h_in", [D, XH], F32)
    io["cc_x2h_out"] = internal("cc_x2h_out", [4 * D, XH], F32)
    io["xe_d"] = internal("xe_d", [D, TE], F32)
    io["cc_c_in"] = internal("cc_c_in", [256, T], BF16)
    io["cc_c_out"] = internal("cc_c_out", [1024, T], BF16)
    io["d_attne_c"] = internal("d_attne_c", [D, TE], BF16)
    io["x3e_d"] = internal("x3e_d", [D, TE], F32)
    io["cc_xAh_in"] = internal("cc_xAh_in", [D, XH], F32)
    io["cc_xAh_out"] = internal("cc_xAh_out", [4 * D, XH], F32)
    io["xa_d"] = internal("xa_d", [D, TE], F32)
    for pf in ("d_", "e_"):
        io[pf + "w1b"] = internal(pf + "w1b", [D, 5632], BF16)
        io[pf + "w2b"] = internal(pf + "w2b", [2816, D], BF16)
        io[pf + "wqkvb"] = internal(pf + "wqkvb", [D, 1536], BF16)
        io[pf + "wob"] = internal(pf + "wob", [512, D], BF16)
    io["d_projb"] = internal("d_projb", [D, D], BF16)
    io["precast_pairs"] = (
        [(io[pf + "w1"], io[pf + "w1b"], D, 5632) for pf in ("d_", "e_")]
        + [(io[pf + "w2"], io[pf + "w2b"], 2816, D) for pf in ("d_", "e_")]
        + [(io[pf + "wqkv"], io[pf + "wqkvb"], D, 1536) for pf in ("d_", "e_")]
        + [(io[pf + "wo"], io[pf + "wob"], 512, D) for pf in ("d_", "e_")]
        + [(io["d_proj"], io["d_projb"], D, D)])
    io["xmd_D"] = internal("xmd_D", [D, S + 1], F32)
    io["xmd_E"] = internal("xmd_E", [D, S + 1], F32)
    io["xe_d_own"] = io["xe_d"][:, XH:XH + S]
    io["xa_d_own"] = io["xa_d"][:, XH:XH + S]
    io["d_xe"] = io["xe_d"]
    io["e_xe"] = io["xa_d"]
    io["e_x3"] = io["x3e_d"]

    dbg_aps = {}
    for nm in debug:
        shp = {"cc_a_in": [128, T], "cc_a_out": [512, T], "b_attn_c": [512, S], "c_q_c": [256, T],
               "c_k_c": [256, T], "c_v_c": [T, 260], "xe_d": [D, TE],
               "cc_c_out": [1024, T], "d_attne_c": [D, TE], "x3e_d": [D, TE],
               "xa_d": [D, TE], "cc_q_out": [4 * D, S]}[nm]
        dtd = F32 if nm in ("xe_d", "x3e_d", "xa_d") else BF16
        dbg_aps[nm] = nc.dram_tensor("dbg_" + nm, shp, dtd, kind="ExternalOutput").ap()

    with ExitStack() as gctx:
        p = Prog(nc)
        sm = {n: p.new_sem(n) for n in
              ("lds", "sts", "xls", "xs0", "xs1", "wl1", "wl2", "wl3",
               "wlsd", "wlsq", "wlso", "tbl", "mds", "ccs")}
        ccs = sm["ccs"]
        sts = sm["sts"]

        oh4 = gctx.enter_context(nc.sbuf_tensor("oh4_sb", [128, 4], F32))
        ohp = gctx.enter_context(nc.sbuf_tensor("ohp_sb", [128, 3], F32))
        p.dma(oh4[:], io["oh4"], sm["lds"])
        p.dma(ohp[:], io["ohp"], sm["lds"])

        lds = sm["lds"]
        const_sbuf(nc, p, gctx, lds, "sel64", sel_matrix(64))
        const_sbuf(nc, p, gctx, lds, "selT64", sel_matrix(64).T.copy())
        const_sbuf(nc, p, gctx, lds, "sel32h", sel_matrix(32, 64))
        const_sbuf(nc, p, gctx, lds, "selT32h", sel_matrix(32, 64).T.copy())
        const_sbuf(nc, p, gctx, lds, "ones1x128", np.ones((1, 128), np.float32))
        const_sbuf(nc, p, gctx, lds, "eps6", np.full((128, 1), 1e-6, np.float32))
        const_sbuf(nc, p, gctx, lds, "ones_col", np.ones((128, 1), np.float32))
        const_sbuf(nc, p, gctx, lds, "eps5", np.full((128, 1), 1e-5, np.float32))
        const_sbuf(nc, p, gctx, lds, "ident2", np.eye(2, dtype=np.float32))
        const_sbuf(nc, p, gctx, lds, "row8", np.full((1, 128), 0.125, np.float32))
        perm = np.zeros((128, 128), np.float32)
        for pp in range(128):
            perm[(pp // 64) * 64 + (pp + 32) % 64, pp] = 1.0
        const_sbuf(nc, p, gctx, lds, "permM", perm)
        m1np = np.zeros((128, 5), np.float32)
        m1np[127, 0] = 1.0
        m1np[:, 1:] = 1.0
        const_sbuf(nc, p, gctx, lds, "mask1", m1np.astype(BF16NP))
        const_sbuf(nc, p, gctx, lds, "identf", np.eye(128, dtype=np.float32))

        emit_A(p, nc, gctx, sm, io, T)
        p.barrier()
        if upto >= 1:
            p.cc("AllGather", ALU.bypass, G4, io["cc_a_in"].opt(), io["cc_a_out"].opt(), ccs)
            p.barrier()
            with ExitStack() as rs:
                emit_select(p, nc, rs, sm,
                            lambda rc, t0, w: io["b_attn_c"][128 * rc:128 * (rc + 1), t0:t0 + w],
                            [(lambda rc, t0, w, j=j: io["cc_a_out"][128 * rc:128 * (rc + 1), S * j + t0:S * j + t0 + w])
                             for j in range(4)],
                            oh4, range(4), [(i * 512, 512) for i in range(S // 512)], BF16, "rsA")
                p.barrier()
        if upto >= 2:
            emit_B(p, nc, gctx, sm, io, T)
            p.barrier()
            p.dma(io["cc_x2h_in"][:, :], io["xe_d"][:, XH + S - XH:XH + S], sm["mds"])
            p.wait("gpsimd", [Marker(sm["mds"], sm["mds"].count)])
            p.cc("AllGather", ALU.bypass, G4, io["cc_q_in"].opt(), io["cc_q_out"].opt(), ccs)
            p.cc("AllGather", ALU.bypass, G4, io["cc_k_in"].opt(), io["cc_k_out"].opt(), ccs)
            p.cc("AllGather", ALU.bypass, G4, io["cc_v_in"].opt(), io["ccv_gath"].opt(), ccs)
            p.cc("AllGather", ALU.bypass, G4, io["cc_x2h_in"].opt(), io["cc_x2h_out"].opt(), ccs)
            p.barrier()
            with ExitStack() as rs:
                for (nm, src, dst) in (("rsQ", io["cc_q_out"], io["c_q_c"]),
                                       ("rsK", io["cc_k_out"], io["c_k_c"])):
                    emit_select(p, nc, rs, sm,
                                lambda rc, t0, w, dst=dst: dst[128 * (rc % 2):128 * (rc % 2 + 1),
                                                               S * (rc // 2) + t0:S * (rc // 2) + t0 + w],
                                [(lambda rc, t0, w, j=j, src=src: src[D * (rc // 2) + 256 * j + 128 * (rc % 2):
                                                                      D * (rc // 2) + 256 * j + 128 * (rc % 2 + 1),
                                                                      t0:t0 + w])
                                 for j in range(4)],
                                oh4, range(8), [(i * 512, 512) for i in range(S // 512)], BF16, nm)
                emit_select(p, nc, rs, sm,
                            lambda rc, t0, w: io["c_v_c"][128 * rc:128 * (rc + 1), t0:t0 + w],
                            [(lambda rc, t0, w, j=j: io["ccv_gath"][S * (rc // 16) + 128 * (rc % 16):
                                                                    S * (rc // 16) + 128 * (rc % 16) + 128,
                                                                    260 * j + t0:260 * j + t0 + w])
                             for j in range(4)],
                            oh4, range(64), [(0, 260)], BF16, "rsV")
                emit_select(p, nc, rs, sm,
                            lambda rc, t0, w: io["xe_d"][128 * rc:128 * (rc + 1), t0:t0 + w],
                            [(lambda rc, t0, w, j=j: io["cc_x2h_out"][D * j + 128 * rc:D * j + 128 * (rc + 1),
                                                                     t0:t0 + w])
                             for j in range(3)],
                            ohp, range(8), [(0, 512), (512, 129)], F32, "rsH2")
                p.barrier()
        if upto >= 3:
            emit_C(p, nc, gctx, sm, io, T)
            p.barrier()
            p.cc("AllGather", ALU.bypass, G4, io["cc_c_in"].opt(), io["cc_c_out"].opt(), ccs)
            p.barrier()
            with ExitStack() as rs:
                emit_select(p, nc, rs, sm,
                            lambda rc, t0, w: io["d_attne_c"][128 * rc:128 * (rc + 1), XH + t0:XH + t0 + w],
                            [(lambda rc, t0, w, j=j: io["cc_c_out"][128 * rc:128 * (rc + 1), S * j + t0:S * j + t0 + w])
                             for j in range(4)],
                            oh4, range(8), [(i * 512, 512) for i in range(S // 512)], BF16, "rsC1")
                emit_select(p, nc, rs, sm,
                            lambda rc, t0, w: io["d_attne_c"][128 * rc:128 * (rc + 1), t0:t0 + w],
                            [(lambda rc, t0, w, j=j: io["cc_c_out"][128 * rc:128 * (rc + 1),
                                                                    S * (j + 1) - XH + t0:S * (j + 1) - XH + t0 + w])
                             for j in range(3)],
                            ohp, range(8), [(0, 512), (512, 129)], BF16, "rsC2")
                p.barrier()
        if upto >= 4:
            emit_DE(p, nc, gctx, sm, io, T, True)
            p.barrier()
            p.dma(io["cc_xAh_in"][:, :], io["xa_d"][:, XH + S - XH:XH + S], sm["mds"])
            p.wait("gpsimd", [Marker(sm["mds"], sm["mds"].count)])
            p.cc("AllGather", ALU.bypass, G4, io["cc_xAh_in"].opt(), io["cc_xAh_out"].opt(), ccs)
            p.barrier()
            with ExitStack() as rs:
                emit_select(p, nc, rs, sm,
                            lambda rc, t0, w: io["xa_d"][128 * rc:128 * (rc + 1), t0:t0 + w],
                            [(lambda rc, t0, w, j=j: io["cc_xAh_out"][D * j + 128 * rc:D * j + 128 * (rc + 1),
                                                                     t0:t0 + w])
                             for j in range(3)],
                            ohp, range(8), [(0, 512), (512, 129)], F32, "rsHA")
                p.barrier()
        if upto >= 5:
            emit_DE(p, nc, gctx, sm, io, T, False)
            p.barrier()
        for nm, ap in dbg_aps.items():
            p.dma(ap, io[nm][:, :] if nm != "c_v_c" else io[nm][:, :], sts)
        p.wait("sync", [Marker(sts, sts.count)])
        p.emit(gctx)
    return nc



def _bf(x):
    return np.ascontiguousarray(x).astype(BF16NP)


def _f32(x):
    return np.ascontiguousarray(x, dtype=np.float32)


def _rope_tables(T, S, hc):
    pos = S * hc + np.arange(-QH, S, dtype=np.float64)
    half = np.arange(0, 64, 2, dtype=np.float64)
    inv_freq = 1.0 / 10000.0 ** (half / 64.0)
    f = pos[None, :] * inv_freq[:, None]
    f = np.concatenate([f, f], axis=0)
    base = (half + 0.4 * 64) / (1.4 * 64)
    power = (pos - T // 2) / 256.0
    scl = base[:, None] ** power[None, :]
    scl = np.concatenate([scl, scl], axis=0)
    cosf, sinf = np.cos(f), np.sin(f)
    sgn = np.ones((64, 1)); sgn[:32] = -1.0
    Aq = cosf * scl
    Bq = sgn * sinf * scl
    Ak = cosf / scl
    Bk = sgn * sinf / scl
    out = [np.tile(x, (2, 1)).astype(np.float32) for x in (Aq, Bq, Ak, Bk)]
    return np.stack(out, axis=0)


def _de_masks(hc):
    m = np.zeros((16, 128, 512), dtype=BF16NP)
    for c in range(8):
        for kk in range(128):
            l = 128 * c + kk
            lo, hi = max(0, l - 512), min(511, l)
            if lo <= hi:
                m[c, kk, lo:hi + 1] = 1.0
            if l >= 512 and lo <= hi:
                m[8 + c, kk, lo:hi + 1] = 1.0
    if hc != 0:
        m[8:] = m[:8]
    return m


def host_inputs(inputs, T):
    S = T // 4
    x = inputs["L0_attn_x"][:, :T]
    wo_eff = np.ascontiguousarray(inputs["L0_attn_Wo"].reshape(16, 64, D)[:, :32].reshape(512, D))
    wq1 = np.ascontiguousarray(np.transpose(inputs["L1_attn_Wq"], (2, 0, 1)).reshape(D, D))
    wk1 = np.ascontiguousarray(np.transpose(inputs["L1_attn_Wk"], (2, 0, 1)).reshape(D, D))
    wv1 = np.ascontiguousarray(np.transpose(inputs["L1_attn_Wv"], (2, 0, 1)).reshape(D, D))

    de_common = {}
    for l, pf in ((0, "d_"), (1, "e_")):
        w1 = inputs["L1_ffn_w1"][l]
        a_pad = np.zeros((D, 2816), np.float32); a_pad[:, :2730] = w1[:, :2730]
        g_pad = np.zeros((D, 2816), np.float32); g_pad[:, :2730] = w1[:, 2730:]
        w1p = np.stack([a_pad.reshape(D, 22, 128), g_pad.reshape(D, 22, 128)], axis=2).reshape(D, 5632)
        b1 = inputs["L1_ffn_b1"][l]
        b1a = np.zeros(2816, np.float32); b1a[:2730] = b1[:2730]
        b1g = np.zeros(2816, np.float32); b1g[:2730] = b1[2730:]
        w2p = np.concatenate([inputs["L1_ffn_w2"][l], np.zeros((86, D), np.float32)], axis=0)
        de_common[pf] = {
            pf + "wqkv": _f32(inputs["L1_ffn_wqkv"][l]),
            pf + "qs": _f32(np.tile(inputs["L1_ffn_q_scale"][l], 2)[:, None]),
            pf + "ks": _f32(np.tile(inputs["L1_ffn_k_scale"][l], 2)[:, None]),
            pf + "qsrow": _f32(inputs["L1_ffn_q_scale"][l][None, :]),
            pf + "ksrow": _f32(inputs["L1_ffn_k_scale"][l][None, :]),
            pf + "wo": _f32(inputs["L1_ffn_wo"][l]),
            pf + "lng": _f32(inputs["L1_ffn_ln_g"][l].reshape(8, 128).T),
            pf + "lnb": _f32(inputs["L1_ffn_ln_b"][l].reshape(8, 128).T),
            pf + "w1": _f32(w1p),
            pf + "b1a": _f32(b1a.reshape(22, 128).T),
            pf + "b1g": _f32(b1g.reshape(22, 128).T),
            pf + "w2": _f32(w2p),
            pf + "b2": _f32(inputs["L1_ffn_b2"][l].reshape(8, 128).T),
        }

    maps = []
    for c in range(NCORES):
        b, hc = c // 4, c % 4
        sl = slice(S * hc, S * (hc + 1))
        oh4 = np.zeros((128, 4), np.float32); oh4[:, hc] = 1.0
        ohp = np.zeros((128, 3), np.float32)
        if hc >= 1:
            ohp[:, hc - 1] = 1.0
        m = {
            "a_x_t": _f32(x[b].T),
            "a_wq": _f32(inputs["L0_attn_Wq"][:, 256 * hc:256 * (hc + 1)]),
            "a_wkv": _f32(inputs["L0_attn_Wkv"]),
            "a_wk": _f32(inputs["L0_attn_Wk"][:, 128 * hc:128 * (hc + 1)]),
            "a_wv": _f32(inputs["L0_attn_Wv"][:, 128 * hc:128 * (hc + 1)]),
            "a_qg": _f32(np.tile(inputs["L0_attn_q_g"], 4).reshape(2, 128).T),
            "a_kg": _f32(np.tile(inputs["L0_attn_k_g"], 4)[:, None]),
            "a_qg32": _f32(inputs["L0_attn_q_g"][:32][None, :]),
            "a_kgr": _f32(inputs["L0_attn_k_g"][None, :]),
            "b_x_t": _f32(x[b].T[:, sl]),
            "b_wo": wo_eff,
            "b_wg": np.ascontiguousarray(inputs["L0_ffn_w_gate"].T),
            "b_wu": np.ascontiguousarray(inputs["L0_ffn_w_up"].T),
            "b_wd": np.ascontiguousarray(inputs["L0_ffn_w_down"].T),
            "b_wq1": wq1, "b_wk1": wk1, "b_wv1": wv1,
            "b_bq1": _f32(inputs["L1_attn_bq"].reshape(8, 128).T),
            "b_bk1": _f32(inputs["L1_attn_bk"].reshape(8, 128).T),
            "b_bv1": _f32(inputs["L1_attn_bv"].reshape(1, D)),
            "d_proj": _f32(inputs["L1_attn_proj"]),
            "m_masks": _de_masks(hc),
            "m_ind": np.full((128, 1), 0.0 if hc == 0 else 1.0, np.float32),
            "m_tabs": _rope_tables(T, S, hc),
            "oh4": oh4,
            "ohp": ohp,
        }
        m.update(de_common["d_"])
        m.update(de_common["e_"])
        maps.append(m)
    return maps


def kernel(**inputs):
    T = inputs["L0_attn_x"].shape[1]
    S = T // 4
    cores = list(range(NCORES))
    nc = build_fused(T)
    res = run_bass_kernel_spmd(nc, host_inputs(inputs, T), core_ids=cores)
    out = np.empty((2, T, D), dtype=np.float32)
    for c in cores:
        b, hc = c // 4, c % 4
        out[b, S * hc:S * (hc + 1)] = np.asarray(res.results[c]["e_out"])
    return out
